# version 07
# baseline (speedup 1.0000x reference)
import numpy as np
from contextlib import ExitStack

import concourse.bass as bass
import concourse.mybir as mybir

F32 = mybir.dt.float32
BF16 = mybir.dt.bfloat16
I32 = mybir.dt.int32
U16 = mybir.dt.uint16
FP16 = mybir.dt.float16
AF = mybir.ActivationFunctionType
ALU = mybir.AluOpType
AX = mybir.AxisListType

D = 2048
NT = 16
T = 1024
S = 4096
NCORES = 8

PI = np.concatenate([np.arange(0, D, 2), np.arange(1, D, 2)])



def prep_w(A):
    A = np.ascontiguousarray(A)
    K, O = A.shape
    KT = K // 128
    out = np.empty((O // 128, 128, K), A.dtype)
    for oc in range(O // 128):
        out[oc] = A[:, oc * 128:(oc + 1) * 128].reshape(KT, 128, 128).transpose(1, 0, 2).reshape(128, K)
    return out


def pack_u16(arr_i32, dense):
    v = np.ascontiguousarray(arr_i32).view(np.uint16)
    if not dense:
        v = np.ascontiguousarray(v[:, 0::2])
    return v


def prep_packed_pm(u16_T, rows_per_chunk):
    Wr, O = u16_T.shape
    n = Wr // rows_per_chunk
    sub = rows_per_chunk // 128
    out = np.empty((n, 128, sub * O), np.uint16)
    for c in range(n):
        blk = u16_T[c * rows_per_chunk:(c + 1) * rows_per_chunk]
        out[c] = blk.reshape(sub, 128, O).transpose(1, 0, 2).reshape(128, sub * O)
    return out


def host_prep(inp):
    g = {}
    x = np.asarray(inp["L0_attn_x"], np.float32)
    xT = np.ascontiguousarray(x.reshape(2 * S, D).T)
    xT_pi = np.ascontiguousarray(xT[PI])

    def pm_tiles(A):
        K, O = A.shape
        return np.ascontiguousarray(A.reshape(K // 128, 128, O).transpose(1, 0, 2).reshape(128, (K // 128) * O))

    g["l0_wq"] = prep_w(np.asarray(inp["L0_attn_Wq"], np.float32).T[PI])
    g["l0_wk_nat"] = pm_tiles(np.asarray(inp["L0_attn_Wk"], np.float32).T[PI])
    g["l0_wv_nat"] = pm_tiles(np.asarray(inp["L0_attn_Wv"], np.float32).T[PI])
    g["l0_mem"] = np.ascontiguousarray(np.asarray(inp["L0_attn_memory"], np.float32))
    g["l0_memn"] = np.ascontiguousarray(np.asarray(inp["L0_attn_memory_norm"], np.float32).reshape(4, 128).T)
    g["l0_memw"] = np.asarray(inp["L0_attn_memory_weight"], np.float32).reshape(1, 1)

    gte = pack_u16(np.asarray(inp["L0_ffn_gate_W_packed"]), dense=False).T
    upe = pack_u16(np.asarray(inp["L0_ffn_up_W_packed"]), dense=False).T
    dne = pack_u16(np.asarray(inp["L0_ffn_down_W_packed"]), dense=False).T
    def tile_gu(A):
        out = np.empty((16, 128, 8192), np.uint16)
        for ncb in range(16):
            blk = A[:, ncb * 1024:(ncb + 1) * 1024]
            out[ncb] = blk.reshape(8, 128, 1024).transpose(1, 0, 2).reshape(128, 8192)
        return out

    def tile_dn(A):
        out = np.empty((16, 128, 8192), np.uint16)
        for ncb in range(16):
            blk = A[ncb * 512:(ncb + 1) * 512]
            out[ncb] = blk.reshape(4, 128, 2048).transpose(1, 0, 2).reshape(128, 8192)
        return out

    g["l0_gpk"] = tile_gu(gte)
    g["l0_upk"] = tile_gu(upe)
    g["l0_dpk"] = tile_dn(dne)
    gscT = np.asarray(inp["L0_ffn_gate_scales"], np.float32).T
    uscT = np.asarray(inp["L0_ffn_up_scales"], np.float32).T
    dscT = np.asarray(inp["L0_ffn_down_scales"], np.float32).T

    def tile_sc_gu(A):
        out = np.empty((16, 2, 8, 4096), np.float32)
        for ncb in range(16):
            for h in range(2):
                for r in range(8):
                    for wc in range(8):
                        out[ncb, h, r, wc * 512:(wc + 1) * 512] = A[8 * wc + r, ncb * 1024 + h * 512:ncb * 1024 + h * 512 + 512]
        return np.ascontiguousarray(out)

    def tile_sc_d(A):
        out = np.empty((16, 2, 8, 4096), np.float32)
        for ncb in range(16):
            for dh in range(2):
                for r in range(8):
                    for ws in range(2):
                        out[ncb, dh, r, ws * 2048:(ws + 1) * 2048] = A[32 * ncb + 16 * dh + 8 * ws + r]
        return np.ascontiguousarray(out)

    g["l0_gsc"] = tile_sc_gu(gscT)
    g["l0_usc"] = tile_sc_gu(uscT)
    g["l0_dsc"] = tile_sc_d(dscT)

    g["l1_wq"] = prep_w(np.asarray(inp["L1_attn_Wq"], np.float32).T[PI])
    g["l1_wk"] = prep_w(np.asarray(inp["L1_attn_Wk"], np.float32).T[PI])
    g["l1_wv"] = prep_w(np.asarray(inp["L1_attn_Wv"], np.float32).T[PI])
    g["l1_wo"] = prep_w(np.ascontiguousarray(np.asarray(inp["L1_attn_Wo"], np.float32)[PI].T))

    gupT = pack_u16(np.asarray(inp["L1_ffn_gate_up_packed"]), dense=True).T
    dnT = pack_u16(np.asarray(inp["L1_ffn_down_packed"]), dense=True).T
    gpk1 = np.empty((22, 128, 2048), np.uint16)
    for s_ in range(22):
        gpk1[s_] = gupT[:, 512 * s_:512 * (s_ + 1)].reshape(4, 128, 512).transpose(1, 0, 2).reshape(128, 2048)
    g["l1_gpk"] = gpk1
    g["l1_dpk"] = np.ascontiguousarray(dnT.reshape(11, 128, 2048))
    gsc1 = np.asarray(inp["L1_ffn_gate_up_scales"])
    gsc_t = np.empty((22, 4, 4, 512), np.float16)
    for s_ in range(22):
        for tin in range(4):
            for r in range(4):
                gsc_t[s_, tin, r] = gsc1[4 * tin + r, 512 * s_:512 * (s_ + 1)]
    g["l1_gsc"] = gsc_t
    g["l1_dsc"] = np.ascontiguousarray(np.asarray(inp["L1_ffn_down_scales"]).reshape(11, 4, 2048))

    consts = np.zeros((128, 8), np.float32)
    consts[:, 0] = 1.0
    consts[:, 1] = -20.0
    consts[:, 2] = 1e-5
    consts[:, 3] = 1e-6
    g["consts"] = consts
    g["onesrow"] = np.ones((1, 128), np.float32)

    percore = []
    for c in range(NCORES):
        b, j = c // 4, c % 4
        pc = dict(
            xT_my=np.ascontiguousarray(xT_pi[:, c * T:(c + 1) * T]),
            xT_b=np.ascontiguousarray(xT_pi[:, b * S:(b + 1) * S]),
        )
        cm = np.zeros((128, 8), np.float32)
        for r in range(b * 4, b * 4 + j):
            cm[:, r] = 1.0
        pc["cmask"] = cm
        percore.append(pc)
    return g, percore


def host_finish(per_core_outs):
    full = np.concatenate(per_core_outs, axis=1)
    nat = np.empty_like(full)
    nat[PI] = full
    return np.ascontiguousarray(nat.T).reshape(2, S, D)



ENGINES = ("sync", "gpsimd", "vector", "scalar", "tensor")


class Sem:
    def __init__(self, handle, name):
        self.h = handle
        self.name = name
        self.n = 0

    def bump(self, k):
        self.n += k
        return self.n


class Tok:
    __slots__ = ("sem", "tgt", "_op")

    def __init__(self, sem, tgt):
        self.sem = sem
        self.tgt = tgt
        self._op = None


class OpRec:
    __slots__ = ("eng", "fn", "exp_sem", "exp_by", "auto", "auto_val", "prewaits", "idx", "_tok", "raw_fn")

    def __init__(self, eng, fn, exp_sem, exp_by, idx):
        self.eng = eng
        self.fn = fn
        self.exp_sem = exp_sem
        self.exp_by = exp_by
        self.auto = False
        self.auto_val = None
        self.prewaits = []
        self.idx = idx
        self._tok = None
        self.raw_fn = False


class Prog:

    def __init__(self, nc):
        self.nc = nc
        self.ops = {e: [] for e in ENGINES}
        self._waited = {e: {} for e in ENGINES}
        self._auto_hw = {e: {} for e in ENGINES}
        self._store = {}
        self._auto_sems = None

    def _edge(self, producer, consumer):
        if producer is consumer:
            return
        if isinstance(consumer, Tok):
            consumer = consumer._op
        if isinstance(producer, Tok):
            if getattr(producer, "_op", None) is consumer:
                return
            consumer.prewaits.append(("tok", producer))
            return
        if producer.eng == "tensor" and consumer.eng == "tensor":
            return
        producer.auto = True
        consumer.prewaits.append(("auto", producer))

    def _add_reader(self, d, r):
        if isinstance(r, Tok) and r._op is not None and r._op.exp_sem is not None:
            d[2].append(r)
        elif isinstance(r, Tok):
            d[2].append(r)
        else:
            d[1][r.eng] = r

    def _note(self, op_or_tok, reads, writes):
        for k in reads:
            d = self._store.get(k)
            if d is not None and d[0] is not None:
                self._edge(d[0], op_or_tok)
            if d is None:
                d = self._store[k] = [None, {}, []]
            self._add_reader(d, op_or_tok)
        for k in writes:
            d = self._store.get(k)
            if d is not None:
                if d[0] is not None:
                    self._edge(d[0], op_or_tok)
                for r in d[1].values():
                    if r is not op_or_tok:
                        self._edge(r, op_or_tok)
                for r in d[2]:
                    if r is not op_or_tok:
                        self._edge(r, op_or_tok)
            self._store[k] = [op_or_tok, {}, []]

    def emit(self, eng, fn, inc=None, inc_by=1, reads=(), writes=()):
        op = OpRec(eng, fn, inc, inc_by, len(self.ops[eng]))
        tok = None
        if inc is not None:
            tgt = inc.bump(inc_by)
            tok = Tok(inc, tgt)
            tok._op = op
        op._tok = tok
        if reads or writes:
            self._note(tok if tok is not None else op, reads, writes)
        self.ops[eng].append(op)
        return tok if inc is not None else op

    def dma(self, eng, sem, out, in_, reads=(), writes=()):
        tgt = sem.bump(16)
        tok = Tok(sem, tgt)
        op = OpRec(eng, lambda e, _o=out, _i=in_: e.dma_start(out=_o, in_=_i), sem, 16, len(self.ops[eng]))
        op._tok = tok
        tok._op = op
        if reads or writes:
            for k in reads:
                d = self._store.get(k)
                if d is not None and d[0] is not None:
                    self._edge(d[0], op)
                if d is None:
                    d = self._store[k] = [None, {}, []]
                d[2].append(tok)
            for k in writes:
                d = self._store.get(k)
                if d is not None:
                    if d[0] is not None:
                        self._edge(d[0], op)
                    for r in d[1].values():
                        self._edge(r, op)
                    for r in d[2]:
                        self._edge(r, op)
                self._store[k] = [tok, {}, []]
        self.ops[eng].append(op)
        return tok

    def join(self, keys, engines=ENGINES):
        for eng in engines:
            marker = OpRec(eng, None, None, 0, len(self.ops[eng]))
            for k in keys:
                d = self._store.get(k)
                if d is None:
                    continue
                if d[0] is not None:
                    self._edge(d[0], marker)
                for r in d[1].values():
                    self._edge(r, marker)
                for r in d[2]:
                    self._edge(r, marker)
            if marker.prewaits:
                self.ops[eng].append(marker)

    def wait(self, eng, tok):
        if tok is None:
            return
        if isinstance(tok, OpRec):
            tok.auto = True
            marker = OpRec(eng, None, None, 0, len(self.ops[eng]))
            marker.prewaits.append(("auto", tok))
            self.ops[eng].append(marker)
            return
        w = self._waited[eng]
        if w.get(tok.sem.name, -1) >= tok.tgt:
            return
        w[tok.sem.name] = tok.tgt
        marker = OpRec(eng, None, None, 0, len(self.ops[eng]))
        marker.prewaits.append(("tok", tok))
        self.ops[eng].append(marker)

    def raw(self, eng, fn):
        op = OpRec(eng, fn, None, 0, len(self.ops[eng]))
        op.raw_fn = True
        self.ops[eng].append(op)

    def finalize(self, block, es):
        auto_sems = {}
        for e in ENGINES:
            if any(getattr(o, "auto", False) for o in self.ops[e]):
                auto_sems[e] = Sem(es.enter_context(self.nc.semaphore("au_" + e)), "au_" + e)
        for e, lst in self.ops.items():
            v = 0
            for o in lst:
                if getattr(o, "auto", False):
                    v += 1
                    o.auto_val = v
        waited_tok = {e: {} for e in ENGINES}
        waited_auto = {e: {} for e in ENGINES}

        for e in ENGINES:
            lst = self.ops[e]
            if not lst:
                continue

            def body(eng_obj, _lst=tuple(lst), _e=e):
                for o in _lst:
                    tok_max = {}
                    auto_max = {}
                    for kind, p in o.prewaits:
                        if kind == "tok":
                            if tok_max.get(p.sem.name, (None, -1))[1] < p.tgt:
                                tok_max[p.sem.name] = (p.sem.h, p.tgt)
                        else:
                            if auto_max.get(p.eng, (None, -1))[1] < p.auto_val:
                                auto_max[p.eng] = (auto_sems[p.eng].h, p.auto_val)
                    for name, (h, tgt) in tok_max.items():
                        wt = waited_tok[_e]
                        if wt.get(name, -1) < tgt:
                            wt[name] = tgt
                            eng_obj.wait_ge(h, tgt)
                    for peng, (h, tgt) in auto_max.items():
                        wa = waited_auto[_e]
                        if wa.get(peng, -1) < tgt:
                            wa[peng] = tgt
                            eng_obj.wait_ge(h, tgt)
                    if o.fn is None:
                        continue
                    inst = o.fn(eng_obj)
                    if getattr(o, "raw_fn", False):
                        continue
                    if o.exp_sem is not None:
                        inst.then_inc(o.exp_sem.h, o.exp_by)
                    if getattr(o, "auto", False):
                        inst.then_inc(auto_sems[o.eng].h, 1)
            getattr(block, e)(body)



class NS(dict):
    __getattr__ = dict.__getitem__
    __setattr__ = dict.__setitem__


class MK:
    def __init__(self, nc, es, P):
        self.nc = nc
        self.es = es
        self.P = P
        self.sem = NS()
        for name in ("d", "g", "v", "s", "t", "cc"):
            self.sem[name] = Sem(es.enter_context(nc.semaphore("sem_" + name)), name)
        self._dsems = {}
        self.t = NS()
        self.ps = NS()
        self.ap = NS()
        self._ln_ps_free = None
        self._ln_bc_free = None

    def S(self, name):
        if name not in self._dsems:
            self._dsems[name] = Sem(self.es.enter_context(self.nc.semaphore("ds_" + name)), "ds_" + name)
        return self._dsems[name]

    def sbuf(self, name, shape, dtype):
        h = self.es.enter_context(self.nc.sbuf_tensor(name, shape, dtype))
        self.t[name] = h
        return h

    def psum(self, name, shape, dtype=F32):
        h = self.es.enter_context(self.nc.psum_tensor(name, shape, dtype))
        self.ps[name] = h
        return h

    def din(self, name, shape, dtype):
        ap = self.nc.dram_tensor(name, shape, dtype, kind="ExternalInput").ap()
        self.ap[name] = ap
        return ap

    def dout(self, name, shape, dtype):
        ap = self.nc.dram_tensor(name, shape, dtype, kind="ExternalOutput").ap()
        self.ap[name] = ap
        return ap

    def setup_consts(self):
        P, sem = self.P, self.sem
        self.din("consts", [128, 8], F32)
        self.din("onesrow", [1, 128], F32)
        cst = self.sbuf("cst", [128, 8], F32)
        orow = self.sbuf("orow", [1, 128], F32)
        orow_b = self.sbuf("orow_b", [1, 128], BF16)
        P.dma("sync", self.S("c0"), cst[:], self.ap.consts, writes=[("cst",)])
        P.dma("sync", self.S("c1"), orow[:], self.ap.onesrow, writes=[("orow",)])
        P.emit("vector", lambda v: v.tensor_copy(orow_b[:], orow[:]), inc=sem.v,
               reads=[("orow",)], writes=[("orowb",)])
        self.ones_col = cst[:, 0:1]
        self.bias_exp = cst[:, 1:2]
        self.eps_ln = cst[0:1, 2:3]
        self.eps_rms = cst[0:1, 3:4]
        self.orow = orow
        self.orow_b = orow_b

    def emit_ln(self, r_aps, r_keys, out_aps, out_keys, sq, rows, ps_s1, ps_s2, ps_b1, ps_b2,
                kn):
        P, sem = self.P, self.sem
        nT = len(r_aps)
        K = lambda n, *i: (kn[n] + i) if isinstance(kn[n], tuple) else ((kn[n],) + i)
        for i, r in enumerate(r_aps):
            buf = sq[:, (i % 2) * T:(i % 2) * T + T]
            P.emit("scalar", lambda s, _b=buf, _r=r: s.activation(_b, _r, AF.Square), inc=sem.s,
                   reads=[r_keys[i], ("cst",)], writes=[K("sq", i % 2)])
            for nh in range(T // 512):
                st, sp = (i == 0), (i == nT - 1)
                P.emit("tensor", lambda t, _r=r, _nh=nh, _st=st, _sp=sp:
                       t.matmul(ps_s1[0:1, _nh * 512:_nh * 512 + 512], self.ones_col,
                                _r[:, _nh * 512:_nh * 512 + 512], start=_st, stop=_sp),
                       inc=sem.t if (sp and nh == 1) else None,
                       reads=[r_keys[i], ("cst",)], writes=[K("s1", nh)])
                P.emit("tensor", lambda t, _b=buf, _nh=nh, _st=st, _sp=sp:
                       t.matmul(ps_s2[0:1, _nh * 512:_nh * 512 + 512], self.ones_col,
                                _b[:, _nh * 512:_nh * 512 + 512], start=_st, stop=_sp),
                       inc=sem.t if (sp and nh == 1) else None,
                       reads=[K("sq", i % 2), ("cst",)], writes=[K("s2", nh)])
        m = rows[0:1, 0:T]; msq = rows[0:1, T:2 * T]; var = rows[0:1, 2 * T:3 * T]
        rstd = rows[0:1, 3 * T:4 * T]; mr = rows[0:1, 4 * T:5 * T]
        RW = [K("rows")]
        P.emit("vector", lambda v: v.tensor_scalar(m, ps_s1[0:1, :], 1.0 / D, None, ALU.mult),
               reads=[K("s1", 0), K("s1", 1)], writes=RW)
        P.emit("vector", lambda v: v.tensor_scalar(msq, ps_s2[0:1, :], 1.0 / D, None, ALU.mult),
               reads=[K("s2", 0), K("s2", 1)], writes=RW)
        P.emit("vector", lambda v: v.scalar_tensor_tensor(var, m, -1.0, m, ALU.mult, ALU.mult),
               reads=RW, writes=RW)
        P.emit("vector", lambda v: v.tensor_tensor(var, msq, var, ALU.add), inc=sem.v,
               reads=RW, writes=RW)
        P.emit("scalar", lambda s: s.activation(var, var, AF.Ln, bias=self.eps_ln),
               reads=RW + [("cst",)], writes=RW)
        P.emit("scalar", lambda s: s.activation(rstd, var, AF.Exp, scale=-0.5), inc=sem.s,
               reads=RW, writes=RW)
        for nh in range(T // 512):
            P.emit("tensor", lambda t, _nh=nh: t.matmul(ps_b1[:, _nh * 512:_nh * 512 + 512],
                   self.orow[0:1, 0:128], rstd[0:1, _nh * 512:_nh * 512 + 512], start=True, stop=True),
                   reads=RW + [("orow",)], writes=[K("b1", nh)])
            P.emit("tensor", lambda t, _nh=nh: t.matmul(ps_b2[:, _nh * 512:_nh * 512 + 512],
                   self.orow[0:1, 0:128], m[0:1, _nh * 512:_nh * 512 + 512], start=True, stop=True),
                   inc=sem.t if nh == 1 else None,
                   reads=RW + [("orow",)], writes=[K("b2", nh)])
        out_toks = []
        for i, (r, o) in enumerate(zip(r_aps, out_aps)):
            buf = sq[:, (i % 2) * T:(i % 2) * T + T]
            P.emit("vector", lambda v, _r=r, _b=buf: v.tensor_tensor(_b, _r, ps_b2[:], ALU.subtract),
                   reads=[r_keys[i], K("b2", 0), K("b2", 1)], writes=[K("sq", i % 2)])
            out_toks.append(P.emit("vector", lambda v, _b=buf, _o=o: v.tensor_tensor(_o, _b, ps_b1[:], ALU.mult),
                                   inc=sem.v,
                                   reads=[K("sq", i % 2), K("b1", 0), K("b1", 1)], writes=[out_keys[i]]))
        return out_toks

    def emit_l0_attn(self, a, upto='full'):
        P, sem = self.P, self.sem
        xT_my = self.din("xT_my", [D, T], F32)
        xT_b = self.din("xT_b", [D, S], F32)
        wq = self.din("l0_wq", [16, 128, D], F32)
        wk = self.din("l0_wk_nat", [128, 16 * 512], F32)
        wv = self.din("l0_wv_nat", [128, 16 * 512], F32)
        mem = self.din("l0_mem", [512, 512], F32)
        memn = self.din("l0_memn", [128, 4], F32)
        memw = self.din("l0_memw", [1, 1], F32)
        self.ap_xT_my = xT_my

        q_sb, k_sb, v_tm, attn = a.q_sb, a.k_sb, a.v_tm, a.attn
        wkv_sb, xsl, wqb = a.wkv_sb, a.xsl, a.wqb
        mw_sb = a.mw_sb
        mem_sb, memn_sb = a.mem_sb, a.memn_sb
        ps_sc, ps_a, ps_bc = a.ps_sc, a.ps_a, a.ps_bc

        P.dma("gpsimd", self.S("wk"), wkv_sb[:, 0:8192], wk, writes=[("wk",)])
        P.dma("gpsimd", self.S("wv"), wkv_sb[:, 8192:], wv, writes=[("wv",)])
        P.dma("gpsimd", self.S("mem"), mem_sb[:], mem.rearrange("(ic p) j -> p ic j", p=128), writes=[("mem",)])
        P.dma("gpsimd", self.S("memn"), memn_sb[:], memn, writes=[("memn",)])
        P.dma("sync", self.S("mw"), mw_sb[0:1, 0:1], memw, writes=[("mw",)])

        for tkc in range(8):
            bi = tkc % 2
            sl = xsl[:, bi * 8192:(bi + 1) * 8192]
            P.dma("gpsimd", self.S("x%d" % bi), sl,
                  xT_b.rearrange("(kc p) s -> p kc s", p=128)[:, :, tkc * 512:(tkc + 1) * 512],
                  writes=[("xsl", bi)])
            for oc in range(4):
                kp, kpn = (a.ps_sc[0], "ps0") if oc % 2 == 0 else (a.ps_a, "ps2")
                for kc in range(NT):
                    P.emit("tensor", lambda t, _oc=oc, _kc=kc, _sl=sl, _ps=kp:
                           t.matmul(_ps[:, 0:512],
                                    wkv_sb[:, _kc * 512 + _oc * 128:_kc * 512 + _oc * 128 + 128],
                                    _sl[:, _kc * 512:_kc * 512 + 512],
                                    start=_kc == 0, stop=_kc == NT - 1),
                           inc=sem.t if kc == NT - 1 else None,
                           reads=[("wk",), ("xsl", bi)], writes=[("P", kpn, 0)])
                P.emit("scalar", lambda s, _oc=oc, _tkc=tkc, _ps=kp:
                       s.copy(k_sb[:, _oc * S + _tkc * 512:_oc * S + _tkc * 512 + 512], _ps[:, 0:512]),
                       inc=sem.s, reads=[("P", kpn, 0)], writes=[("k",)])
            for tsub in range(4):
                vt = tkc * 4 + tsub
                vp, vpn = (a.ps_sc[1], "ps1") if tsub % 2 == 0 else (a.ps_bc, "ps3")
                for kc in range(NT):
                    P.emit("tensor", lambda t, _kc=kc, _tsub=tsub, _sl=sl, _ps=vp:
                           t.matmul(_ps[:, 0:512],
                                    _sl[:, _kc * 512 + _tsub * 128:_kc * 512 + _tsub * 128 + 128],
                                    wkv_sb[:, 8192 + _kc * 512:8192 + (_kc + 1) * 512],
                                    start=_kc == 0, stop=_kc == NT - 1),
                           inc=sem.t if kc == NT - 1 else None,
                           reads=[("wv",), ("xsl", bi)], writes=[("P", vpn, 0)])
                src = vp[:, 0:512].rearrange("p (h dd par) -> p h par dd", h=8, dd=32)
                dst = v_tm[:, vt * 520:(vt + 1) * 520].rearrange("p (h x) -> p h x", h=8)[:, :, 0:64].rearrange(
                    "p h (par dd) -> p h par dd", par=2)
                P.emit("vector", lambda v, _s=src, _d=dst: v.tensor_copy(_d, _s), inc=sem.v,
                       reads=[("P", vpn, 0)], writes=[("v",)])
                ones_ap = v_tm[:, vt * 520:(vt + 1) * 520].rearrange("p (h x) -> p h x", h=8)[:, :, 64:65]
                P.emit("vector", lambda v, _a=ones_ap: v.memset(_a, 1.0), writes=[("v",)])

        if upto in ('a1', 'a1v'):
            self._phase_tail_toks = [Tok(sem.s, sem.s.n), Tok(sem.v, sem.v.n), Tok(sem.t, sem.t.n)]
            return a
        tok_xmy = P.dma("gpsimd", self.S("x0"), xsl[:, 0:16 * T].rearrange("p (kc t) -> p kc t", kc=16),
                        xT_my.rearrange("(kc p) t -> p kc t", p=128),
                        writes=[("xsl", 0), ("xsl", 1)])
        a.tok_xmy = tok_xmy
        q_evac = []
        for oc in range(16):
            bi = oc % 2
            psn = "ps0" if bi == 0 else "ps1"
            psx = ps_sc[bi]
            P.dma("gpsimd", self.S("w%d" % bi), wqb[:, bi * D:(bi + 1) * D], wq[oc], writes=[("wqb", bi)])
            for kc in range(NT):
                for nh in range(2):
                    P.emit("tensor", lambda t, _bi=bi, _kc=kc, _nh=nh, _ps=psx:
                           t.matmul(_ps[:, _nh * 512:_nh * 512 + 512],
                                    wqb[:, _bi * D + _kc * 128:_bi * D + _kc * 128 + 128],
                                    xsl[:, _kc * T + _nh * 512:_kc * T + _nh * 512 + 512],
                                    start=_kc == 0, stop=_kc == NT - 1),
                           inc=sem.t if (kc == NT - 1 and nh == 1) else None,
                           reads=[("wqb", bi), ("xsl", 0), ("xsl", 1)], writes=[("P", psn, nh)])
            tok_ev = P.emit("scalar", lambda s, _oc=oc, _ps=psx: s.copy(q_sb[:, _oc * T:(_oc + 1) * T], _ps[:]),
                            inc=sem.s, reads=[("P", psn, 0), ("P", psn, 1)], writes=[("q",)])
            q_evac.append(tok_ev)

        MW = [("mw",)]
        P.emit("scalar", lambda s: s.activation(mw_sb[0:1, 1:2], mw_sb[0:1, 0:1], AF.Exp, scale=-1.0),
               inc=sem.s, reads=MW, writes=MW)
        P.emit("vector", lambda v: v.tensor_scalar(mw_sb[0:1, 2:3], mw_sb[0:1, 1:2], 1.0, None, ALU.add),
               reads=MW, writes=MW)
        P.emit("vector", lambda v: v.reciprocal(mw_sb[0:1, 2:3], mw_sb[0:1, 2:3]), reads=MW, writes=MW)
        tok_w = P.emit("vector", lambda v: v.tensor_tensor(mw_sb[0:1, 3:4], mw_sb[0:1, 1:2], mw_sb[0:1, 2:3], ALU.mult),
                       inc=sem.v, reads=MW, writes=MW)
        a.tok_w = tok_w
        a.q_evac = q_evac
        return a

    def emit_l0_scores(self, a, nqh=32, sub='full'):
        P, sem = self.P, self.sem
        q_sb, k_sb, v_tm, attn = a.q_sb, a.k_sb, a.v_tm, a.attn
        pbuf, tmp64, rowsA, rbf, mw_sb = a.pbuf, a.tmp64, a.rowsA, a.rbf, a.mw_sb
        ps_sc, ps_a, ps_bc = a.ps_sc, a.ps_a, a.ps_bc

        attn_toks = [None] * 16
        items = [(qh, tkc) for qh in range(nqh) for tkc in range(32)]
        PSN = ["ps0", "ps1"]

        def emit_scores(idx):
            qh, tkc = items[idx]
            h = qh % 8
            koc, koff = h // 2, (h % 2) * 64
            qoc, qoff = qh // 2, (qh % 2) * 64
            bi = idx % 2
            pi3 = idx % 3
            psn = PSN[bi]
            for nh in range(2):
                P.emit("tensor", lambda t, _bi=bi, _koc=koc, _koff=koff, _qoc=qoc, _qoff=qoff, _tkc=tkc, _nh=nh:
                       t.matmul(ps_sc[_bi][:, _nh * 512:_nh * 512 + 512],
                                k_sb[_koff:_koff + 64, _koc * S + _tkc * 128:_koc * S + (_tkc + 1) * 128],
                                q_sb[_qoff:_qoff + 64, _qoc * T + _nh * 512:_qoc * T + _nh * 512 + 512],
                                start=True, stop=True),
                       inc=sem.t if nh == 1 else None,
                       reads=[("k",), ("q",)], writes=[("P", psn, nh)])
            P.emit("scalar", lambda s, _bi=bi, _pi=pi3:
                   s.activation(pbuf[:, _pi * T:(_pi + 1) * T], ps_sc[_bi][:], AF.Exp,
                                bias=self.bias_exp, scale=0.125), inc=sem.s,
                   reads=[("P", psn, 0), ("P", psn, 1), ("cst",)], writes=[("pb", pi3)])

        def emit_pv(idx):
            qh, tkc = items[idx]
            h = qh % 8
            pi3 = idx % 3
            for nh in range(2):
                P.emit("tensor", lambda t, _pi=pi3, _h=h, _tkc=tkc, _nh=nh:
                       t.matmul(ps_a[0:65, _nh * 512:_nh * 512 + 512],
                                v_tm[:, _tkc * 520 + _h * 65:_tkc * 520 + _h * 65 + 65],
                                pbuf[:, _pi * T + _nh * 512:_pi * T + _nh * 512 + 512],
                                start=_tkc == 0, stop=_tkc == 31),
                       inc=sem.t if nh == 1 else None,
                       reads=[("v",), ("pb", pi3)], writes=[("P", "ps2", nh)])

        def emit_tail(qh):
            r1 = rowsA[0:1, 0:T]
            P.emit("vector", lambda v, _r=r1: v.reciprocal(_r, ps_a[64:65, :]),
                   reads=[("P", "ps2", 0), ("P", "ps2", 1)], writes=[("r1",)])
            P.emit("vector", lambda v, _r=r1: v.tensor_scalar(_r, _r, mw_sb[0:1, 3:4], None, ALU.mult),
                   reads=[("r1",), ("mw",)], writes=[("r1",)])
            P.emit("vector", lambda v, _r=r1: v.tensor_copy(rbf[:], _r), inc=sem.v,
                   reads=[("r1",)], writes=[("rbf",)])
            for nh in range(2):
                P.emit("tensor", lambda t, _nh=nh:
                       t.matmul(ps_bc[0:64, _nh * 512:_nh * 512 + 512], self.orow_b[0:1, 0:64],
                                rbf[0:1, _nh * 512:_nh * 512 + 512], start=True, stop=True),
                       inc=sem.t if nh == 1 else None,
                       reads=[("rbf",), ("orowb",)], writes=[("P", "ps3", nh)])
            P.emit("scalar", lambda s: s.copy(tmp64[:], ps_a[0:64, :]), inc=sem.s,
                   reads=[("P", "ps2", 0), ("P", "ps2", 1)], writes=[("tmp",)])
            d_ev, o_ev = (qh * 32) // 128, (qh * 32) % 128
            attn_toks[d_ev] = P.emit("vector", lambda v, _d=d_ev, _o=o_ev:
                   v.tensor_tensor(attn[_o:_o + 32, _d * T:(_d + 1) * T], tmp64[0:32, :], ps_bc[0:32, :], ALU.mult),
                   inc=sem.v, reads=[("tmp",), ("P", "ps3", 0), ("P", "ps3", 1)], writes=[("at", d_ev)])
            attn_toks[d_ev + 8] = P.emit("vector", lambda v, _d=d_ev + 8, _o=o_ev:
                   v.tensor_tensor(attn[_o:_o + 32, _d * T:(_d + 1) * T], tmp64[32:64, :], ps_bc[32:64, :], ALU.mult),
                   inc=sem.v, reads=[("tmp",), ("P", "ps3", 0), ("P", "ps3", 1)], writes=[("at", d_ev + 8)])

        LAG = 2
        if sub == 'scores':
            for idx in range(len(items)):
                emit_scores(idx)
        else:
            for idx in range(len(items) + LAG):
                if idx < len(items):
                    emit_scores(idx)
                j = idx - LAG
                if j >= 0:
                    emit_pv(j)
                    if items[j][1] == 31:
                        emit_tail(items[j][0])
        a.attn_toks = attn_toks
        return a

    def emit_a_mem(self, a):
        P, sem = self.P, self.sem
        q_sb, sig_sb, attn = a.q_sb, a.sig_sb, a.attn
        mem_sb, memn_sb, mw_sb = a.mem_sb, a.memn_sb, a.mw_sb
        rowsA, rbf, sctch = a.rowsA, a.rbf, a.sctch
        ps_m, ps_d, ps_bc = a.ps_sc[0], a.ps_a, a.ps_bc
        attn_toks = a.attn_toks
        for qq in range(4):
            for ic in range(4):
                oc = qq * 4 + ic
                qa = q_sb[:, oc * T:(oc + 1) * T]
                mbuf = sctch[:, 0:T]
                pbuf2 = sctch[:, T:2 * T]
                P.emit("vector", lambda v, _q=qa, _m=mbuf: v.tensor_scalar(_m, _q, 0.0, None, ALU.min),
                       reads=[("q",)], writes=[("sct", 0)])
                P.emit("vector", lambda v, _q=qa, _p=pbuf2: v.tensor_scalar(_p, _q, 0.0, None, ALU.max),
                       inc=sem.v, reads=[("q",)], writes=[("sct", 1)])
                P.emit("scalar", lambda s, _m=mbuf: s.activation(_m, _m, AF.Exp), inc=sem.s,
                       reads=[("sct", 0)], writes=[("sct", 0)])
                P.emit("vector", lambda v, _ic=ic, _m=mbuf, _p=pbuf2:
                       v.scalar_tensor_tensor(sig_sb[:, _ic * T:(_ic + 1) * T], _m, -1.0, _p, ALU.add, ALU.add),
                       inc=sem.v, reads=[("sct", 0), ("sct", 1)], writes=[("sig",)])
            for ic in range(4):
                for nh in range(2):
                    P.emit("tensor", lambda t, _ic=ic, _nh=nh:
                           t.matmul(ps_d[64:65, _nh * 512:_nh * 512 + 512], memn_sb[:, _ic:_ic + 1],
                                    sig_sb[:, _ic * T + _nh * 512:_ic * T + _nh * 512 + 512],
                                    start=_ic == 0, stop=_ic == 3),
                           inc=sem.t if (ic == 3 and nh == 1) else None,
                           reads=[("memn",), ("sig",)], writes=[("P", "ps2", nh)])
            r2 = rowsA[0:1, 0:T]
            P.emit("vector", lambda v, _r=r2: v.reciprocal(_r, ps_d[64:65, :]),
                   reads=[("P", "ps2", 0), ("P", "ps2", 1)], writes=[("r1",)])
            P.emit("vector", lambda v, _r=r2: v.tensor_scalar(_r, _r, mw_sb[0:1, 2:3], None, ALU.mult),
                   reads=[("r1",), ("mw",)], writes=[("r1",)])
            P.emit("vector", lambda v, _r=r2: v.tensor_copy(rbf[:], _r), inc=sem.v,
                   reads=[("r1",)], writes=[("rbf",)])
            for nh in range(2):
                P.emit("tensor", lambda t, _nh=nh:
                       t.matmul(ps_bc[:, _nh * 512:_nh * 512 + 512], self.orow_b[0:1, 0:128],
                                rbf[0:1, _nh * 512:_nh * 512 + 512], start=True, stop=True),
                       inc=sem.t if nh == 1 else None,
                       reads=[("rbf",), ("orowb",)], writes=[("P", "ps3", nh)])
            for js in range(4):
                par = 0 if js < 2 else 1
                for ic in range(4):
                    for nh in range(2):
                        P.emit("tensor", lambda t, _ic=ic, _js=js, _par=par, _nh=nh:
                               t.matmul(ps_m[:, _nh * 512:_nh * 512 + 512],
                                        mem_sb[:, _ic * 512:(_ic + 1) * 512].rearrange(
                                            "p (a two) -> p a two", two=2)[:, (_js % 2) * 128:(_js % 2) * 128 + 128, _par],
                                        sig_sb[:, _ic * T + _nh * 512:_ic * T + _nh * 512 + 512],
                                        start=_ic == 0, stop=_ic == 3),
                               inc=sem.t if (ic == 3 and nh == 1) else None,
                               reads=[("mem",), ("sig",)], writes=[("P", "ps0", nh)])
                P.emit("scalar", lambda s: s.copy(sctch[:, 0:T], ps_m[:]), inc=sem.s,
                       reads=[("P", "ps0", 0), ("P", "ps0", 1)], writes=[("sct", 0)])
                dt_i = qq * 2 + (js % 2) + (8 if par else 0)
                P.emit("vector", lambda v: v.tensor_tensor(sctch[:, T:2 * T], sctch[:, 0:T], ps_bc[:], ALU.mult),
                       reads=[("sct", 0), ("P", "ps3", 0), ("P", "ps3", 1)], writes=[("sct", 1)])
                attn_toks[dt_i] = P.emit("vector", lambda v, _d=dt_i:
                       v.tensor_tensor(attn[:, _d * T:(_d + 1) * T], attn[:, _d * T:(_d + 1) * T],
                                       sctch[:, T:2 * T], ALU.add),
                       inc=sem.v, reads=[("at", dt_i), ("sct", 1)], writes=[("at", dt_i)])
        a.attn_toks = attn_toks
        return a

    GELU_TANH_OK = True

    def emit_decode(self, pk_ap, sc_ap, out_ap, dense_shift, tmpa, tmpb, keys_in, keys_sc, keys_out, key_tmp):
        P, sem = self.P, self.sem
        pk32 = pk_ap.bitcast(I32)
        W2 = pk32.shape[-1]
        a32 = tmpa[:, 0:W2]
        b32 = tmpb[:, 0:W2]
        j = dense_shift
        if j == 0:
            P.emit("vector", lambda v: v.tensor_scalar(a32, pk32, 0x00070007, 9, ALU.bitwise_and, ALU.logical_shift_left),
                   reads=keys_in, writes=[key_tmp + ("a",)])
            P.emit("vector", lambda v: v.tensor_scalar(b32, pk32, 0x00080008, 12, ALU.bitwise_and, ALU.logical_shift_left),
                   reads=keys_in, writes=[key_tmp + ("b",)])
        else:
            P.emit("vector", lambda v: v.tensor_scalar(a32, pk32, 4 * j, 0x00070007, ALU.logical_shift_right, ALU.bitwise_and),
                   reads=keys_in, writes=[key_tmp + ("a",)])
            P.emit("vector", lambda v: v.tensor_scalar(a32, a32, 9, None, ALU.logical_shift_left),
                   reads=[key_tmp + ("a",)], writes=[key_tmp + ("a",)])
            P.emit("vector", lambda v: v.tensor_scalar(b32, pk32, 4 * j, 0x00080008, ALU.logical_shift_right, ALU.bitwise_and),
                   reads=keys_in, writes=[key_tmp + ("b",)])
            P.emit("vector", lambda v: v.tensor_scalar(b32, b32, 12, None, ALU.logical_shift_left),
                   reads=[key_tmp + ("b",)], writes=[key_tmp + ("b",)])
        P.emit("vector", lambda v: v.tensor_tensor(a32, a32, b32, ALU.bitwise_or),
               reads=[key_tmp + ("a",), key_tmp + ("b",)], writes=[key_tmp + ("a",)])
        P.emit("vector", lambda v: v.tensor_tensor(out_ap, a32.bitcast(FP16), sc_ap, ALU.mult),
               inc=sem.v, reads=[key_tmp + ("a",)] + keys_sc, writes=keys_out)

    def emit_l0_ffn(self, b):
        P, sem = self.P, self.sem
        gpk = self.din("l0_gpk", [16, 128, 8192], U16)
        upk = self.din("l0_upk", [16, 128, 8192], U16)
        dpk = self.din("l0_dpk", [16, 128, 8192], U16)
        gsc = self.din("l0_gsc", [16, 2, 8, 4096], F32)
        usc = self.din("l0_usc", [16, 2, 8, 4096], F32)
        dsc = self.din("l0_dsc", [16, 2, 8, 4096], F32)

        x1 = b.x1_aps
        x1k = b.x1_keys
        wbuf = b.wbuf
        pkg, pku = b.pkg, b.pku
        scg, scu = b.scg, b.scu
        inter = b.inter
        y = b.y_acc
        tmpa, tmpb = b.tmpa, b.tmpb
        gb, ub = b.gb, b.ub
        PS = [b.ps_g, b.ps_u, b.ps_d, b.ps_3]
        PSN = ["ps0", "ps1", "ps2", "ps3"]

        stage = 0
        y_keys = [("y", hc) for hc in range(NT)]
        for ncb in range(16):
            half_bid = {}
            for half in range(2):
                bid = (stage + half) % 2
                half_bid[half] = bid
                buf = wbuf[bid]
                for pc in range(2):
                    P.dma("sync", self.S("pkg"), pkg[:],
                          gpk[ncb].rearrange("p (wc o) -> p wc o", wc=8)[:, pc * 4:pc * 4 + 4, half * 512:half * 512 + 512],
                          writes=[("pkg",)])
                    P.dma("sync", self.S("pku"), pku[:],
                          upk[ncb].rearrange("p (wc o) -> p wc o", wc=8)[:, pc * 4:pc * 4 + 4, half * 512:half * 512 + 512],
                          writes=[("pku",)])
                    P.dma("gpsimd", self.S("scg"), scg[:],
                          gsc[ncb, half, :, pc * 2048:(pc + 1) * 2048].unsqueeze(1).broadcast_to((8, 16, 2048)),
                          writes=[("scg",)])
                    P.dma("gpsimd", self.S("scu"), scu[:],
                          usc[ncb, half, :, pc * 2048:(pc + 1) * 2048].unsqueeze(1).broadcast_to((8, 16, 2048)),
                          writes=[("scu",)])
                    for wc in range(pc * 4, pc * 4 + 4):
                        off = (wc % 4) * 512
                        for nib in range(2):
                            ki = nib * 8 + wc
                            self.emit_decode(pkg[:, off:off + 512], scg[:, off:off + 512],
                                             buf[:, ki * 512:(ki + 1) * 512], nib, tmpa, tmpb,
                                             [("pkg",)], [("scg",)], [("wb", bid, ki)], ("t",))
                            self.emit_decode(pku[:, off:off + 512], scu[:, off:off + 512],
                                             buf[:, (16 + ki) * 512:(17 + ki) * 512], nib, tmpa, tmpb,
                                             [("pku",)], [("scu",)], [("wb", bid, 16 + ki)], ("t",))
            def emit_mm_half(half):
                bid = half_bid[half]
                buf = wbuf[bid]
                for si in range(4):
                    wsub = 2 * half + si // 2
                    nib_o = si % 2
                    off = 256 * (wsub % 2)
                    sub_idx = wsub * 2 + nib_o
                    pg, pgn = (PS[0], PSN[0]) if si % 2 == 0 else (PS[2], PSN[2])
                    pu, pun = (PS[1], PSN[1]) if si % 2 == 0 else (PS[3], PSN[3])
                    for ki in range(16):
                        lhs_g = buf[:, ki * 512 + off:ki * 512 + off + 256].rearrange(
                            "p (a two) -> p a two", two=2)[:, :, nib_o]
                        lhs_u = buf[:, (16 + ki) * 512 + off:(16 + ki) * 512 + off + 256].rearrange(
                            "p (a two) -> p a two", two=2)[:, :, nib_o]
                        for nh in range(2):
                            P.emit("tensor", lambda t, _l=lhs_g, _ki=ki, _nh=nh, _ps=pg:
                                   t.matmul(_ps[:, _nh * 512:_nh * 512 + 512], _l,
                                            x1[_ki][:, _nh * 512:_nh * 512 + 512],
                                            start=_ki == 0, stop=_ki == 15),
                                   inc=sem.t if (ki == 15 and nh == 1) else None,
                                   reads=[("wb", bid, ki), x1k[ki]], writes=[("P", pgn, nh)])
                            P.emit("tensor", lambda t, _l=lhs_u, _ki=ki, _nh=nh, _ps=pu:
                                   t.matmul(_ps[:, _nh * 512:_nh * 512 + 512], _l,
                                            x1[_ki][:, _nh * 512:_nh * 512 + 512],
                                            start=_ki == 0, stop=_ki == 15),
                                   inc=sem.t if (ki == 15 and nh == 1) else None,
                                   reads=[("wb", bid, 16 + ki), x1k[ki]], writes=[("P", pun, nh)])
                    it = inter[:, sub_idx * T:(sub_idx + 1) * T]
                    gk = ("gb", si % 2)
                    uk = ("ub", si % 2)
                    gbx = gb[:, (si % 2) * T:(si % 2) * T + T]
                    ubx = ub[:, (si % 2) * T:(si % 2) * T + T]
                    P.emit("scalar", lambda s, _g=gbx, _ps=pg: s.activation(_g, _ps[:], AF.Gelu_apprx_tanh, scale=float(2 ** 14)),
                           inc=sem.s, reads=[("P", pgn, 0), ("P", pgn, 1)], writes=[gk])
                    P.emit("scalar", lambda s, _u=ubx, _ps=pu: s.activation(_u, _ps[:], AF.Copy, scale=float(2 ** 14)),
                           inc=sem.s, reads=[("P", pun, 0), ("P", pun, 1)], writes=[uk])
                    P.emit("vector", lambda v, _it=it, _g=gbx, _u=ubx: v.scalar_tensor_tensor(_it, _g, float(2 ** 14), _u, ALU.mult, ALU.mult),
                           inc=sem.v, reads=[gk, uk], writes=[("int", sub_idx)])

            def emit_down_dec(bid):
                buf = wbuf[bid]
                for ws4 in range(4):
                    pkb, pkk = (pkg, "pkg") if ws4 % 2 == 0 else (pku, "pku")
                    scb, sck = (scg, "scg") if ws4 % 2 == 0 else (scu, "scu")
                    P.dma("sync", self.S(pkk), pkb[:],
                          dpk[ncb].rearrange("p (ws h) -> p ws h", ws=4)[:, ws4, :],
                          writes=[(pkk,)])
                    P.dma("gpsimd", self.S(sck), scb[:],
                          dsc[ncb, ws4 // 2, :, (ws4 % 2) * 2048:(ws4 % 2) * 2048 + 2048].unsqueeze(1).broadcast_to((8, 16, 2048)),
                          writes=[(sck,)])
                    for nib in range(2):
                        pl8 = ws4 * 2 + nib
                        for piece in range(2):
                            self.emit_decode(pkb[:, piece * 1024:piece * 1024 + 1024],
                                             scb[:, piece * 1024:piece * 1024 + 1024],
                                             buf[:, pl8 * 2048 + piece * 1024:pl8 * 2048 + piece * 1024 + 1024],
                                             nib, tmpa, tmpb,
                                             [(pkk,)], [(sck,)],
                                             [("wb", bid, pl8 * 4 + piece * 2),
                                              ("wb", bid, pl8 * 4 + piece * 2 + 1)], ("t",))

            emit_mm_half(0)
            down_bid = stage % 2
            emit_down_dec(down_bid)
            emit_mm_half(1)
            stage += 2
            bid = stage % 2
            buf = wbuf[bid]
            for hc in range(NT):
                hoff = 256 * (hc % 8)
                hpar = 1 if hc >= 8 else 0
                pd, pdn = (PS[2], PSN[2]) if hc % 2 == 0 else (PS[3], PSN[3])
                for pl8 in range(8):
                    lhs = buf[:, pl8 * 2048 + hoff:pl8 * 2048 + hoff + 256].rearrange(
                        "p (a two) -> p a two", two=2)[:, :, hpar]
                    rkeys = [("wb", bid, pl8 * 4 + q) for q in range(4)]
                    for nh in range(2):
                        P.emit("tensor", lambda t, _l=lhs, _pl=pl8, _nh=nh, _ps=pd, _f=pl8 == 0, _la=pl8 == 7:
                               t.matmul(_ps[:, _nh * 512:_nh * 512 + 512], _l,
                                        inter[:, _pl * T + _nh * 512:_pl * T + _nh * 512 + 512],
                                        start=_f, stop=_la),
                               inc=sem.t if (pl8 == 7 and nh == 1) else None,
                               reads=rkeys + [("int", pl8)], writes=[("P", pdn, nh)])
                if ncb == 0:
                    P.emit("scalar", lambda s, _hc=hc, _ps=pd: s.copy(y[:, _hc * T:(_hc + 1) * T], _ps[:]),
                           inc=sem.s, reads=[("P", pdn, 0), ("P", pdn, 1)], writes=[("y", hc)])
                else:
                    P.emit("vector", lambda v, _hc=hc, _ps=pd: v.tensor_tensor(y[:, _hc * T:(_hc + 1) * T],
                           y[:, _hc * T:(_hc + 1) * T], _ps[:], ALU.add),
                           inc=sem.v, reads=[("y", hc), ("P", pdn, 0), ("P", pdn, 1)], writes=[("y", hc)])
            stage += 1
        return y_keys

    def emit_resid_ln(self, x_hbm_ap, add_aps, add_keys, out_aps, out_keys, r_sb, sq, rows,
                      ps_s1, ps_s2, ps_b1, ps_b2, kn):
        P, sem = self.P, self.sem
        P.dma("sync", self.S("xr"), r_sb[:].rearrange("p (kc t) -> p kc t", kc=16),
              x_hbm_ap.rearrange("(kc p) t -> p kc t", p=128),
              writes=[(kn["r"], kc) for kc in range(NT)])
        r_keys = []
        for kc in range(NT):
            r = r_sb[:, kc * T:(kc + 1) * T]
            P.emit("vector", lambda v, _r=r, _a=add_aps[kc]: v.tensor_tensor(_r, _r, _a, ALU.add),
                   inc=sem.v, reads=[(kn["r"], kc), add_keys[kc]], writes=[(kn["r"], kc)])
            r_keys.append((kn["r"], kc))
        r_aps = [r_sb[:, kc * T:(kc + 1) * T] for kc in range(NT)]
        return self.emit_ln(r_aps, r_keys, out_aps, out_keys, sq, rows, ps_s1, ps_s2, ps_b1, ps_b2, kn)


def alloc_phase_a_scratch(mk, right):
    nc = mk.nc
    a = NS()
    R = lambda *ar: right.enter_context(nc.sbuf_tensor(*ar, side="right"))
    a.q_sb = R("q_sb", [128, 16 * T], BF16)
    a.k_sb = R("k_sb", [128, 4 * S], BF16)
    a.v_tm = R("v_tm", [128, 32 * 520], BF16)
    return a


def build_phase_a(nc, upto='full'):
    es = ExitStack()
    with es:
        P = Prog(nc)
        mk = MK(nc, es, P)
        x1_out = mk.dout("x1_out", [D, T], F32)
        dbg = mk.dout("dbg", [128, 20 * 1024], F32)

        mk.setup_consts()
        attn = mk.sbuf("attn", [128, 16 * T], BF16)
        mem_sb = mk.sbuf("mem_sb", [128, 4 * 512], BF16)
        memn_sb = mk.sbuf("memn_sb", [128, 4], BF16)
        mw_sb = mk.sbuf("mw_sb", [1, 8], F32)

        ps0 = mk.psum("ps0", [128, 1024])
        ps1 = mk.psum("ps1", [128, 1024])
        ps2 = mk.psum("ps2", [128, 1024])
        ps3 = mk.psum("ps3", [128, 1024])

        a = NS()
        a.attn = attn; a.mem_sb = mem_sb; a.memn_sb = memn_sb; a.mw_sb = mw_sb
        a.ps_sc = [ps0, ps1]; a.ps_a = ps2; a.ps_bc = ps3

        def dump_and_finish(tile, ncols, rkeys):
            tok_out = mk.P.dma("gpsimd", mk.S("out"), dbg[:, 0:ncols], tile[:, 0:ncols], reads=rkeys)
            mk.P.raw("gpsimd", lambda e, _t=tok_out: e.wait_ge(_t.sem.h, _t.tgt))
            P.finalize(es.enter_context(nc.Block()), es)

        with ExitStack() as ph:
            R = lambda *ar: ph.enter_context(nc.sbuf_tensor(*ar, side="right"))
            a.q_sb = R("q_sb", [128, 16 * T], BF16)
            a.k_sb = R("k_sb", [128, 4 * S], BF16)
            a.v_tm = R("v_tm", [128, 32 * 520], BF16)
            with ExitStack() as ph12:
                R2 = lambda *ar: ph12.enter_context(nc.sbuf_tensor(*ar, side="right"))
                a.xsl = R2("xsl", [128, 2 * 8192], BF16)
                a.wkv_sb = R2("wkv_sb", [128, 2 * 8192], BF16)
                a.wqb = R2("wqb", [128, 2 * D], BF16)
                mk.emit_l0_attn(a, upto)
            if upto == 'a1':
                dump_and_finish(a.k_sb, 16384, [("k",)])
                return nc
            if upto == 'a1v':
                dump_and_finish(a.v_tm, 16640, [("v",)])
                return nc
            if upto == 'a2':
                dump_and_finish(a.q_sb, 16384, [("q",)])
                return nc
            P.join([("xsl", 0), ("xsl", 1), ("wk",), ("wv",), ("wqb", 0), ("wqb", 1)])
            with ExitStack() as ph34:
                R3 = lambda *ar: ph34.enter_context(nc.sbuf_tensor(*ar, side="right"))
                a.sig_sb = R3("sig_sb", [128, 4 * T], BF16)
                a.sctch = R3("sctch", [128, 2 * T], F32)
                a.pbuf = R3("pbuf", [128, 3 * T], BF16)
                a.tmp64 = R3("tmp64", [64, T], BF16)
                a.rowsA = R3("rowsA", [1, T], F32)
                a.rbf = R3("rbf", [1, T], BF16)
                if upto.startswith('a3:'):
                    _, nqh_s, sub_s = upto.split(':')
                    mk.emit_l0_scores(a, nqh=int(nqh_s), sub=sub_s)
                    dump_and_finish(a.pbuf, 2048, [("pb", 0), ("pb", 1)])
                    return nc
                mk.emit_l0_scores(a)
                if upto == 'a3':
                    dump_and_finish(a.attn, 16384, [("at", i) for i in range(16)])
                    return nc
                mk.emit_a_mem(a)
                if upto == 'a4':
                    dump_and_finish(a.attn, 16384, [("at", i) for i in range(16)])
                    return nc
        P.join([("q",), ("k",), ("v",), ("sig",), ("sct", 0), ("sct", 1), ("pb", 0), ("pb", 1),
                ("tmp",), ("r1",), ("rbf",), ("xsl", 0), ("xsl", 1), ("wk",), ("wv",), ("wqb", 0), ("wqb", 1)])
        with ExitStack() as ph5:
            R5 = lambda *ar: ph5.enter_context(nc.sbuf_tensor(*ar, side="right"))
            r_sb = R5("r_sb", [128, 16 * T], F32)
            sq = R5("sq", [128, 2 * T], F32)
            rows = R5("rows", [1, 5 * T], F32)
            out_aps = [attn[:, kc * T:(kc + 1) * T] for kc in range(NT)]
            kn = dict(r="rA", sq="sqA", rows="rwA", s1=("P", "ps0"), s2=("P", "ps1"),
                      b1=("P", "ps2"), b2=("P", "ps3"))
            mk.emit_resid_ln(mk.ap_xT_my, out_aps, [("at", kc) for kc in range(NT)],
                             out_aps, [("x1", kc) for kc in range(NT)],
                             r_sb, sq, rows, ps0[0:1, :], ps1[0:1, :], ps2, ps3, kn)
            tok_out = None
            for kc in range(NT):
                tok_out = mk.P.dma("gpsimd", mk.S("out"),
                                   x1_out.rearrange("(kc p) t -> kc p t", p=128)[kc],
                                   attn[:, kc * T:(kc + 1) * T], reads=[("x1", kc)])
            mk.P.raw("gpsimd", lambda e, _t=tok_out: e.wait_ge(_t.sem.h, _t.tgt))
        P.finalize(es.enter_context(nc.Block()), es)
    return nc


def build_phase_b(nc, upto='full'):
    es = ExitStack()
    with es:
        P = Prog(nc)
        mk = MK(nc, es, P)
        x2_out = mk.dout("x2_out", [D, T], F32)
        dbg = mk.dout("dbg", [128, 16 * 1024], F32)
        stream_in = mk.din("stream_in", [D, T], F32)

        mk.setup_consts()
        x1_sb = mk.sbuf("x1_sb", [128, 16 * T], BF16)
        ps_g = mk.psum("psg", [128, 1024])
        ps_u = mk.psum("psu", [128, 1024])
        ps_d = mk.psum("psd", [128, 1024])
        ps_3 = mk.psum("ps3", [128, 1024])

        P.dma("gpsimd", mk.S("si"), x1_sb[:].rearrange("p (kc t) -> p kc t", kc=16),
              stream_in.rearrange("(kc p) t -> p kc t", p=128),
              writes=[("x1", kc) for kc in range(NT)])

        b = NS()
        b.x1_aps = [x1_sb[:, kc * T:(kc + 1) * T] for kc in range(NT)]
        b.x1_keys = [("x1", kc) for kc in range(NT)]
        b.ps_g, b.ps_u, b.ps_d, b.ps_3 = ps_g, ps_u, ps_d, ps_3

        def dump_and_finish(tile, ncols, rkeys):
            tok_out = mk.P.dma("gpsimd", mk.S("out"), dbg[:, 0:ncols], tile[:, 0:ncols], reads=rkeys)
            mk.P.raw("gpsimd", lambda e, _t=tok_out: e.wait_ge(_t.sem.h, _t.tgt))
            P.finalize(es.enter_context(nc.Block()), es)

        y_guard = nc.sbuf_tensor("y_acc", [128, 16 * T], F32, side="right")
        b.y_acc = es.enter_context(y_guard)
        with ExitStack() as ph:
            R = lambda *ar: ph.enter_context(nc.sbuf_tensor(*ar, side="right"))
            b.wbuf = [R("wbuf0", [128, 16384], BF16), R("wbuf1", [128, 16384], BF16)]
            b.pkg = R("pkg", [128, 2048], U16)
            b.pku = R("pku", [128, 2048], U16)
            b.scg = R("scg", [128, 2048], FP16)
            b.scu = R("scu", [128, 2048], FP16)
            b.inter = R("inter", [128, 8 * T], BF16)
            b.tmpa = R("tmpa", [128, 512], I32)
            b.tmpb = R("tmpb", [128, 512], I32)
            b.gb = R("gb", [128, 2 * T], BF16)
            b.ub = R("ub", [128, 2 * T], BF16)
            y_keys = mk.emit_l0_ffn(b)
        if upto == 'y':
            dump_and_finish(b.y_acc, 16384, y_keys)
            return nc
        for kc in range(NT):
            P.emit("vector", lambda v, _kc=kc: v.tensor_tensor(
                   b.y_acc[:, _kc * T:(_kc + 1) * T], b.y_acc[:, _kc * T:(_kc + 1) * T],
                   b.x1_aps[_kc], ALU.add),
                   inc=mk.sem.v, reads=[("y", kc), ("x1", kc)], writes=[("y", kc)])
        P.join([("wb", bid, u) for bid in range(2) for u in range(32)] +
               [("int", si) for si in range(8)] +
               [("pkg",), ("pku",), ("scg",), ("scu",), ("gb", 0), ("gb", 1), ("ub", 0), ("ub", 1),
                ("t", "a"), ("t", "b")])
        with ExitStack() as ph2:
            R2 = lambda *ar: ph2.enter_context(nc.sbuf_tensor(*ar, side="right"))
            sq = R2("sq", [128, 2 * T], F32)
            rows = R2("rows", [1, 5 * T], F32)
            r_aps = [b.y_acc[:, kc * T:(kc + 1) * T] for kc in range(NT)]
            kn = dict(sq="sqB", rows="rwB", s1=("P", "ps0"), s2=("P", "ps1"),
                      b1=("P", "ps2"), b2=("P", "ps3"))
            mk.emit_ln(r_aps, y_keys, b.x1_aps, [("x2", kc) for kc in range(NT)],
                       sq, rows, ps_g[0:1, :], ps_u[0:1, :], ps_d, ps_3, kn)
            tok_out = None
            for kc in range(NT):
                tok_out = mk.P.dma("gpsimd", mk.S("out"),
                                   x2_out.rearrange("(kc p) t -> kc p t", p=128)[kc],
                                   x1_sb[:, kc * T:(kc + 1) * T], reads=[("x2", kc)])
            mk.P.raw("gpsimd", lambda e, _t=tok_out: e.wait_ge(_t.sem.h, _t.tgt))
        P.finalize(es.enter_context(nc.Block()), es)
    return nc


DENOM_L1 = float(np.sqrt(2048.0) / 4096.0)


def build_phase_c(nc, upto='full', ncores=NCORES):
    es = ExitStack()
    with es:
        P = Prog(nc)
        mk = MK(nc, es, P)
        sem = mk.sem
        x3_out = mk.dout("x3_out", [D, T], F32)
        dbg = mk.dout("dbg", [128, 16 * 1024], F32)
        stream_in = mk.din("stream_in", [D, T], F32)
        wq = mk.din("l1_wq", [16, 128, D], F32)
        wk = mk.din("l1_wk", [16, 128, D], F32)
        wv = mk.din("l1_wv", [16, 128, D], F32)
        wo = mk.din("l1_wo", [16, 128, D], F32)
        cmask = mk.din("cmask", [128, 8], F32)
        cc_in = nc.dram_tensor("cc_in", [128, 16], F32, kind="Internal")
        cc_out = nc.dram_tensor("cc_out", [128 * ncores, 16], F32, kind="Internal")

        mk.setup_consts()
        x2_sb = mk.sbuf("x2_sb", [128, 16 * T], BF16)
        cm_sb = mk.sbuf("cm_sb", [128, 8], F32)
        cs_sb = mk.sbuf("cs_sb", [128, 16], F32)
        cy_sb = mk.sbuf("cy_sb", [128, 16], F32)
        ag_sb = mk.sbuf("ag_sb", [128, 16 * ncores], F32)
        ps0 = mk.psum("ps0", [128, 1024])
        ps1 = mk.psum("ps1", [128, 1024])
        ps2 = mk.psum("ps2", [128, 1024])
        ps3 = mk.psum("ps3", [128, 1024])

        P.dma("gpsimd", mk.S("si"), x2_sb[:].rearrange("p (kc t) -> p kc t", kc=16),
              stream_in.rearrange("(kc p) t -> p kc t", p=128),
              writes=[("x2", kc) for kc in range(NT)])
        P.dma("sync", mk.S("cm"), cm_sb[:], cmask, writes=[("cm",)])

        def dump_and_finish(tile, ncols, rkeys):
            tok_out = mk.P.dma("gpsimd", mk.S("out"), dbg[:, 0:ncols], tile[:, 0:ncols], reads=rkeys)
            mk.P.raw("gpsimd", lambda e, _t=tok_out: e.wait_ge(_t.sem.h, _t.tgt))
            P.finalize(es.enter_context(nc.Block()), es)

        q2 = es.enter_context(nc.sbuf_tensor("q2", [128, 16 * T], BF16, side="right"))
        wb = es.enter_context(nc.sbuf_tensor("wb", [128, 2 * D], BF16, side="right"))

        X2 = lambda kc: x2_sb[:, kc * T:(kc + 1) * T]

        def proj(dst_writer, w_ap, wkey_pfx, extra=None):
            for oc in range(16):
                bi = oc % 2
                P.dma("gpsimd", mk.S(wkey_pfx + str(bi)), wb[:, bi * D:(bi + 1) * D], w_ap[oc],
                      writes=[(wkey_pfx, bi)])
                for kc in range(NT):
                    for nh in range(2):
                        P.emit("tensor", lambda t, _bi=bi, _kc=kc, _nh=nh, _ps=dst_writer[0]:
                               t.matmul(_ps[:, _nh * 512:_nh * 512 + 512],
                                        wb[:, _bi * D + _kc * 128:_bi * D + _kc * 128 + 128],
                                        X2(_kc)[:, _nh * 512:_nh * 512 + 512],
                                        start=_kc == 0, stop=_kc == NT - 1),
                               inc=sem.t if (kc == NT - 1 and nh == 1) else None,
                               reads=[(wkey_pfx, bi), ("x2", kc)], writes=[dst_writer[1] + (nh,)])
                dst_writer[2](oc)

        with ExitStack() as phkv:
            Rr = lambda *ar: phkv.enter_context(nc.sbuf_tensor(*ar, side="right"))
            kv_sb = Rr("kv_sb", [128, 16 * T], BF16)
            kbuf = Rr("kbuf", [128, T], BF16)
            kvt_t = Rr("kvt_t", [128, T], BF16)

            def q_ev(oc):
                P.emit("scalar", lambda s, _oc=oc: s.copy(q2[:, _oc * T:(_oc + 1) * T], ps0[:]),
                       inc=sem.s, reads=[("P", "ps0", 0), ("P", "ps0", 1)], writes=[("q2", oc)])
            proj((ps0, ("P", "ps0"), q_ev)[0:3], wq, "wA")

            for oc in range(16):
                bi = oc % 2
                P.dma("gpsimd", mk.S("wA" + str(bi)), wb[:, bi * D:(bi + 1) * D], wk[oc], writes=[("wA", bi)])
                for kc in range(NT):
                    for nh in range(2):
                        P.emit("tensor", lambda t, _bi=bi, _kc=kc, _nh=nh:
                               t.matmul(ps0[:, _nh * 512:_nh * 512 + 512],
                                        wb[:, _bi * D + _kc * 128:_bi * D + _kc * 128 + 128],
                                        X2(_kc)[:, _nh * 512:_nh * 512 + 512],
                                        start=_kc == 0, stop=_kc == NT - 1),
                               inc=sem.t if (kc == NT - 1 and nh == 1) else None,
                               reads=[("wA", bi), ("x2", kc)], writes=[("P", "ps0", 0 if nh == 0 else 1)])
                P.emit("scalar", lambda s: s.copy(kbuf[:], ps0[:]),
                       inc=sem.s, reads=[("P", "ps0", 0), ("P", "ps0", 1)], writes=[("kb",)])
                P.dma("gpsimd", mk.S("wB" + str(bi)), wb[:, bi * D:(bi + 1) * D], wv[oc], writes=[("wA", bi)])
                for kc in range(NT):
                    for nh in range(2):
                        P.emit("tensor", lambda t, _bi=bi, _kc=kc, _nh=nh:
                               t.matmul(ps1[:, _nh * 512:_nh * 512 + 512],
                                        wb[:, _bi * D + _kc * 128:_bi * D + _kc * 128 + 128],
                                        X2(_kc)[:, _nh * 512:_nh * 512 + 512],
                                        start=_kc == 0, stop=_kc == NT - 1),
                               inc=sem.t if (kc == NT - 1 and nh == 1) else None,
                               reads=[("wA", bi), ("x2", kc)], writes=[("P", "ps1", nh)])
                P.emit("vector", lambda v, _oc=oc: v.tensor_tensor(kv_sb[:, _oc * T:(_oc + 1) * T], kbuf[:], ps1[:], ALU.mult),
                       inc=sem.v, reads=[("kb",), ("P", "ps1", 0), ("P", "ps1", 1)], writes=[("kv", oc)])
                P.emit("vector", lambda v, _oc=oc: v.tensor_reduce(cs_sb[:, _oc:_oc + 1], kv_sb[:, _oc * T:(_oc + 1) * T], AX.X, ALU.add),
                       inc=sem.v, reads=[("kv", oc)], writes=[("cs",)])

            P.dma("gpsimd", mk.S("ccw"), cc_in.ap(), cs_sb[:], reads=[("cs",)], writes=[("ccin",)])
            P.emit("gpsimd", lambda g: g.collective_compute(
                       "AllGather", ALU.bypass, replica_groups=[list(range(ncores))],
                       ins=[cc_in.ap()], outs=[cc_out.ap()]),
                   inc=sem.cc, reads=[("ccin",)], writes=[("ccout",)])
            P.dma("gpsimd", mk.S("agr"), ag_sb[:, 0:16 * ncores],
                  cc_out.ap().rearrange("(r p) oc -> p r oc", p=128), reads=[("ccout",)], writes=[("ag",)])
            P.emit("vector", lambda v: v.memset(cy_sb[:], 0.0), writes=[("cy",)])
            for r in range(ncores):
                P.emit("vector", lambda v, _r=r: v.scalar_tensor_tensor(
                           cy_sb[:], ag_sb[:, _r * 16:(_r + 1) * 16], cm_sb[:, _r:_r + 1], cy_sb[:],
                           ALU.mult, ALU.add),
                       inc=sem.v if r == ncores - 1 else None,
                       reads=[("ag",), ("cm",), ("cy",)], writes=[("cy",)])

            for oc in range(16):
                P.emit("vector", lambda v, _oc=oc: v.tensor_tensor_scan(
                           kvt_t[:], kv_sb[:, _oc * T:(_oc + 1) * T], kv_sb[:, _oc * T:(_oc + 1) * T],
                           cy_sb[:, _oc:_oc + 1], ALU.add, ALU.bypass),
                       inc=sem.v, reads=[("kv", oc), ("cy",)], writes=[("kvt",)])
                P.emit("vector", lambda v, _oc=oc: v.scalar_tensor_tensor(
                           q2[:, _oc * T:(_oc + 1) * T], q2[:, _oc * T:(_oc + 1) * T], DENOM_L1, kvt_t[:],
                           ALU.mult, ALU.mult),
                       inc=sem.v, reads=[("q2", oc), ("kvt",)], writes=[("q2", oc)])

        if upto == 'vhat':
            dump_and_finish(q2, 16384, [("q2", oc) for oc in range(16)])
            return nc

        P.join([("kv", oc) for oc in range(16)] + [("kb",), ("kvt",)])
        with ExitStack() as phr:
            Rr = lambda *ar: phr.enter_context(nc.sbuf_tensor(*ar, side="right"))
            r_sb = Rr("r_sb", [128, 16 * T], F32)
            sq = Rr("sq", [128, 2 * T], F32)
            rows = Rr("rows", [1, 5 * T], F32)
            for oc in range(16):
                bi = oc % 2
                P.dma("gpsimd", mk.S("wA" + str(bi)), wb[:, bi * D:(bi + 1) * D], wo[oc], writes=[("wA", bi)])
                for kc in range(NT):
                    for nh in range(2):
                        P.emit("tensor", lambda t, _bi=bi, _kc=kc, _nh=nh:
                               t.matmul(ps2[:, _nh * 512:_nh * 512 + 512],
                                        wb[:, _bi * D + _kc * 128:_bi * D + _kc * 128 + 128],
                                        q2[:, _kc * T + _nh * 512:_kc * T + _nh * 512 + 512],
                                        start=_kc == 0, stop=_kc == NT - 1),
                               inc=sem.t if (kc == NT - 1 and nh == 1) else None,
                               reads=[("wA", bi), ("q2", kc)], writes=[("P", "ps2", nh)])
                P.emit("vector", lambda v, _oc=oc: v.tensor_tensor(r_sb[:, _oc * T:(_oc + 1) * T],
                           x2_sb[:, _oc * T:(_oc + 1) * T], ps2[:], ALU.add),
                       inc=sem.v, reads=[("x2", oc), ("P", "ps2", 0), ("P", "ps2", 1)], writes=[("rC", oc)])
            r_aps = [r_sb[:, kc * T:(kc + 1) * T] for kc in range(NT)]
            kn = dict(sq="sqC", rows="rwC", s1=("P", "ps0"), s2=("P", "ps1"),
                      b1=("P", "ps2"), b2=("P", "ps3"))
            P.join([("q2", oc) for oc in range(16)], engines=("tensor",))
            mk.emit_ln(r_aps, [("rC", kc) for kc in range(NT)],
                       [X2(kc) for kc in range(NT)], [("x3", kc) for kc in range(NT)],
                       sq, rows, ps0[0:1, :], ps1[0:1, :], ps2, ps3, kn)
            tok_out = None
            for kc in range(NT):
                tok_out = mk.P.dma("gpsimd", mk.S("out"),
                                   x3_out.rearrange("(kc p) t -> kc p t", p=128)[kc],
                                   x2_sb[:, kc * T:(kc + 1) * T], reads=[("x3", kc)])
            mk.P.raw("gpsimd", lambda e, _t=tok_out: e.wait_ge(_t.sem.h, _t.tgt))
        P.finalize(es.enter_context(nc.Block()), es)
    return nc


L1_CHUNKS = [(0, 2), (2, 2), (4, 2), (6, 2), (8, 2), (10, 1)]


def build_phase_d(nc, upto='full'):
    es = ExitStack()
    with es:
        P = Prog(nc)
        mk = MK(nc, es, P)
        sem = mk.sem
        x4_out = mk.dout("x4_out", [D, T], F32)
        dbg = mk.dout("dbg", [128, 16 * 1024], F32)
        stream_in = mk.din("stream_in", [D, T], F32)
        gpk = mk.din("l1_gpk", [22, 128, 2048], U16)
        dpk = mk.din("l1_dpk", [11, 128, 2048], U16)
        gsc = mk.din("l1_gsc", [22, 4, 4, 512], FP16)
        dsc = mk.din("l1_dsc", [11, 4, 2048], FP16)
        x3d = nc.dram_tensor("x3d", [D, T], BF16, kind="Internal")

        mk.setup_consts()
        x3_sb = mk.sbuf("x3_sb", [128, 16 * T], BF16)
        ps0 = mk.psum("ps0", [128, 1024])
        ps1 = mk.psum("ps1", [128, 1024])
        ps2 = mk.psum("ps2", [128, 1024])
        ps3 = mk.psum("ps3", [128, 1024])

        P.dma("gpsimd", mk.S("si"), x3_sb[:].rearrange("p (kc t) -> p kc t", kc=16),
              stream_in.rearrange("(kc p) t -> p kc t", p=128),
              writes=[("x3", kc) for kc in range(NT)])

        def dump_and_finish(tile, ncols, rkeys):
            tok_out = mk.P.dma("gpsimd", mk.S("out"), dbg[:, 0:ncols], tile[:, 0:ncols], reads=rkeys)
            mk.P.raw("gpsimd", lambda e, _t=tok_out: e.wait_ge(_t.sem.h, _t.tgt))
            P.finalize(es.enter_context(nc.Block()), es)

        for kc in range(NT):
            P.dma("sync", mk.S("bw"), x3d.ap().rearrange("(kc p) t -> kc p t", p=128)[kc],
                  x3_sb[:, kc * T:(kc + 1) * T], reads=[("x3", kc)], writes=[("x3d", kc)])
        xj = es.enter_context(nc.sbuf_tensor("xj", [128, 16 * T], BF16, side="right"))
        y = es.enter_context(nc.sbuf_tensor("y_acc", [128, 16 * T], F32, side="right"))
        x3r = x3d.ap().rearrange("(a two) t -> two a t", two=2)
        for jp in range(4):
            par, hi = jp % 2, jp // 2
            par = 1 if jp in (2, 3) else 0
            hi = 1 if jp in (1, 3) else 0
            for tt in range(4):
                a0 = hi * 512 + tt * 128
                P.dma("sync", mk.S("xj%d" % (jp * 4 + tt)), xj[:, (jp * 4 + tt) * T:(jp * 4 + tt + 1) * T],
                      x3r[par, a0:a0 + 128, :],
                      reads=[("x3d", kc) for kc in range(NT)], writes=[("xj", jp, tt)])

        XJ = lambda jp, tt: xj[:, (jp * 4 + tt) * T:(jp * 4 + tt + 1) * T]

        with ExitStack() as ph:
            Rr = lambda *ar: ph.enter_context(nc.sbuf_tensor(*ar, side="right"))
            wdec = Rr("wdec", [128, 16384], BF16)
            pk0 = Rr("pk0", [128, 2048], U16)
            pk1 = Rr("pk1", [128, 2048], U16)
            sc0 = Rr("sc0", [128, 2048], FP16)
            sc1 = Rr("sc1", [128, 2048], FP16)
            inter = Rr("interD", [128, 8 * T], BF16)
            tmpa = Rr("tmpa", [128, 1024], I32)
            tmpb = Rr("tmpb", [128, 1024], I32)
            gsb = Rr("gsb", [128, T], BF16)

            GUK = lambda ki: ("gu", ki)
            y_keys = [("y", hc) for hc in range(NT)]
            for ci, (cs, nt) in enumerate(L1_CHUNKS):
                for tt in range(nt):
                    for half, span in ((0, cs + tt), (1, 11 + cs + tt)):
                        pkb, pkk = (pk0, ("pk", 0)) if half == 0 else (pk1, ("pk", 1))
                        scb, sck = (sc0, ("sc", 0)) if half == 0 else (sc1, ("sc", 1))
                        P.dma("sync", mk.S("pk%d" % half), pkb[:], gpk[span], writes=[pkk])
                        for tin in range(4):
                            P.dma("sync", mk.S("sc%d%d" % (half, tin)), scb[:, tin * 512:(tin + 1) * 512],
                                  gsc[span, tin].unsqueeze(1).broadcast_to((4, 32, 512)),
                                  writes=[sck + (tin,)])
                        for jin in range(4):
                            for tin in range(4):
                                ki = jin * 4 + tin
                                dst = wdec[:, (half * 16 + ki) * 512:(half * 16 + ki + 1) * 512]
                                self_keys = [("w", half, ki)]
                                mk.emit_decode(pkb[:, tin * 512:(tin + 1) * 512], scb[:, tin * 512:(tin + 1) * 512],
                                               dst, jin, tmpa, tmpb,
                                               [pkk], [sck + (tin,)], self_keys, ("t",))
                    for jo in range(4):
                        for half in range(2):
                            psx, psk = (ps0, ("P", "ps0")) if half == 0 else (ps1, ("P", "ps1"))
                            for jin in range(4):
                                for tin in range(4):
                                    ki = jin * 4 + tin
                                    lhs = wdec[:, (half * 16 + ki) * 512:(half * 16 + ki + 1) * 512].rearrange(
                                        "p (a four) -> p a four", four=4)[:, :, jo]
                                    for nh in range(2):
                                        P.emit("tensor", lambda t, _l=lhs, _jin=jin, _tin=tin, _nh=nh, _ps=psx,
                                               _f=ki == 0, _la=ki == 15:
                                               t.matmul(_ps[:, _nh * 512:_nh * 512 + 512], _l,
                                                        XJ(_jin, _tin)[:, _nh * 512:_nh * 512 + 512],
                                                        start=_f, stop=_la),
                                               inc=sem.t if (ki == 15 and nh == 1) else None,
                                               reads=[("w", half, ki), ("xj", jin, tin)], writes=[psk + (nh,)])
                        it = inter[:, (tt * 4 + jo) * T:(tt * 4 + jo + 1) * T]
                        P.emit("scalar", lambda s: s.activation(gsb[:], ps0[:], AF.Silu, scale=float(2 ** 14)),
                               inc=sem.s, reads=[("P", "ps0", 0), ("P", "ps0", 1)], writes=[("gsb",)])
                        P.emit("vector", lambda v, _it=it: v.scalar_tensor_tensor(_it, gsb[:], float(2 ** 28), ps1[:], ALU.mult, ALU.mult),
                               inc=sem.v, reads=[("gsb",), ("P", "ps1", 0), ("P", "ps1", 1)],
                               writes=[("int", tt * 4 + jo)])
                for tt in range(nt):
                    pkb, pkk = (pk0, ("pk", 0)) if tt % 2 == 0 else (pk1, ("pk", 1))
                    scb, sck = (sc0, ("sc", 0)) if tt % 2 == 0 else (sc1, ("sc", 1))
                    P.dma("sync", mk.S("pk%d" % (tt % 2)), pkb[:], dpk[cs + tt], writes=[pkk])
                    P.dma("sync", mk.S("scd%d" % (tt % 2)), scb[:],
                          dsc[cs + tt].unsqueeze(1).broadcast_to((4, 32, 2048)), writes=[sck + (0,), sck + (1,), sck + (2,), sck + (3,)])
                    for jd in range(4):
                        pl = tt * 4 + jd
                        dst = wdec[:, pl * 2048:(pl + 1) * 2048]
                        alias = [("w", 0, 4 * pl + q) for q in range(4)] if pl < 4 else \
                                [("w", 1, 4 * (pl - 4) + q) for q in range(4)]
                        mk.emit_decode(pkb[:], scb[:], dst, jd, tmpa, tmpb,
                                       [pkk], [sck + (q,) for q in range(4)], [("dw", pl)] + alias, ("t",))
                for hc in range(NT):
                    hoff = 256 * (hc % 8)
                    hpar = 1 if hc >= 8 else 0
                    np_ = nt * 4
                    for pl in range(np_):
                        tt, jd = pl // 4, pl % 4
                        lhs = wdec[:, pl * 2048 + hoff:pl * 2048 + hoff + 256].rearrange(
                            "p (a two) -> p a two", two=2)[:, :, hpar]
                        alias_r = ([("w", 0, 4 * pl + q) for q in range(4)] if pl < 4 else
                                   [("w", 1, 4 * (pl - 4) + q) for q in range(4)])
                        for nh in range(2):
                            P.emit("tensor", lambda t, _l=lhs, _pl=pl, _nh=nh, _f=pl == 0, _la=pl == np_ - 1:
                                   t.matmul(ps2[:, _nh * 512:_nh * 512 + 512], _l,
                                            inter[:, _pl * T + _nh * 512:_pl * T + _nh * 512 + 512],
                                            start=_f, stop=_la),
                                   inc=sem.t if (pl == np_ - 1 and nh == 1) else None,
                                   reads=[("dw", pl), ("int", pl)] + alias_r, writes=[("P", "ps2", nh)])
                    if ci == 0:
                        P.emit("scalar", lambda s, _hc=hc: s.copy(y[:, _hc * T:(_hc + 1) * T], ps2[:]),
                               inc=sem.s, reads=[("P", "ps2", 0), ("P", "ps2", 1)], writes=[("y", hc)])
                    else:
                        P.emit("vector", lambda v, _hc=hc: v.tensor_tensor(y[:, _hc * T:(_hc + 1) * T],
                               y[:, _hc * T:(_hc + 1) * T], ps2[:], ALU.add),
                               inc=sem.v, reads=[("y", hc), ("P", "ps2", 0), ("P", "ps2", 1)], writes=[("y", hc)])
        if upto == 'y':
            dump_and_finish(y, 16384, y_keys)
            return nc

        P.join([("w", 0, k) for k in range(16)] + [("w", 1, k) for k in range(16)] +
               [("dw", pl) for pl in range(8)] + [("int", q) for q in range(8)] +
               [("pk", 0), ("pk", 1), ("sc", 0, 0), ("sc", 0, 1), ("sc", 0, 2), ("sc", 0, 3),
                ("sc", 1, 0), ("sc", 1, 1), ("sc", 1, 2), ("sc", 1, 3), ("gsb",), ("t", "a"), ("t", "b")])
        with ExitStack() as ph2:
            Rr = lambda *ar: ph2.enter_context(nc.sbuf_tensor(*ar, side="right"))
            sq = Rr("sq", [128, 2 * T], F32)
            rows = Rr("rows", [1, 5 * T], F32)
            RW = [("rwD",)]
            for hc in range(NT):
                buf = sq[:, (hc % 2) * T:(hc % 2) * T + T]
                P.emit("scalar", lambda s, _b=buf, _hc=hc: s.activation(_b, y[:, _hc * T:(_hc + 1) * T], AF.Square),
                       inc=sem.s, reads=[("y", hc)], writes=[("sqD", hc % 2)])
                for nh in range(2):
                    P.emit("tensor", lambda t, _b=buf, _nh=nh, _f=hc == 0, _la=hc == NT - 1:
                           t.matmul(ps0[0:1, _nh * 512:_nh * 512 + 512], mk.ones_col,
                                    _b[:, _nh * 512:_nh * 512 + 512], start=_f, stop=_la),
                           inc=sem.t if (hc == NT - 1 and nh == 1) else None,
                           reads=[("sqD", hc % 2), ("cst",)], writes=[("P", "ps0", nh)])
            msq = rows[0:1, 0:T]
            P.emit("vector", lambda v: v.tensor_scalar(msq, ps0[0:1, :], 1.0 / D, None, ALU.mult),
                   inc=sem.v, reads=[("P", "ps0", 0), ("P", "ps0", 1)], writes=RW)
            P.emit("scalar", lambda s: s.activation(msq, msq, AF.Ln, bias=mk.eps_rms),
                   reads=RW + [("cst",)], writes=RW)
            P.emit("scalar", lambda s: s.activation(msq, msq, AF.Exp, scale=-0.5), inc=sem.s,
                   reads=RW, writes=RW)
            for nh in range(2):
                P.emit("tensor", lambda t, _nh=nh: t.matmul(ps1[:, _nh * 512:_nh * 512 + 512],
                       mk.orow[0:1, 0:128], msq[0:1, _nh * 512:_nh * 512 + 512], start=True, stop=True),
                       inc=sem.t if nh == 1 else None,
                       reads=RW + [("orow",)], writes=[("P", "ps1", nh)])
            for kc in range(NT):
                P.emit("vector", lambda v, _kc=kc: v.tensor_tensor(y[:, _kc * T:(_kc + 1) * T],
                       y[:, _kc * T:(_kc + 1) * T], ps1[:], ALU.mult),
                       reads=[("y", kc), ("P", "ps1", 0), ("P", "ps1", 1)], writes=[("y", kc)])
                P.emit("vector", lambda v, _kc=kc: v.tensor_tensor(y[:, _kc * T:(_kc + 1) * T],
                       y[:, _kc * T:(_kc + 1) * T], x3_sb[:, _kc * T:(_kc + 1) * T], ALU.add),
                       inc=sem.v, reads=[("y", kc), ("x3", kc)], writes=[("y", kc)])
            r_aps = [y[:, kc * T:(kc + 1) * T] for kc in range(NT)]
            kn = dict(sq="sqD2", rows="rwD2", s1=("P", "ps0"), s2=("P", "ps2"),
                      b1=("P", "ps1"), b2=("P", "ps3"))
            mk.emit_ln(r_aps, [("y", kc) for kc in range(NT)],
                       [x3_sb[:, kc * T:(kc + 1) * T] for kc in range(NT)],
                       [("x4", kc) for kc in range(NT)],
                       sq, rows, ps0[0:1, :], ps2[0:1, :], ps1, ps3, kn)
            tok_out = None
            for kc in range(NT):
                tok_out = mk.P.dma("gpsimd", mk.S("out"),
                                   x4_out.rearrange("(kc p) t -> kc p t", p=128)[kc],
                                   x3_sb[:, kc * T:(kc + 1) * T], reads=[("x4", kc)])
            mk.P.raw("gpsimd", lambda e, _t=tok_out: e.wait_ge(_t.sem.h, _t.tgt))
        P.finalize(es.enter_context(nc.Block()), es)
    return nc



def build_fused(nc, ncores=NCORES):
    es = ExitStack()
    with es:
        P = Prog(nc)
        mk = MK(nc, es, P)
        sem = mk.sem
        out_ap = mk.dout("x4_out", [D, T], F32)

        mk.setup_consts()
        stream = mk.sbuf("stream", [128, 16 * T], BF16)
        ps0 = mk.psum("ps0", [128, 1024])
        ps1 = mk.psum("ps1", [128, 1024])
        ps2 = mk.psum("ps2", [128, 1024])
        ps3 = mk.psum("ps3", [128, 1024])
        ST = lambda kc: stream[:, kc * T:(kc + 1) * T]

        a = NS()
        a.attn = stream
        a.ps_sc = [ps0, ps1]; a.ps_a = ps2; a.ps_bc = ps3
        with ExitStack() as ph:
            R = lambda *ar: ph.enter_context(nc.sbuf_tensor(*ar, side="right"))
            a.mem_sb = R("mem_sb", [128, 4 * 512], BF16)
            a.memn_sb = R("memn_sb", [128, 4], BF16)
            a.mw_sb = R("mw_sb", [1, 8], F32)
            a.q_sb = R("q_sb", [128, 16 * T], BF16)
            a.k_sb = R("k_sb", [128, 4 * S], BF16)
            a.v_tm = R("v_tm", [128, 32 * 520], BF16)
            with ExitStack() as ph12:
                R2 = lambda *ar: ph12.enter_context(nc.sbuf_tensor(*ar, side="right"))
                a.xsl = R2("xsl", [128, 2 * 8192], BF16)
                a.wkv_sb = R2("wkv_sb", [128, 2 * 8192], BF16)
                a.wqb = R2("wqb", [128, 2 * D], BF16)
                mk.emit_l0_attn(a, 'full')
            P.join([("xsl", 0), ("xsl", 1), ("wk",), ("wv",), ("wqb", 0), ("wqb", 1)])
            with ExitStack() as ph34:
                R3 = lambda *ar: ph34.enter_context(nc.sbuf_tensor(*ar, side="right"))
                a.sig_sb = R3("sig_sb", [128, 4 * T], BF16)
                a.sctch = R3("sctch", [128, 2 * T], F32)
                a.pbuf = R3("pbuf", [128, 3 * T], BF16)
                a.tmp64 = R3("tmp64", [64, T], BF16)
                a.rowsA = R3("rowsA", [1, T], F32)
                a.rbf = R3("rbf", [1, T], BF16)
                mk.emit_l0_scores(a)
                mk.emit_a_mem(a)
        P.join([("q",), ("k",), ("v",), ("sig",), ("sct", 0), ("sct", 1), ("pb", 0), ("pb", 1),
                ("tmp",), ("r1",), ("rbf",), ("xsl", 0), ("xsl", 1), ("wk",), ("wv",),
                ("wqb", 0), ("wqb", 1), ("mem",), ("memn",), ("mw",)])
        with ExitStack() as ph5:
            R5 = lambda *ar: ph5.enter_context(nc.sbuf_tensor(*ar, side="right"))
            r_sb = R5("r_sbA", [128, 16 * T], F32)
            sq = R5("sqA", [128, 2 * T], F32)
            rows = R5("rowsA5", [1, 5 * T], F32)
            out_aps = [ST(kc) for kc in range(NT)]
            kn = dict(r="rA", sq="sqA", rows="rwA", s1=("P", "ps0"), s2=("P", "ps1"),
                      b1=("P", "ps2"), b2=("P", "ps3"))
            mk.emit_resid_ln(mk.ap_xT_my, out_aps, [("at", kc) for kc in range(NT)],
                             out_aps, [("x1", kc) for kc in range(NT)],
                             r_sb, sq, rows, ps0[0:1, :], ps1[0:1, :], ps2, ps3, kn)
        P.join([("rA", kc) for kc in range(NT)] + [("sqA", 0), ("sqA", 1), ("rwA",)])

        b = NS()
        b.x1_aps = [ST(kc) for kc in range(NT)]
        b.x1_keys = [("x1", kc) for kc in range(NT)]
        b.ps_g, b.ps_u, b.ps_d, b.ps_3 = ps0, ps1, ps2, ps3
        b.y_acc = es.enter_context(nc.sbuf_tensor("y_accB", [128, 16 * T], F32, side="right"))
        with ExitStack() as ph:
            R = lambda *ar: ph.enter_context(nc.sbuf_tensor(*ar, side="right"))
            b.wbuf = [R("wbuf0B", [128, 16384], BF16), R("wbuf1B", [128, 16384], BF16)]
            b.pkg = R("pkgB", [128, 2048], U16)
            b.pku = R("pkuB", [128, 2048], U16)
            b.scg = R("scgB", [128, 2048], FP16)
            b.scu = R("scuB", [128, 2048], FP16)
            b.inter = R("interB", [128, 8 * T], BF16)
            b.tmpa = R("tmpaB", [128, 512], I32)
            b.tmpb = R("tmpbB", [128, 512], I32)
            b.gb = R("gbB", [128, 2 * T], BF16)
            b.ub = R("ubB", [128, 2 * T], BF16)
            y_keys = mk.emit_l0_ffn(b)
        for kc in range(NT):
            P.emit("vector", lambda v, _kc=kc: v.tensor_tensor(
                   b.y_acc[:, _kc * T:(_kc + 1) * T], b.y_acc[:, _kc * T:(_kc + 1) * T],
                   ST(_kc), ALU.add),
                   inc=sem.v, reads=[("y", kc), ("x1", kc)], writes=[("y", kc)])
        P.join([("wb", bid, u) for bid in range(2) for u in range(32)] +
               [("int", si) for si in range(8)] +
               [("pkg",), ("pku",), ("scg",), ("scu",), ("gb", 0), ("gb", 1), ("ub", 0), ("ub", 1),
                ("t", "a"), ("t", "b")])
        with ExitStack() as ph2:
            R2 = lambda *ar: ph2.enter_context(nc.sbuf_tensor(*ar, side="right"))
            sq = R2("sqB", [128, 2 * T], F32)
            rows = R2("rowsB", [1, 5 * T], F32)
            r_aps = [b.y_acc[:, kc * T:(kc + 1) * T] for kc in range(NT)]
            kn = dict(sq="sqB", rows="rwB", s1=("P", "ps0"), s2=("P", "ps1"),
                      b1=("P", "ps2"), b2=("P", "ps3"))
            mk.emit_ln(r_aps, y_keys, b.x1_aps, [("x2", kc) for kc in range(NT)],
                       sq, rows, ps0[0:1, :], ps1[0:1, :], ps2, ps3, kn)
        P.join([("y", kc) for kc in range(NT)] + [("sqB", 0), ("sqB", 1), ("rwB",)])

        cmask = mk.din("cmask", [128, 8], F32)
        wq1 = mk.din("l1_wq", [16, 128, D], F32)
        wk1 = mk.din("l1_wk", [16, 128, D], F32)
        wv1 = mk.din("l1_wv", [16, 128, D], F32)
        wo1 = mk.din("l1_wo", [16, 128, D], F32)
        cc_in = nc.dram_tensor("cc_in", [128, 16], F32, kind="Internal")
        cc_out = nc.dram_tensor("cc_out", [128 * ncores, 16], F32, kind="Internal")
        cm_sb = mk.sbuf("cm_sb", [128, 8], F32)
        cs_sb = mk.sbuf("cs_sb", [128, 16], F32)
        cy_sb = mk.sbuf("cy_sb", [128, 16], F32)
        ag_sb = mk.sbuf("ag_sb", [128, 16 * ncores], F32)
        P.dma("sync", mk.S("cm"), cm_sb[:], cmask, writes=[("cm",)])

        with ExitStack() as phC:
            RC = lambda *ar: phC.enter_context(nc.sbuf_tensor(*ar, side="right"))
            q2 = RC("q2", [128, 16 * T], BF16)
            wb = RC("wbC", [128, 4 * D], BF16)
            with ExitStack() as phkv:
                Rr = lambda *ar: phkv.enter_context(nc.sbuf_tensor(*ar, side="right"))
                kv_sb = Rr("kv_sb", [128, 16 * T], BF16)
                kbuf = Rr("kbuf", [128, T], BF16)
                kvt_t = Rr("kvt_t", [128, T], BF16)
                for oc in range(16):
                    bi = (2 * oc) % 4
                    P.dma("gpsimd", mk.S("wA" + str(bi)), wb[:, bi * D:(bi + 1) * D], wk1[oc], writes=[("wA", bi)])
                    for kc in range(NT):
                        for nh in range(2):
                            P.emit("tensor", lambda t, _bi=bi, _kc=kc, _nh=nh:
                                   t.matmul(ps0[:, _nh * 512:_nh * 512 + 512],
                                            wb[:, _bi * D + _kc * 128:_bi * D + _kc * 128 + 128],
                                            ST(_kc)[:, _nh * 512:_nh * 512 + 512],
                                            start=_kc == 0, stop=_kc == NT - 1),
                                   inc=sem.t if (kc == NT - 1 and nh == 1) else None,
                                   reads=[("wA", bi), ("x2", kc)], writes=[("P", "ps0", nh)])
                    P.emit("scalar", lambda s: s.copy(kbuf[:], ps0[:]),
                           inc=sem.s, reads=[("P", "ps0", 0), ("P", "ps0", 1)], writes=[("kb",)])
                    bi2 = (2 * oc + 1) % 4
                    P.dma("gpsimd", mk.S("wB" + str(bi2)), wb[:, bi2 * D:(bi2 + 1) * D], wv1[oc], writes=[("wA", bi2)])
                    vp, vpn = (ps1, "ps1") if oc % 2 == 0 else (ps3, "ps3")
                    for kc in range(NT):
                        for nh in range(2):
                            P.emit("tensor", lambda t, _bi=bi2, _kc=kc, _nh=nh, _ps=vp:
                                   t.matmul(_ps[:, _nh * 512:_nh * 512 + 512],
                                            wb[:, _bi * D + _kc * 128:_bi * D + _kc * 128 + 128],
                                            ST(_kc)[:, _nh * 512:_nh * 512 + 512],
                                            start=_kc == 0, stop=_kc == NT - 1),
                                   inc=sem.t if (kc == NT - 1 and nh == 1) else None,
                                   reads=[("wA", bi2), ("x2", kc)], writes=[("P", vpn, nh)])
                    P.emit("vector", lambda v, _oc=oc, _ps=vp: v.tensor_tensor(kv_sb[:, _oc * T:(_oc + 1) * T], kbuf[:], _ps[:], ALU.mult),
                           inc=sem.v, reads=[("kb",), ("P", vpn, 0), ("P", vpn, 1)], writes=[("kv", oc)])
                    P.emit("vector", lambda v, _oc=oc: v.tensor_reduce(cs_sb[:, _oc:_oc + 1], kv_sb[:, _oc * T:(_oc + 1) * T], AX.X, ALU.add),
                           inc=sem.v, reads=[("kv", oc)], writes=[("cs",)])
                P.dma("gpsimd", mk.S("ccw"), cc_in.ap(), cs_sb[:], reads=[("cs",)], writes=[("ccin",)])
                P.emit("gpsimd", lambda g: g.collective_compute(
                           "AllGather", ALU.bypass, replica_groups=[list(range(ncores))],
                           ins=[cc_in.ap()], outs=[cc_out.ap()]),
                       inc=sem.cc, reads=[("ccin",)], writes=[("ccout",)])
                P.dma("gpsimd", mk.S("agr"), ag_sb[:, 0:16 * ncores],
                      cc_out.ap().rearrange("(r p) oc -> p r oc", p=128), reads=[("ccout",)], writes=[("ag",)])
                for oc in range(16):
                    bi = oc % 4
                    qp, qpn = (ps0, "ps0") if oc % 2 == 0 else (ps1, "ps1")
                    P.dma("gpsimd", mk.S("wA" + str(bi)), wb[:, bi * D:(bi + 1) * D], wq1[oc],
                          writes=[("wA", bi)])
                    for kc in range(NT):
                        for nh in range(2):
                            P.emit("tensor", lambda t, _bi=bi, _kc=kc, _nh=nh, _ps=qp:
                                   t.matmul(_ps[:, _nh * 512:_nh * 512 + 512],
                                            wb[:, _bi * D + _kc * 128:_bi * D + _kc * 128 + 128],
                                            ST(_kc)[:, _nh * 512:_nh * 512 + 512],
                                            start=_kc == 0, stop=_kc == NT - 1),
                                   inc=sem.t if (kc == NT - 1 and nh == 1) else None,
                                   reads=[("wA", bi), ("x2", kc)], writes=[("P", qpn, nh)])
                    P.emit("scalar", lambda s, _oc=oc, _ps=qp: s.copy(q2[:, _oc * T:(_oc + 1) * T], _ps[:]),
                           inc=sem.s, reads=[("P", qpn, 0), ("P", qpn, 1)], writes=[("q2", oc)])
                P.emit("vector", lambda v: v.memset(cy_sb[:], 0.0), writes=[("cy",)])
                for r in range(ncores):
                    P.emit("vector", lambda v, _r=r: v.scalar_tensor_tensor(
                               cy_sb[:], ag_sb[:, _r * 16:(_r + 1) * 16], cm_sb[:, _r:_r + 1], cy_sb[:],
                               ALU.mult, ALU.add),
                           inc=sem.v if r == ncores - 1 else None,
                           reads=[("ag",), ("cm",), ("cy",)], writes=[("cy",)])
                for oc in range(16):
                    P.emit("vector", lambda v, _oc=oc: v.tensor_tensor_scan(
                               kvt_t[:], kv_sb[:, _oc * T:(_oc + 1) * T], kv_sb[:, _oc * T:(_oc + 1) * T],
                               cy_sb[:, _oc:_oc + 1], ALU.add, ALU.bypass),
                           inc=sem.v, reads=[("kv", oc), ("cy",)], writes=[("kvt",)])
                    P.emit("vector", lambda v, _oc=oc: v.scalar_tensor_tensor(
                               q2[:, _oc * T:(_oc + 1) * T], q2[:, _oc * T:(_oc + 1) * T], DENOM_L1, kvt_t[:],
                               ALU.mult, ALU.mult),
                           inc=sem.v, reads=[("q2", oc), ("kvt",)], writes=[("q2", oc)])
            P.join([("kv", oc) for oc in range(16)] + [("kb",), ("kvt",)])
            with ExitStack() as phr:
                Rr = lambda *ar: phr.enter_context(nc.sbuf_tensor(*ar, side="right"))
                r_sb = b.y_acc
                sq = Rr("sqC", [128, 2 * T], F32)
                rows = Rr("rowsC", [1, 5 * T], F32)
                for oc in range(16):
                    bi = oc % 4
                    op_, opn = (ps2, "ps2") if oc % 2 == 0 else (ps3, "ps3")
                    P.dma("gpsimd", mk.S("wA" + str(bi)), wb[:, bi * D:(bi + 1) * D], wo1[oc], writes=[("wA", bi)])
                    for kc in range(NT):
                        for nh in range(2):
                            P.emit("tensor", lambda t, _bi=bi, _kc=kc, _nh=nh, _ps=op_:
                                   t.matmul(_ps[:, _nh * 512:_nh * 512 + 512],
                                            wb[:, _bi * D + _kc * 128:_bi * D + _kc * 128 + 128],
                                            q2[:, _kc * T + _nh * 512:_kc * T + _nh * 512 + 512],
                                            start=_kc == 0, stop=_kc == NT - 1),
                                   inc=sem.t if (kc == NT - 1 and nh == 1) else None,
                                   reads=[("wA", bi), ("q2", kc)], writes=[("P", opn, nh)])
                    P.emit("vector", lambda v, _oc=oc, _ps=op_: v.tensor_tensor(r_sb[:, _oc * T:(_oc + 1) * T],
                               ST(_oc), _ps[:], ALU.add),
                           inc=sem.v, reads=[("x2", oc), ("P", opn, 0), ("P", opn, 1)], writes=[("rC", oc)])
                r_aps = [r_sb[:, kc * T:(kc + 1) * T] for kc in range(NT)]
                kn = dict(sq="sqC", rows="rwC", s1=("P", "ps0"), s2=("P", "ps1"),
                          b1=("P", "ps2"), b2=("P", "ps3"))
                mk.emit_ln(r_aps, [("rC", kc) for kc in range(NT)],
                           [ST(kc) for kc in range(NT)], [("x3", kc) for kc in range(NT)],
                           sq, rows, ps0[0:1, :], ps1[0:1, :], ps2, ps3, kn)
        P.join([("q2", oc) for oc in range(16)] + [("wA", 0), ("wA", 1)] +
               [("rC", kc) for kc in range(NT)] + [("sqC", 0), ("sqC", 1), ("rwC",)])

        gpk1 = mk.din("l1_gpk", [22, 128, 2048], U16)
        dpk1 = mk.din("l1_dpk", [11, 128, 2048], U16)
        gsc1 = mk.din("l1_gsc", [22, 4, 4, 512], FP16)
        dsc1 = mk.din("l1_dsc", [11, 4, 2048], FP16)
        x3d = nc.dram_tensor("x3d", [D, T], BF16, kind="Internal")
        for kc in range(NT):
            P.dma("sync", mk.S("bw"), x3d.ap().rearrange("(kc p) t -> kc p t", p=128)[kc],
                  ST(kc), reads=[("x3", kc)], writes=[("x3d", kc)])
        y = b.y_acc
        with ExitStack() as phD:
            RD = lambda *ar: phD.enter_context(nc.sbuf_tensor(*ar, side="right"))
            xj = RD("xj", [128, 16 * T], BF16)
            x3r = x3d.ap().rearrange("(a two) t -> two a t", two=2)
            for jp in range(4):
                par = 1 if jp in (2, 3) else 0
                hi = 1 if jp in (1, 3) else 0
                for tt in range(4):
                    a0 = hi * 512 + tt * 128
                    P.dma("sync", mk.S("xj%d" % (jp * 4 + tt)), xj[:, (jp * 4 + tt) * T:(jp * 4 + tt + 1) * T],
                          x3r[par, a0:a0 + 128, :],
                          reads=[("x3d", kc) for kc in range(NT)], writes=[("xj", jp, tt)])
            XJ = lambda jp, tt: xj[:, (jp * 4 + tt) * T:(jp * 4 + tt + 1) * T]
            with ExitStack() as ph:
                Rr = lambda *ar: ph.enter_context(nc.sbuf_tensor(*ar, side="right"))
                wdec = Rr("wdecD", [128, 16384], BF16)
                pk0 = Rr("pk0D", [128, 2048], U16)
                pk1 = Rr("pk1D", [128, 2048], U16)
                sc0 = Rr("sc0D", [128, 2048], FP16)
                sc1 = Rr("sc1D", [128, 2048], FP16)
                inter = Rr("interD", [128, 8 * T], BF16)
                tmpa = Rr("tmpaD", [128, 1024], I32)
                tmpb = Rr("tmpbD", [128, 1024], I32)
                gsb = Rr("gsbD", [128, T], BF16)
                y_keys = [("yD", hc) for hc in range(NT)]
                for ci, (cs, nt) in enumerate(L1_CHUNKS):
                    for tt in range(nt):
                        for half, span in ((0, cs + tt), (1, 11 + cs + tt)):
                            pkb, pkk = (pk0, ("pkD", 0)) if half == 0 else (pk1, ("pkD", 1))
                            scb, sck = (sc0, ("scD", 0)) if half == 0 else (sc1, ("scD", 1))
                            P.dma("sync", mk.S("pkD%d" % half), pkb[:], gpk1[span], writes=[pkk])
                            for tin in range(4):
                                P.dma("sync", mk.S("scD%d%d" % (half, tin)), scb[:, tin * 512:(tin + 1) * 512],
                                      gsc1[span, tin].unsqueeze(1).broadcast_to((4, 32, 512)),
                                      writes=[sck + (tin,)])
                            for jin in range(4):
                                for tin in range(4):
                                    ki = jin * 4 + tin
                                    dst = wdec[:, (half * 16 + ki) * 512:(half * 16 + ki + 1) * 512]
                                    mk.emit_decode(pkb[:, tin * 512:(tin + 1) * 512], scb[:, tin * 512:(tin + 1) * 512],
                                                   dst, jin, tmpa, tmpb,
                                                   [pkk], [sck + (tin,)], [("w", half, ki)], ("t",))
                        for jo in range(4):
                            for half in range(2):
                                if jo % 2 == 0:
                                    psx, psk = (ps0, ("P", "ps0")) if half == 0 else (ps1, ("P", "ps1"))
                                else:
                                    psx, psk = (ps2, ("P", "ps2")) if half == 0 else (ps3, ("P", "ps3"))
                                for jin in range(4):
                                    for tin in range(4):
                                        ki = jin * 4 + tin
                                        lhs = wdec[:, (half * 16 + ki) * 512:(half * 16 + ki + 1) * 512].rearrange(
                                            "p (a four) -> p a four", four=4)[:, :, jo]
                                        for nh in range(2):
                                            P.emit("tensor", lambda t, _l=lhs, _jin=jin, _tin=tin, _nh=nh, _ps=psx,
                                                   _f=ki == 0, _la=ki == 15:
                                                   t.matmul(_ps[:, _nh * 512:_nh * 512 + 512], _l,
                                                            XJ(_jin, _tin)[:, _nh * 512:_nh * 512 + 512],
                                                            start=_f, stop=_la),
                                                   inc=sem.t if (ki == 15 and nh == 1) else None,
                                                   reads=[("w", half, ki), ("xj", jin, tin)], writes=[psk + (nh,)])
                            it = inter[:, (tt * 4 + jo) * T:(tt * 4 + jo + 1) * T]
                            pg_, pu_ = (ps0, ps1) if jo % 2 == 0 else (ps2, ps3)
                            pgn_, pun_ = ("ps0", "ps1") if jo % 2 == 0 else ("ps2", "ps3")
                            P.emit("scalar", lambda s, _pg=pg_: s.activation(gsb[:], _pg[:], AF.Silu, scale=float(2 ** 14)),
                                   inc=sem.s, reads=[("P", pgn_, 0), ("P", pgn_, 1)], writes=[("gsb",)])
                            P.emit("vector", lambda v, _it=it, _pu=pu_: v.scalar_tensor_tensor(_it, gsb[:], float(2 ** 28), _pu[:], ALU.mult, ALU.mult),
                                   inc=sem.v, reads=[("gsb",), ("P", pun_, 0), ("P", pun_, 1)],
                                   writes=[("intD", tt * 4 + jo)])
                    for tt in range(nt):
                        pkb, pkk = (pk0, ("pkD", 0)) if tt % 2 == 0 else (pk1, ("pkD", 1))
                        scb, sck = (sc0, ("scD", 0)) if tt % 2 == 0 else (sc1, ("scD", 1))
                        P.dma("sync", mk.S("pkD%d" % (tt % 2)), pkb[:], dpk1[cs + tt], writes=[pkk])
                        P.dma("sync", mk.S("scDd%d" % (tt % 2)), scb[:],
                              dsc1[cs + tt].unsqueeze(1).broadcast_to((4, 32, 2048)),
                              writes=[sck + (0,), sck + (1,), sck + (2,), sck + (3,)])
                        for jd in range(4):
                            pl = tt * 4 + jd
                            dst = wdec[:, pl * 2048:(pl + 1) * 2048]
                            alias = ([("w", 0, 4 * pl + q) for q in range(4)] if pl < 4 else
                                     [("w", 1, 4 * (pl - 4) + q) for q in range(4)])
                            mk.emit_decode(pkb[:], scb[:], dst, jd, tmpa, tmpb,
                                           [pkk], [sck + (q,) for q in range(4)], [("dw", pl)] + alias, ("t",))
                    for hc in range(NT):
                        hoff = 256 * (hc % 8)
                        hpar = 1 if hc >= 8 else 0
                        np_ = nt * 4
                        pd_, pdn_ = (ps2, "ps2") if hc % 2 == 0 else (ps3, "ps3")
                        for pl in range(np_):
                            lhs = wdec[:, pl * 2048 + hoff:pl * 2048 + hoff + 256].rearrange(
                                "p (a two) -> p a two", two=2)[:, :, hpar]
                            alias_r = ([("w", 0, 4 * pl + q) for q in range(4)] if pl < 4 else
                                       [("w", 1, 4 * (pl - 4) + q) for q in range(4)])
                            for nh in range(2):
                                P.emit("tensor", lambda t, _l=lhs, _pl=pl, _nh=nh, _ps=pd_, _f=pl == 0, _la=pl == np_ - 1:
                                       t.matmul(_ps[:, _nh * 512:_nh * 512 + 512], _l,
                                                inter[:, _pl * T + _nh * 512:_pl * T + _nh * 512 + 512],
                                                start=_f, stop=_la),
                                       inc=sem.t if (pl == np_ - 1 and nh == 1) else None,
                                       reads=[("dw", pl), ("intD", pl)] + alias_r, writes=[("P", pdn_, nh)])
                        if ci == 0:
                            P.emit("scalar", lambda s, _hc=hc, _ps=pd_: s.copy(y[:, _hc * T:(_hc + 1) * T], _ps[:]),
                                   inc=sem.s, reads=[("P", pdn_, 0), ("P", pdn_, 1)], writes=[("yD", hc)])
                        else:
                            P.emit("vector", lambda v, _hc=hc, _ps=pd_: v.tensor_tensor(y[:, _hc * T:(_hc + 1) * T],
                                   y[:, _hc * T:(_hc + 1) * T], _ps[:], ALU.add),
                                   inc=sem.v, reads=[("yD", hc), ("P", pdn_, 0), ("P", pdn_, 1)], writes=[("yD", hc)])
            P.join([("w", 0, k) for k in range(16)] + [("w", 1, k) for k in range(16)] +
                   [("dw", pl) for pl in range(8)] + [("intD", q) for q in range(8)] +
                   [("pkD", 0), ("pkD", 1)] +
                   [("scD", hq, q) for hq in range(2) for q in range(4)] +
                   [("gsb",), ("t", "a"), ("t", "b")] +
                   [("xj", jp, tt) for jp in range(4) for tt in range(4)])
            with ExitStack() as ph2:
                Rr = lambda *ar: ph2.enter_context(nc.sbuf_tensor(*ar, side="right"))
                sq = Rr("sqD", [128, 2 * T], F32)
                rows = Rr("rowsD", [1, 5 * T], F32)
                RW = [("rwD",)]
                for hc in range(NT):
                    buf = sq[:, (hc % 2) * T:(hc % 2) * T + T]
                    P.emit("scalar", lambda s, _b=buf, _hc=hc: s.activation(_b, y[:, _hc * T:(_hc + 1) * T], AF.Square),
                           inc=sem.s, reads=[("yD", hc)], writes=[("sqD", hc % 2)])
                    for nh in range(2):
                        P.emit("tensor", lambda t, _b=buf, _nh=nh, _f=hc == 0, _la=hc == NT - 1:
                               t.matmul(ps0[0:1, _nh * 512:_nh * 512 + 512], mk.ones_col,
                                        _b[:, _nh * 512:_nh * 512 + 512], start=_f, stop=_la),
                               inc=sem.t if (hc == NT - 1 and nh == 1) else None,
                               reads=[("sqD", hc % 2), ("cst",)], writes=[("P", "ps0", nh)])
                msq = rows[0:1, 0:T]
                P.emit("vector", lambda v: v.tensor_scalar(msq, ps0[0:1, :], 1.0 / D, None, ALU.mult),
                       inc=sem.v, reads=[("P", "ps0", 0), ("P", "ps0", 1)], writes=RW)
                P.emit("scalar", lambda s: s.activation(msq, msq, AF.Ln, bias=mk.eps_rms),
                       reads=RW + [("cst",)], writes=RW)
                P.emit("scalar", lambda s: s.activation(msq, msq, AF.Exp, scale=-0.5), inc=sem.s,
                       reads=RW, writes=RW)
                for nh in range(2):
                    P.emit("tensor", lambda t, _nh=nh: t.matmul(ps1[:, _nh * 512:_nh * 512 + 512],
                           mk.orow[0:1, 0:128], msq[0:1, _nh * 512:_nh * 512 + 512], start=True, stop=True),
                           inc=sem.t if nh == 1 else None,
                           reads=RW + [("orow",)], writes=[("P", "ps1", nh)])
                for kc in range(NT):
                    P.emit("vector", lambda v, _kc=kc: v.tensor_tensor(y[:, _kc * T:(_kc + 1) * T],
                           y[:, _kc * T:(_kc + 1) * T], ps1[:], ALU.mult),
                           reads=[("yD", kc), ("P", "ps1", 0), ("P", "ps1", 1)], writes=[("yD", kc)])
                    P.emit("vector", lambda v, _kc=kc: v.tensor_tensor(y[:, _kc * T:(_kc + 1) * T],
                           y[:, _kc * T:(_kc + 1) * T], ST(_kc), ALU.add),
                           inc=sem.v, reads=[("yD", kc), ("x3", kc)], writes=[("yD", kc)])
                r_aps = [y[:, kc * T:(kc + 1) * T] for kc in range(NT)]
                kn = dict(sq="sqD2", rows="rwD2", s1=("P", "ps0"), s2=("P", "ps2"),
                          b1=("P", "ps1"), b2=("P", "ps3"))
                mk.emit_ln(r_aps, [("yD", kc) for kc in range(NT)],
                           [ST(kc) for kc in range(NT)], [("x4", kc) for kc in range(NT)],
                           sq, rows, ps0[0:1, :], ps2[0:1, :], ps1, ps3, kn)
                tok_out = None
                for kc in range(NT):
                    tok_out = mk.P.dma("gpsimd", mk.S("outF"),
                                       out_ap.rearrange("(kc p) t -> kc p t", p=128)[kc],
                                       ST(kc), reads=[("x4", kc)])
                mk.P.raw("gpsimd", lambda e, _t=tok_out: e.wait_ge(_t.sem.h, _t.tgt))
        P.finalize(es.enter_context(nc.Block()), es)
    return nc


from concourse.bass_utils import run_bass_kernel_spmd


def kernel(**inputs):
    g, percore = host_prep(inputs)
    nc = bass.Bass("TRN2", target_bir_lowering=False)
    build_fused(nc)
    in_maps = []
    for c in range(NCORES):
        m = dict(percore[c])
        m.update(consts=g["consts"], onesrow=g["onesrow"],
                 l0_wq=g["l0_wq"], l0_wk_nat=g["l0_wk_nat"], l0_wv_nat=g["l0_wv_nat"],
                 l0_mem=g["l0_mem"], l0_memn=g["l0_memn"], l0_memw=g["l0_memw"],
                 l0_gpk=g["l0_gpk"], l0_upk=g["l0_upk"], l0_dpk=g["l0_dpk"],
                 l0_gsc=g["l0_gsc"], l0_usc=g["l0_usc"], l0_dsc=g["l0_dsc"],
                 l1_wq=g["l1_wq"], l1_wk=g["l1_wk"], l1_wv=g["l1_wv"], l1_wo=g["l1_wo"],
                 l1_gpk=g["l1_gpk"], l1_dpk=g["l1_dpk"], l1_gsc=g["l1_gsc"], l1_dsc=g["l1_dsc"])
        in_maps.append(m)
    res = run_bass_kernel_spmd(nc, in_maps, core_ids=list(range(NCORES)))
    return host_finish([np.asarray(res.results[c]["x4_out"], np.float32) for c in range(NCORES)])


# version 08 (final)
# speedup vs baseline: 1.1215x; 1.1215x over previous
import numpy as np
from contextlib import ExitStack

import concourse.bass as bass
import concourse.mybir as mybir

F32 = mybir.dt.float32
BF16 = mybir.dt.bfloat16
I32 = mybir.dt.int32
U16 = mybir.dt.uint16
FP16 = mybir.dt.float16
AF = mybir.ActivationFunctionType
ALU = mybir.AluOpType
AX = mybir.AxisListType

D = 2048
NT = 16
T = 1024
S = 4096
NCORES = 8

PI = np.concatenate([np.arange(0, D, 2), np.arange(1, D, 2)])



def prep_w(A):
    A = np.ascontiguousarray(A)
    K, O = A.shape
    KT = K // 128
    out = np.empty((O // 128, 128, K), A.dtype)
    for oc in range(O // 128):
        out[oc] = A[:, oc * 128:(oc + 1) * 128].reshape(KT, 128, 128).transpose(1, 0, 2).reshape(128, K)
    return out


def pack_u16(arr_i32, dense):
    v = np.ascontiguousarray(arr_i32).view(np.uint16)
    if not dense:
        v = np.ascontiguousarray(v[:, 0::2])
    return v


def prep_packed_pm(u16_T, rows_per_chunk):
    Wr, O = u16_T.shape
    n = Wr // rows_per_chunk
    sub = rows_per_chunk // 128
    out = np.empty((n, 128, sub * O), np.uint16)
    for c in range(n):
        blk = u16_T[c * rows_per_chunk:(c + 1) * rows_per_chunk]
        out[c] = blk.reshape(sub, 128, O).transpose(1, 0, 2).reshape(128, sub * O)
    return out


def host_prep(inp):
    g = {}
    x = np.asarray(inp["L0_attn_x"], np.float32)
    xT = np.ascontiguousarray(x.reshape(2 * S, D).T)
    xT_pi = np.ascontiguousarray(xT[PI])

    def pm_tiles(A):
        K, O = A.shape
        return np.ascontiguousarray(A.reshape(K // 128, 128, O).transpose(1, 0, 2).reshape(128, (K // 128) * O))

    g["l0_wq"] = prep_w(np.asarray(inp["L0_attn_Wq"], np.float32).T[PI])
    g["l0_wk_nat"] = pm_tiles(np.asarray(inp["L0_attn_Wk"], np.float32).T[PI])
    g["l0_wv_nat"] = pm_tiles(np.asarray(inp["L0_attn_Wv"], np.float32).T[PI])
    g["l0_mem"] = np.ascontiguousarray(np.asarray(inp["L0_attn_memory"], np.float32))
    g["l0_memn"] = np.ascontiguousarray(np.asarray(inp["L0_attn_memory_norm"], np.float32).reshape(4, 128).T)
    g["l0_memw"] = np.asarray(inp["L0_attn_memory_weight"], np.float32).reshape(1, 1)

    gte = pack_u16(np.asarray(inp["L0_ffn_gate_W_packed"]), dense=False).T
    upe = pack_u16(np.asarray(inp["L0_ffn_up_W_packed"]), dense=False).T
    dne = pack_u16(np.asarray(inp["L0_ffn_down_W_packed"]), dense=False).T
    def tile_gu(A):
        out = np.empty((16, 128, 8192), np.uint16)
        for ncb in range(16):
            blk = A[:, ncb * 1024:(ncb + 1) * 1024]
            out[ncb] = blk.reshape(8, 128, 1024).transpose(1, 0, 2).reshape(128, 8192)
        return out

    def tile_dn(A):
        out = np.empty((16, 128, 8192), np.uint16)
        for ncb in range(16):
            blk = A[ncb * 512:(ncb + 1) * 512]
            out[ncb] = blk.reshape(4, 128, 2048).transpose(1, 0, 2).reshape(128, 8192)
        return out

    g["l0_gpk"] = tile_gu(gte)
    g["l0_upk"] = tile_gu(upe)
    g["l0_dpk"] = tile_dn(dne)
    gscT = np.asarray(inp["L0_ffn_gate_scales"], np.float32).T
    uscT = np.asarray(inp["L0_ffn_up_scales"], np.float32).T
    dscT = np.asarray(inp["L0_ffn_down_scales"], np.float32).T

    def tile_sc_gu(A):
        out = np.empty((16, 2, 8, 4096), np.float32)
        for ncb in range(16):
            for h in range(2):
                for r in range(8):
                    for wc in range(8):
                        out[ncb, h, r, wc * 512:(wc + 1) * 512] = A[8 * wc + r, ncb * 1024 + h * 512:ncb * 1024 + h * 512 + 512]
        return np.ascontiguousarray(out)

    def tile_sc_d(A):
        out = np.empty((16, 2, 8, 4096), np.float32)
        for ncb in range(16):
            for dh in range(2):
                for r in range(8):
                    for ws in range(2):
                        out[ncb, dh, r, ws * 2048:(ws + 1) * 2048] = A[32 * ncb + 16 * dh + 8 * ws + r]
        return np.ascontiguousarray(out)

    g["l0_gsc"] = tile_sc_gu(gscT)
    g["l0_usc"] = tile_sc_gu(uscT)
    g["l0_dsc"] = tile_sc_d(dscT)

    g["l1_wq"] = prep_w(np.asarray(inp["L1_attn_Wq"], np.float32).T[PI])
    g["l1_wk"] = prep_w(np.asarray(inp["L1_attn_Wk"], np.float32).T[PI])
    g["l1_wv"] = prep_w(np.asarray(inp["L1_attn_Wv"], np.float32).T[PI])
    g["l1_wo"] = prep_w(np.ascontiguousarray(np.asarray(inp["L1_attn_Wo"], np.float32)[PI].T))

    gupT = pack_u16(np.asarray(inp["L1_ffn_gate_up_packed"]), dense=True).T
    dnT = pack_u16(np.asarray(inp["L1_ffn_down_packed"]), dense=True).T
    gpk1 = np.empty((22, 128, 2048), np.uint16)
    for s_ in range(22):
        gpk1[s_] = gupT[:, 512 * s_:512 * (s_ + 1)].reshape(4, 128, 512).transpose(1, 0, 2).reshape(128, 2048)
    g["l1_gpk"] = gpk1
    g["l1_dpk"] = np.ascontiguousarray(dnT.reshape(11, 128, 2048))
    gsc1 = np.asarray(inp["L1_ffn_gate_up_scales"])
    gsc_t = np.empty((22, 4, 4, 512), np.float16)
    for s_ in range(22):
        for tin in range(4):
            for r in range(4):
                gsc_t[s_, tin, r] = gsc1[4 * tin + r, 512 * s_:512 * (s_ + 1)]
    g["l1_gsc"] = gsc_t
    g["l1_dsc"] = np.ascontiguousarray(np.asarray(inp["L1_ffn_down_scales"]).reshape(11, 4, 2048))

    consts = np.zeros((128, 8), np.float32)
    consts[:, 0] = 1.0
    consts[:, 1] = -20.0
    consts[:, 2] = 1e-5
    consts[:, 3] = 1e-6
    g["consts"] = consts
    g["onesrow"] = np.ones((1, 128), np.float32)

    percore = []
    for c in range(NCORES):
        b, j = c // 4, c % 4
        pc = dict(
            xT_my=np.ascontiguousarray(xT_pi[:, c * T:(c + 1) * T]),
            xT_b=np.ascontiguousarray(xT_pi[:, b * S:(b + 1) * S]),
        )
        cm = np.zeros((128, 8), np.float32)
        for r in range(b * 4, b * 4 + j):
            cm[:, r] = 1.0
        pc["cmask"] = cm
        percore.append(pc)
    return g, percore


def host_finish(per_core_outs):
    full = np.concatenate(per_core_outs, axis=1)
    nat = np.empty_like(full)
    nat[PI] = full
    return np.ascontiguousarray(nat.T).reshape(2, S, D)



ENGINES = ("sync", "gpsimd", "vector", "scalar", "tensor")


class Sem:
    def __init__(self, handle, name):
        self.h = handle
        self.name = name
        self.n = 0

    def bump(self, k):
        self.n += k
        return self.n


class Tok:
    __slots__ = ("sem", "tgt", "_op")

    def __init__(self, sem, tgt):
        self.sem = sem
        self.tgt = tgt
        self._op = None


class OpRec:
    __slots__ = ("eng", "fn", "exp_sem", "exp_by", "auto", "auto_val", "prewaits", "idx", "_tok", "raw_fn")

    def __init__(self, eng, fn, exp_sem, exp_by, idx):
        self.eng = eng
        self.fn = fn
        self.exp_sem = exp_sem
        self.exp_by = exp_by
        self.auto = False
        self.auto_val = None
        self.prewaits = []
        self.idx = idx
        self._tok = None
        self.raw_fn = False


class Prog:

    def __init__(self, nc):
        self.nc = nc
        self.ops = {e: [] for e in ENGINES}
        self._waited = {e: {} for e in ENGINES}
        self._auto_hw = {e: {} for e in ENGINES}
        self._store = {}
        self._auto_sems = None

    def _edge(self, producer, consumer):
        if producer is consumer:
            return
        if isinstance(consumer, Tok):
            consumer = consumer._op
        if isinstance(producer, Tok):
            if getattr(producer, "_op", None) is consumer:
                return
            consumer.prewaits.append(("tok", producer))
            return
        if producer.eng == "tensor" and consumer.eng == "tensor":
            return
        producer.auto = True
        consumer.prewaits.append(("auto", producer))

    def _note(self, op_or_tok, reads, writes):
        for k in reads:
            d = self._store.get(k)
            if d is not None and d[0] is not None:
                self._edge(d[0], op_or_tok)
            if d is None:
                d = self._store[k] = [None, []]
            d[1].append(op_or_tok)
        for k in writes:
            d = self._store.get(k)
            if d is not None:
                if d[0] is not None:
                    self._edge(d[0], op_or_tok)
                for r in d[1]:
                    if r is not op_or_tok:
                        self._edge(r, op_or_tok)
            self._store[k] = [op_or_tok, []]

    def emit(self, eng, fn, inc=None, inc_by=1, reads=(), writes=()):
        op = OpRec(eng, fn, inc, inc_by, len(self.ops[eng]))
        tok = None
        if inc is not None:
            tgt = inc.bump(inc_by)
            tok = Tok(inc, tgt)
            tok._op = op
        op._tok = tok
        if reads or writes:
            self._note(tok if tok is not None else op, reads, writes)
        self.ops[eng].append(op)
        return tok if inc is not None else op

    def dma(self, eng, sem, out, in_, reads=(), writes=()):
        tgt = sem.bump(16)
        tok = Tok(sem, tgt)
        op = OpRec(eng, lambda e, _o=out, _i=in_: e.dma_start(out=_o, in_=_i), sem, 16, len(self.ops[eng]))
        op._tok = tok
        tok._op = op
        if reads or writes:
            for k in reads:
                d = self._store.get(k)
                if d is not None and d[0] is not None:
                    self._edge(d[0], op)
                if d is None:
                    d = self._store[k] = [None, []]
                d[1].append(tok)
            for k in writes:
                d = self._store.get(k)
                if d is not None:
                    if d[0] is not None:
                        self._edge(d[0], op)
                    for r in d[1]:
                        self._edge(r, op)
                self._store[k] = [tok, []]
        self.ops[eng].append(op)
        return tok

    def join(self, keys, engines=ENGINES):
        for eng in engines:
            marker = OpRec(eng, None, None, 0, len(self.ops[eng]))
            for k in keys:
                d = self._store.get(k)
                if d is None:
                    continue
                if d[0] is not None:
                    self._edge(d[0], marker)
                for r in d[1]:
                    self._edge(r, marker)
            if marker.prewaits:
                self.ops[eng].append(marker)

    def wait(self, eng, tok):
        if tok is None:
            return
        if isinstance(tok, OpRec):
            tok.auto = True
            marker = OpRec(eng, None, None, 0, len(self.ops[eng]))
            marker.prewaits.append(("auto", tok))
            self.ops[eng].append(marker)
            return
        w = self._waited[eng]
        if w.get(tok.sem.name, -1) >= tok.tgt:
            return
        w[tok.sem.name] = tok.tgt
        marker = OpRec(eng, None, None, 0, len(self.ops[eng]))
        marker.prewaits.append(("tok", tok))
        self.ops[eng].append(marker)

    def raw(self, eng, fn):
        op = OpRec(eng, fn, None, 0, len(self.ops[eng]))
        op.raw_fn = True
        self.ops[eng].append(op)

    def finalize(self, block, es):
        auto_sems = {}
        for e in ENGINES:
            if any(getattr(o, "auto", False) for o in self.ops[e]):
                auto_sems[e] = Sem(es.enter_context(self.nc.semaphore("au_" + e)), "au_" + e)
        for e, lst in self.ops.items():
            v = 0
            for o in lst:
                if getattr(o, "auto", False):
                    v += 1
                    o.auto_val = v
        waited_tok = {e: {} for e in ENGINES}
        waited_auto = {e: {} for e in ENGINES}

        for e in ENGINES:
            lst = self.ops[e]
            if not lst:
                continue

            def body(eng_obj, _lst=tuple(lst), _e=e):
                for o in _lst:
                    tok_max = {}
                    auto_max = {}
                    for kind, p in o.prewaits:
                        if kind == "tok":
                            if tok_max.get(p.sem.name, (None, -1))[1] < p.tgt:
                                tok_max[p.sem.name] = (p.sem.h, p.tgt)
                        else:
                            if auto_max.get(p.eng, (None, -1))[1] < p.auto_val:
                                auto_max[p.eng] = (auto_sems[p.eng].h, p.auto_val)
                    for name, (h, tgt) in tok_max.items():
                        wt = waited_tok[_e]
                        if wt.get(name, -1) < tgt:
                            wt[name] = tgt
                            eng_obj.wait_ge(h, tgt)
                    for peng, (h, tgt) in auto_max.items():
                        wa = waited_auto[_e]
                        if wa.get(peng, -1) < tgt:
                            wa[peng] = tgt
                            eng_obj.wait_ge(h, tgt)
                    if o.fn is None:
                        continue
                    inst = o.fn(eng_obj)
                    if getattr(o, "raw_fn", False):
                        continue
                    if o.exp_sem is not None:
                        inst.then_inc(o.exp_sem.h, o.exp_by)
                    if getattr(o, "auto", False):
                        inst.then_inc(auto_sems[o.eng].h, 1)
            getattr(block, e)(body)



class NS(dict):
    __getattr__ = dict.__getitem__
    __setattr__ = dict.__setitem__


class MK:
    def __init__(self, nc, es, P):
        self.nc = nc
        self.es = es
        self.P = P
        self.sem = NS()
        for name in ("d", "g", "v", "s", "t", "cc"):
            self.sem[name] = Sem(es.enter_context(nc.semaphore("sem_" + name)), name)
        self._dsems = {}
        self.t = NS()
        self.ps = NS()
        self.ap = NS()
        self._ln_ps_free = None
        self._ln_bc_free = None

    def S(self, name):
        if name not in self._dsems:
            self._dsems[name] = Sem(self.es.enter_context(self.nc.semaphore("ds_" + name)), "ds_" + name)
        return self._dsems[name]

    def sbuf(self, name, shape, dtype):
        h = self.es.enter_context(self.nc.sbuf_tensor(name, shape, dtype))
        self.t[name] = h
        return h

    def psum(self, name, shape, dtype=F32):
        h = self.es.enter_context(self.nc.psum_tensor(name, shape, dtype))
        self.ps[name] = h
        return h

    def din(self, name, shape, dtype):
        ap = self.nc.dram_tensor(name, shape, dtype, kind="ExternalInput").ap()
        self.ap[name] = ap
        return ap

    def dout(self, name, shape, dtype):
        ap = self.nc.dram_tensor(name, shape, dtype, kind="ExternalOutput").ap()
        self.ap[name] = ap
        return ap

    def setup_consts(self):
        P, sem = self.P, self.sem
        self.din("consts", [128, 8], F32)
        self.din("onesrow", [1, 128], F32)
        cst = self.sbuf("cst", [128, 8], F32)
        orow = self.sbuf("orow", [1, 128], F32)
        orow_b = self.sbuf("orow_b", [1, 128], BF16)
        P.dma("sync", self.S("c0"), cst[:], self.ap.consts, writes=[("cst",)])
        P.dma("sync", self.S("c1"), orow[:], self.ap.onesrow, writes=[("orow",)])
        P.emit("vector", lambda v: v.tensor_copy(orow_b[:], orow[:]), inc=sem.v,
               reads=[("orow",)], writes=[("orowb",)])
        self.ones_col = cst[:, 0:1]
        self.bias_exp = cst[:, 1:2]
        self.eps_ln = cst[0:1, 2:3]
        self.eps_rms = cst[0:1, 3:4]
        self.orow = orow
        self.orow_b = orow_b

    def emit_ln(self, r_aps, r_keys, out_aps, out_keys, sq, rows, ps_s1, ps_s2, ps_b1, ps_b2,
                kn):
        P, sem = self.P, self.sem
        nT = len(r_aps)
        K = lambda n, *i: (kn[n] + i) if isinstance(kn[n], tuple) else ((kn[n],) + i)
        for i, r in enumerate(r_aps):
            buf = sq[:, (i % 2) * T:(i % 2) * T + T]
            P.emit("scalar", lambda s, _b=buf, _r=r: s.activation(_b, _r, AF.Square), inc=sem.s,
                   reads=[r_keys[i], ("cst",)], writes=[K("sq", i % 2)])
            for nh in range(T // 512):
                st, sp = (i == 0), (i == nT - 1)
                P.emit("tensor", lambda t, _r=r, _nh=nh, _st=st, _sp=sp:
                       t.matmul(ps_s1[0:1, _nh * 512:_nh * 512 + 512], self.ones_col,
                                _r[:, _nh * 512:_nh * 512 + 512], start=_st, stop=_sp),
                       inc=sem.t if (sp and nh == 1) else None,
                       reads=[r_keys[i], ("cst",)], writes=[K("s1", nh)])
                P.emit("tensor", lambda t, _b=buf, _nh=nh, _st=st, _sp=sp:
                       t.matmul(ps_s2[0:1, _nh * 512:_nh * 512 + 512], self.ones_col,
                                _b[:, _nh * 512:_nh * 512 + 512], start=_st, stop=_sp),
                       inc=sem.t if (sp and nh == 1) else None,
                       reads=[K("sq", i % 2), ("cst",)], writes=[K("s2", nh)])
        m = rows[0:1, 0:T]; msq = rows[0:1, T:2 * T]; var = rows[0:1, 2 * T:3 * T]
        rstd = rows[0:1, 3 * T:4 * T]; mr = rows[0:1, 4 * T:5 * T]
        RW = [K("rows")]
        P.emit("vector", lambda v: v.tensor_scalar(m, ps_s1[0:1, :], 1.0 / D, None, ALU.mult),
               reads=[K("s1", 0), K("s1", 1)], writes=RW)
        P.emit("vector", lambda v: v.tensor_scalar(msq, ps_s2[0:1, :], 1.0 / D, None, ALU.mult),
               reads=[K("s2", 0), K("s2", 1)], writes=RW)
        P.emit("vector", lambda v: v.scalar_tensor_tensor(var, m, -1.0, m, ALU.mult, ALU.mult),
               reads=RW, writes=RW)
        P.emit("vector", lambda v: v.tensor_tensor(var, msq, var, ALU.add), inc=sem.v,
               reads=RW, writes=RW)
        P.emit("scalar", lambda s: s.activation(var, var, AF.Ln, bias=self.eps_ln),
               reads=RW + [("cst",)], writes=RW)
        P.emit("scalar", lambda s: s.activation(rstd, var, AF.Exp, scale=-0.5), inc=sem.s,
               reads=RW, writes=RW)
        for nh in range(T // 512):
            P.emit("tensor", lambda t, _nh=nh: t.matmul(ps_b1[:, _nh * 512:_nh * 512 + 512],
                   self.orow[0:1, 0:128], rstd[0:1, _nh * 512:_nh * 512 + 512], start=True, stop=True),
                   reads=RW + [("orow",)], writes=[K("b1", nh)])
            P.emit("tensor", lambda t, _nh=nh: t.matmul(ps_b2[:, _nh * 512:_nh * 512 + 512],
                   self.orow[0:1, 0:128], m[0:1, _nh * 512:_nh * 512 + 512], start=True, stop=True),
                   inc=sem.t if nh == 1 else None,
                   reads=RW + [("orow",)], writes=[K("b2", nh)])
        out_toks = []
        for i, (r, o) in enumerate(zip(r_aps, out_aps)):
            buf = sq[:, (i % 2) * T:(i % 2) * T + T]
            P.emit("vector", lambda v, _r=r, _b=buf: v.tensor_tensor(_b, _r, ps_b2[:], ALU.subtract),
                   reads=[r_keys[i], K("b2", 0), K("b2", 1)], writes=[K("sq", i % 2)])
            out_toks.append(P.emit("vector", lambda v, _b=buf, _o=o: v.tensor_tensor(_o, _b, ps_b1[:], ALU.mult),
                                   inc=sem.v,
                                   reads=[K("sq", i % 2), K("b1", 0), K("b1", 1)], writes=[out_keys[i]]))
        return out_toks

    def emit_l0_attn(self, a, upto='full'):
        P, sem = self.P, self.sem
        xT_my = self.din("xT_my", [D, T], F32)
        xT_b = self.din("xT_b", [D, S], F32)
        wq = self.din("l0_wq", [16, 128, D], F32)
        wk = self.din("l0_wk_nat", [128, 16 * 512], F32)
        wv = self.din("l0_wv_nat", [128, 16 * 512], F32)
        mem = self.din("l0_mem", [512, 512], F32)
        memn = self.din("l0_memn", [128, 4], F32)
        memw = self.din("l0_memw", [1, 1], F32)
        self.ap_xT_my = xT_my

        q_sb, k_sb, v_tm, attn = a.q_sb, a.k_sb, a.v_tm, a.attn
        wkv_sb, xsl, wqb = a.wkv_sb, a.xsl, a.wqb
        mw_sb = a.mw_sb
        mem_sb, memn_sb = a.mem_sb, a.memn_sb
        ps_sc, ps_a, ps_bc = a.ps_sc, a.ps_a, a.ps_bc

        P.dma("gpsimd", self.S("wk"), wkv_sb[:, 0:8192], wk, writes=[("wk",)])
        P.dma("gpsimd", self.S("wv"), wkv_sb[:, 8192:], wv, writes=[("wv",)])
        P.dma("gpsimd", self.S("mem"), mem_sb[:], mem.rearrange("(ic p) j -> p ic j", p=128), writes=[("mem",)])
        P.dma("gpsimd", self.S("memn"), memn_sb[:], memn, writes=[("memn",)])
        P.dma("sync", self.S("mw"), mw_sb[0:1, 0:1], memw, writes=[("mw",)])

        for tkc in range(8):
            bi = tkc % 2
            sl = xsl[:, bi * 8192:(bi + 1) * 8192]
            P.dma("gpsimd", self.S("x%d" % bi), sl,
                  xT_b.rearrange("(kc p) s -> p kc s", p=128)[:, :, tkc * 512:(tkc + 1) * 512],
                  writes=[("xsl", bi)])
            for oc in range(4):
                kp, kpn = (a.ps_sc[0], "ps0") if oc % 2 == 0 else (a.ps_a, "ps2")
                for kc in range(NT):
                    P.emit("tensor", lambda t, _oc=oc, _kc=kc, _sl=sl, _ps=kp:
                           t.matmul(_ps[:, 0:512],
                                    wkv_sb[:, _kc * 512 + _oc * 128:_kc * 512 + _oc * 128 + 128],
                                    _sl[:, _kc * 512:_kc * 512 + 512],
                                    start=_kc == 0, stop=_kc == NT - 1),
                           inc=sem.t if kc == NT - 1 else None,
                           reads=[("wk",), ("xsl", bi)], writes=[("P", kpn, 0)])
                P.emit("scalar", lambda s, _oc=oc, _tkc=tkc, _ps=kp:
                       s.copy(k_sb[:, _oc * S + _tkc * 512:_oc * S + _tkc * 512 + 512], _ps[:, 0:512]),
                       inc=sem.s, reads=[("P", kpn, 0)], writes=[("k",)])
            for tsub in range(4):
                vt = tkc * 4 + tsub
                vp, vpn = (a.ps_sc[1], "ps1") if tsub % 2 == 0 else (a.ps_bc, "ps3")
                for kc in range(NT):
                    P.emit("tensor", lambda t, _kc=kc, _tsub=tsub, _sl=sl, _ps=vp:
                           t.matmul(_ps[:, 0:512],
                                    _sl[:, _kc * 512 + _tsub * 128:_kc * 512 + _tsub * 128 + 128],
                                    wkv_sb[:, 8192 + _kc * 512:8192 + (_kc + 1) * 512],
                                    start=_kc == 0, stop=_kc == NT - 1),
                           inc=sem.t if kc == NT - 1 else None,
                           reads=[("wv",), ("xsl", bi)], writes=[("P", vpn, 0)])
                src = vp[:, 0:512].rearrange("p (h dd par) -> p h par dd", h=8, dd=32)
                dst = v_tm[:, vt * 520:(vt + 1) * 520].rearrange("p (h x) -> p h x", h=8)[:, :, 0:64].rearrange(
                    "p h (par dd) -> p h par dd", par=2)
                P.emit("vector", lambda v, _s=src, _d=dst: v.tensor_copy(_d, _s), inc=sem.v,
                       reads=[("P", vpn, 0)], writes=[("v",)])
                ones_ap = v_tm[:, vt * 520:(vt + 1) * 520].rearrange("p (h x) -> p h x", h=8)[:, :, 64:65]
                P.emit("vector", lambda v, _a=ones_ap: v.memset(_a, 1.0), writes=[("v",)])

        if upto in ('a1', 'a1v'):
            self._phase_tail_toks = [Tok(sem.s, sem.s.n), Tok(sem.v, sem.v.n), Tok(sem.t, sem.t.n)]
            return a
        tok_xmy = P.dma("gpsimd", self.S("x0"), xsl[:, 0:16 * T].rearrange("p (kc t) -> p kc t", kc=16),
                        xT_my.rearrange("(kc p) t -> p kc t", p=128),
                        writes=[("xsl", 0), ("xsl", 1)])
        a.tok_xmy = tok_xmy
        q_evac = []
        for oc in range(16):
            bi = oc % 2
            psn = "ps0" if bi == 0 else "ps1"
            psx = ps_sc[bi]
            P.dma("gpsimd", self.S("w%d" % bi), wqb[:, bi * D:(bi + 1) * D], wq[oc], writes=[("wqb", bi)])
            for kc in range(NT):
                for nh in range(2):
                    P.emit("tensor", lambda t, _bi=bi, _kc=kc, _nh=nh, _ps=psx:
                           t.matmul(_ps[:, _nh * 512:_nh * 512 + 512],
                                    wqb[:, _bi * D + _kc * 128:_bi * D + _kc * 128 + 128],
                                    xsl[:, _kc * T + _nh * 512:_kc * T + _nh * 512 + 512],
                                    start=_kc == 0, stop=_kc == NT - 1),
                           inc=sem.t if (kc == NT - 1 and nh == 1) else None,
                           reads=[("wqb", bi), ("xsl", 0), ("xsl", 1)], writes=[("P", psn, nh)])
            tok_ev = P.emit("scalar", lambda s, _oc=oc, _ps=psx: s.copy(q_sb[:, _oc * T:(_oc + 1) * T], _ps[:]),
                            inc=sem.s, reads=[("P", psn, 0), ("P", psn, 1)], writes=[("q",)])
            q_evac.append(tok_ev)

        MW = [("mw",)]
        P.emit("scalar", lambda s: s.activation(mw_sb[0:1, 1:2], mw_sb[0:1, 0:1], AF.Exp, scale=-1.0),
               inc=sem.s, reads=MW, writes=MW)
        P.emit("vector", lambda v: v.tensor_scalar(mw_sb[0:1, 2:3], mw_sb[0:1, 1:2], 1.0, None, ALU.add),
               reads=MW, writes=MW)
        P.emit("vector", lambda v: v.reciprocal(mw_sb[0:1, 2:3], mw_sb[0:1, 2:3]), reads=MW, writes=MW)
        tok_w = P.emit("vector", lambda v: v.tensor_tensor(mw_sb[0:1, 3:4], mw_sb[0:1, 1:2], mw_sb[0:1, 2:3], ALU.mult),
                       inc=sem.v, reads=MW, writes=MW)
        a.tok_w = tok_w
        a.q_evac = q_evac
        return a

    def emit_l0_scores(self, a, nqh=32, sub='full'):
        P, sem = self.P, self.sem
        q_sb, k_sb, v_tm, attn = a.q_sb, a.k_sb, a.v_tm, a.attn
        pbuf, tmp64, rowsA, rbf, mw_sb = a.pbuf, a.tmp64, a.rowsA, a.rbf, a.mw_sb
        ps_sc, ps_a, ps_bc = a.ps_sc, a.ps_a, a.ps_bc

        attn_toks = [None] * 16
        items = [(qh, tkc) for qh in range(nqh) for tkc in range(32)]
        PSN = ["ps0", "ps1"]

        def emit_scores(idx):
            qh, tkc = items[idx]
            h = qh % 8
            koc, koff = h // 2, (h % 2) * 64
            qoc, qoff = qh // 2, (qh % 2) * 64
            bi = idx % 2
            pi3 = idx % 3
            psn = PSN[bi]
            for nh in range(2):
                P.emit("tensor", lambda t, _bi=bi, _koc=koc, _koff=koff, _qoc=qoc, _qoff=qoff, _tkc=tkc, _nh=nh:
                       t.matmul(ps_sc[_bi][:, _nh * 512:_nh * 512 + 512],
                                k_sb[_koff:_koff + 64, _koc * S + _tkc * 128:_koc * S + (_tkc + 1) * 128],
                                q_sb[_qoff:_qoff + 64, _qoc * T + _nh * 512:_qoc * T + _nh * 512 + 512],
                                start=True, stop=True),
                       inc=sem.t if nh == 1 else None,
                       reads=[("k",), ("q",)], writes=[("P", psn, nh)])
            P.emit("scalar", lambda s, _bi=bi, _pi=pi3:
                   s.activation(pbuf[:, _pi * T:(_pi + 1) * T], ps_sc[_bi][:], AF.Exp,
                                bias=self.bias_exp, scale=0.125), inc=sem.s,
                   reads=[("P", psn, 0), ("P", psn, 1), ("cst",)], writes=[("pb", pi3)])

        def emit_pv(idx):
            qh, tkc = items[idx]
            h = qh % 8
            pi3 = idx % 3
            for nh in range(2):
                P.emit("tensor", lambda t, _pi=pi3, _h=h, _tkc=tkc, _nh=nh:
                       t.matmul(ps_a[0:65, _nh * 512:_nh * 512 + 512],
                                v_tm[:, _tkc * 520 + _h * 65:_tkc * 520 + _h * 65 + 65],
                                pbuf[:, _pi * T + _nh * 512:_pi * T + _nh * 512 + 512],
                                start=_tkc == 0, stop=_tkc == 31),
                       inc=sem.t if nh == 1 else None,
                       reads=[("v",), ("pb", pi3)], writes=[("P", "ps2", nh)])

        def emit_tail(qh):
            r1 = rowsA[0:1, 0:T]
            P.emit("vector", lambda v, _r=r1: v.reciprocal(_r, ps_a[64:65, :]),
                   reads=[("P", "ps2", 0), ("P", "ps2", 1)], writes=[("r1",)])
            P.emit("vector", lambda v, _r=r1: v.tensor_scalar(_r, _r, mw_sb[0:1, 3:4], None, ALU.mult),
                   reads=[("r1",), ("mw",)], writes=[("r1",)])
            P.emit("vector", lambda v, _r=r1: v.tensor_copy(rbf[:], _r), inc=sem.v,
                   reads=[("r1",)], writes=[("rbf",)])
            for nh in range(2):
                P.emit("tensor", lambda t, _nh=nh:
                       t.matmul(ps_bc[0:64, _nh * 512:_nh * 512 + 512], self.orow_b[0:1, 0:64],
                                rbf[0:1, _nh * 512:_nh * 512 + 512], start=True, stop=True),
                       inc=sem.t if nh == 1 else None,
                       reads=[("rbf",), ("orowb",)], writes=[("P", "ps3", nh)])
            P.emit("scalar", lambda s: s.copy(tmp64[:], ps_a[0:64, :]), inc=sem.s,
                   reads=[("P", "ps2", 0), ("P", "ps2", 1)], writes=[("tmp",)])
            d_ev, o_ev = (qh * 32) // 128, (qh * 32) % 128
            attn_toks[d_ev] = P.emit("vector", lambda v, _d=d_ev, _o=o_ev:
                   v.tensor_tensor(attn[_o:_o + 32, _d * T:(_d + 1) * T], tmp64[0:32, :], ps_bc[0:32, :], ALU.mult),
                   inc=sem.v, reads=[("tmp",), ("P", "ps3", 0), ("P", "ps3", 1)], writes=[("at", d_ev)])
            attn_toks[d_ev + 8] = P.emit("vector", lambda v, _d=d_ev + 8, _o=o_ev:
                   v.tensor_tensor(attn[_o:_o + 32, _d * T:(_d + 1) * T], tmp64[32:64, :], ps_bc[32:64, :], ALU.mult),
                   inc=sem.v, reads=[("tmp",), ("P", "ps3", 0), ("P", "ps3", 1)], writes=[("at", d_ev + 8)])

        LAG = 2
        if sub == 'scores':
            for idx in range(len(items)):
                emit_scores(idx)
        else:
            for idx in range(len(items) + LAG):
                if idx < len(items):
                    emit_scores(idx)
                j = idx - LAG
                if j >= 0:
                    emit_pv(j)
                    if items[j][1] == 31:
                        emit_tail(items[j][0])
        a.attn_toks = attn_toks
        return a

    def emit_a_mem(self, a):
        P, sem = self.P, self.sem
        q_sb, sig_sb, attn = a.q_sb, a.sig_sb, a.attn
        mem_sb, memn_sb, mw_sb = a.mem_sb, a.memn_sb, a.mw_sb
        rowsA, rbf, sctch = a.rowsA, a.rbf, a.sctch
        ps_m, ps_d, ps_bc = a.ps_sc[0], a.ps_a, a.ps_bc
        attn_toks = a.attn_toks
        for qq in range(4):
            for ic in range(4):
                oc = qq * 4 + ic
                qa = q_sb[:, oc * T:(oc + 1) * T]
                mbuf = sctch[:, 0:T]
                pbuf2 = sctch[:, T:2 * T]
                P.emit("vector", lambda v, _q=qa, _m=mbuf: v.tensor_scalar(_m, _q, 0.0, None, ALU.min),
                       reads=[("q",)], writes=[("sct", 0)])
                P.emit("vector", lambda v, _q=qa, _p=pbuf2: v.tensor_scalar(_p, _q, 0.0, None, ALU.max),
                       inc=sem.v, reads=[("q",)], writes=[("sct", 1)])
                P.emit("scalar", lambda s, _m=mbuf: s.activation(_m, _m, AF.Exp), inc=sem.s,
                       reads=[("sct", 0)], writes=[("sct", 0)])
                P.emit("vector", lambda v, _ic=ic, _m=mbuf, _p=pbuf2:
                       v.scalar_tensor_tensor(sig_sb[:, _ic * T:(_ic + 1) * T], _m, -1.0, _p, ALU.add, ALU.add),
                       inc=sem.v, reads=[("sct", 0), ("sct", 1)], writes=[("sig",)])
            for ic in range(4):
                for nh in range(2):
                    P.emit("tensor", lambda t, _ic=ic, _nh=nh:
                           t.matmul(ps_d[64:65, _nh * 512:_nh * 512 + 512], memn_sb[:, _ic:_ic + 1],
                                    sig_sb[:, _ic * T + _nh * 512:_ic * T + _nh * 512 + 512],
                                    start=_ic == 0, stop=_ic == 3),
                           inc=sem.t if (ic == 3 and nh == 1) else None,
                           reads=[("memn",), ("sig",)], writes=[("P", "ps2", nh)])
            r2 = rowsA[0:1, 0:T]
            P.emit("vector", lambda v, _r=r2: v.reciprocal(_r, ps_d[64:65, :]),
                   reads=[("P", "ps2", 0), ("P", "ps2", 1)], writes=[("r1",)])
            P.emit("vector", lambda v, _r=r2: v.tensor_scalar(_r, _r, mw_sb[0:1, 2:3], None, ALU.mult),
                   reads=[("r1",), ("mw",)], writes=[("r1",)])
            P.emit("vector", lambda v, _r=r2: v.tensor_copy(rbf[:], _r), inc=sem.v,
                   reads=[("r1",)], writes=[("rbf",)])
            for nh in range(2):
                P.emit("tensor", lambda t, _nh=nh:
                       t.matmul(ps_bc[:, _nh * 512:_nh * 512 + 512], self.orow_b[0:1, 0:128],
                                rbf[0:1, _nh * 512:_nh * 512 + 512], start=True, stop=True),
                       inc=sem.t if nh == 1 else None,
                       reads=[("rbf",), ("orowb",)], writes=[("P", "ps3", nh)])
            for js in range(4):
                par = 0 if js < 2 else 1
                for ic in range(4):
                    for nh in range(2):
                        P.emit("tensor", lambda t, _ic=ic, _js=js, _par=par, _nh=nh:
                               t.matmul(ps_m[:, _nh * 512:_nh * 512 + 512],
                                        mem_sb[:, _ic * 512:(_ic + 1) * 512].rearrange(
                                            "p (a two) -> p a two", two=2)[:, (_js % 2) * 128:(_js % 2) * 128 + 128, _par],
                                        sig_sb[:, _ic * T + _nh * 512:_ic * T + _nh * 512 + 512],
                                        start=_ic == 0, stop=_ic == 3),
                               inc=sem.t if (ic == 3 and nh == 1) else None,
                               reads=[("mem",), ("sig",)], writes=[("P", "ps0", nh)])
                P.emit("scalar", lambda s: s.copy(sctch[:, 0:T], ps_m[:]), inc=sem.s,
                       reads=[("P", "ps0", 0), ("P", "ps0", 1)], writes=[("sct", 0)])
                dt_i = qq * 2 + (js % 2) + (8 if par else 0)
                P.emit("vector", lambda v: v.tensor_tensor(sctch[:, T:2 * T], sctch[:, 0:T], ps_bc[:], ALU.mult),
                       reads=[("sct", 0), ("P", "ps3", 0), ("P", "ps3", 1)], writes=[("sct", 1)])
                attn_toks[dt_i] = P.emit("vector", lambda v, _d=dt_i:
                       v.tensor_tensor(attn[:, _d * T:(_d + 1) * T], attn[:, _d * T:(_d + 1) * T],
                                       sctch[:, T:2 * T], ALU.add),
                       inc=sem.v, reads=[("at", dt_i), ("sct", 1)], writes=[("at", dt_i)])
        a.attn_toks = attn_toks
        return a

    GELU_TANH_OK = True

    def emit_decode(self, pk_ap, sc_ap, out_ap, dense_shift, tmpa, tmpb, keys_in, keys_sc, keys_out, key_tmp):
        P, sem = self.P, self.sem
        pk32 = pk_ap.bitcast(I32)
        W2 = pk32.shape[-1]
        a32 = tmpa[:, 0:W2]
        b32 = tmpb[:, 0:W2]
        j = dense_shift
        mag = (0x00070007 << (4 * j)) & 0xFFFFFFFF
        sgn = (0x00080008 << (4 * j)) & 0xFFFFFFFF
        if sgn >= 1 << 31:
            sgn = sgn - (1 << 32)
        if j <= 2:
            P.emit("vector", lambda v: v.tensor_scalar(a32, pk32, mag, 9 - 4 * j, ALU.bitwise_and, ALU.logical_shift_left),
                   reads=keys_in, writes=[key_tmp + ("a",)])
        else:
            P.emit("vector", lambda v: v.tensor_scalar(a32, pk32, mag, 4 * j - 9, ALU.bitwise_and, ALU.logical_shift_right),
                   reads=keys_in, writes=[key_tmp + ("a",)])
        P.emit("vector", lambda v: v.tensor_scalar(b32, pk32, sgn, 12 - 4 * j, ALU.bitwise_and, ALU.logical_shift_left),
               reads=keys_in, writes=[key_tmp + ("b",)])
        P.emit("vector", lambda v: v.tensor_tensor(a32, a32, b32, ALU.bitwise_or),
               reads=[key_tmp + ("a",), key_tmp + ("b",)], writes=[key_tmp + ("a",)])
        P.emit("vector", lambda v: v.tensor_tensor(out_ap, a32.bitcast(FP16), sc_ap, ALU.mult),
               inc=sem.v, reads=[key_tmp + ("a",)] + keys_sc, writes=keys_out)

    def emit_l0_ffn(self, b):
        P, sem = self.P, self.sem
        gpk = self.din("l0_gpk", [16, 128, 8192], U16)
        upk = self.din("l0_upk", [16, 128, 8192], U16)
        dpk = self.din("l0_dpk", [16, 128, 8192], U16)
        gsc = self.din("l0_gsc", [16, 2, 8, 4096], F32)
        usc = self.din("l0_usc", [16, 2, 8, 4096], F32)
        dsc = self.din("l0_dsc", [16, 2, 8, 4096], F32)

        x1 = b.x1_aps
        x1k = b.x1_keys
        wbuf = b.wbuf
        pkg, pku = b.pkg, b.pku
        scg, scu = b.scg, b.scu
        inter = b.inter
        y = b.y_acc
        tmpa, tmpb = b.tmpa, b.tmpb
        gb, ub = b.gb, b.ub
        PS = [b.ps_g, b.ps_u, b.ps_d, b.ps_3]
        PSN = ["ps0", "ps1", "ps2", "ps3"]

        stage = 0
        y_keys = [("y", hc) for hc in range(NT)]

        def dec_half(ncb, half, bid):
            buf = wbuf[bid]
            for pc in range(2):
                P.dma("sync", self.S("pkg"), pkg[:],
                      gpk[ncb].rearrange("p (wc o) -> p wc o", wc=8)[:, pc * 4:pc * 4 + 4, half * 512:half * 512 + 512],
                      writes=[("pkg",)])
                P.dma("sync", self.S("pku"), pku[:],
                      upk[ncb].rearrange("p (wc o) -> p wc o", wc=8)[:, pc * 4:pc * 4 + 4, half * 512:half * 512 + 512],
                      writes=[("pku",)])
                P.dma("gpsimd", self.S("scg"), scg[:],
                      gsc[ncb, half, :, pc * 2048:(pc + 1) * 2048].unsqueeze(1).broadcast_to((8, 16, 2048)),
                      writes=[("scg",)])
                P.dma("gpsimd", self.S("scu"), scu[:],
                      usc[ncb, half, :, pc * 2048:(pc + 1) * 2048].unsqueeze(1).broadcast_to((8, 16, 2048)),
                      writes=[("scu",)])
                for wcp in (pc * 4, pc * 4 + 2):
                    off = (wcp % 4) * 512
                    for nib in range(2):
                        ki = nib * 8 + wcp
                        self.emit_decode(pkg[:, off:off + 1024], scg[:, off:off + 1024],
                                         buf[:, ki * 512:(ki + 2) * 512], nib, tmpa, tmpb,
                                         [("pkg",)], [("scg",)], [("wb", bid, ki), ("wb", bid, ki + 1)], ("t",))
                        self.emit_decode(pku[:, off:off + 1024], scu[:, off:off + 1024],
                                         buf[:, (16 + ki) * 512:(18 + ki) * 512], nib, tmpa, tmpb,
                                         [("pku",)], [("scu",)], [("wb", bid, 16 + ki), ("wb", bid, 17 + ki)], ("t",))

        for ncb in range(16):
            half_bid = {0: stage % 2, 1: (stage + 1) % 2}
            if ncb == 0:
                dec_half(0, 0, half_bid[0])
                dec_half(0, 1, half_bid[1])
            def emit_mm_half(half):
                bid = half_bid[half]
                buf = wbuf[bid]
                for si in range(4):
                    wsub = 2 * half + si // 2
                    nib_o = si % 2
                    off = 256 * (wsub % 2)
                    sub_idx = wsub * 2 + nib_o
                    pg, pgn = (PS[0], PSN[0]) if si % 2 == 0 else (PS[2], PSN[2])
                    pu, pun = (PS[1], PSN[1]) if si % 2 == 0 else (PS[3], PSN[3])
                    for ki in range(16):
                        lhs_g = buf[:, ki * 512 + off:ki * 512 + off + 256].rearrange(
                            "p (a two) -> p a two", two=2)[:, :, nib_o]
                        lhs_u = buf[:, (16 + ki) * 512 + off:(16 + ki) * 512 + off + 256].rearrange(
                            "p (a two) -> p a two", two=2)[:, :, nib_o]
                        for nh in range(2):
                            P.emit("tensor", lambda t, _l=lhs_g, _ki=ki, _nh=nh, _ps=pg:
                                   t.matmul(_ps[:, _nh * 512:_nh * 512 + 512], _l,
                                            x1[_ki][:, _nh * 512:_nh * 512 + 512],
                                            start=_ki == 0, stop=_ki == 15),
                                   inc=sem.t if (ki == 15 and nh == 1) else None,
                                   reads=[("wb", bid, ki), x1k[ki]], writes=[("P", pgn, nh)])
                            P.emit("tensor", lambda t, _l=lhs_u, _ki=ki, _nh=nh, _ps=pu:
                                   t.matmul(_ps[:, _nh * 512:_nh * 512 + 512], _l,
                                            x1[_ki][:, _nh * 512:_nh * 512 + 512],
                                            start=_ki == 0, stop=_ki == 15),
                                   inc=sem.t if (ki == 15 and nh == 1) else None,
                                   reads=[("wb", bid, 16 + ki), x1k[ki]], writes=[("P", pun, nh)])
                    it = inter[:, sub_idx * T:(sub_idx + 1) * T]
                    gk = ("gb", si % 2)
                    uk = ("ub", si % 2)
                    gbx = gb[:, (si % 2) * T:(si % 2) * T + T]
                    ubx = ub[:, (si % 2) * T:(si % 2) * T + T]
                    P.emit("scalar", lambda s, _g=gbx, _ps=pg: s.activation(_g, _ps[:], AF.Gelu_apprx_tanh, scale=float(2 ** 14)),
                           inc=sem.s, reads=[("P", pgn, 0), ("P", pgn, 1)], writes=[gk])
                    P.emit("scalar", lambda s, _u=ubx, _ps=pu: s.activation(_u, _ps[:], AF.Copy, scale=float(2 ** 14)),
                           inc=sem.s, reads=[("P", pun, 0), ("P", pun, 1)], writes=[uk])
                    P.emit("vector", lambda v, _it=it, _g=gbx, _u=ubx: v.scalar_tensor_tensor(_it, _g, float(2 ** 14), _u, ALU.mult, ALU.mult),
                           inc=sem.v, reads=[gk, uk], writes=[("int", sub_idx)])

            def emit_down_dec(bid):
                buf = wbuf[bid]
                for ws4 in range(4):
                    pkb, pkk = (pkg, "pkg") if ws4 % 2 == 0 else (pku, "pku")
                    scb, sck = (scg, "scg") if ws4 % 2 == 0 else (scu, "scu")
                    P.dma("sync", self.S(pkk), pkb[:],
                          dpk[ncb].rearrange("p (ws h) -> p ws h", ws=4)[:, ws4, :],
                          writes=[(pkk,)])
                    P.dma("gpsimd", self.S(sck), scb[:],
                          dsc[ncb, ws4 // 2, :, (ws4 % 2) * 2048:(ws4 % 2) * 2048 + 2048].unsqueeze(1).broadcast_to((8, 16, 2048)),
                          writes=[(sck,)])
                    for nib in range(2):
                        pl8 = ws4 * 2 + nib
                        for piece in range(2):
                            self.emit_decode(pkb[:, piece * 1024:piece * 1024 + 1024],
                                             scb[:, piece * 1024:piece * 1024 + 1024],
                                             buf[:, pl8 * 2048 + piece * 1024:pl8 * 2048 + piece * 1024 + 1024],
                                             nib, tmpa, tmpb,
                                             [(pkk,)], [(sck,)],
                                             [("wb", bid, pl8 * 4 + piece * 2),
                                              ("wb", bid, pl8 * 4 + piece * 2 + 1)], ("t",))

            emit_mm_half(0)
            down_bid = stage % 2
            emit_down_dec(down_bid)
            emit_mm_half(1)
            if ncb + 1 < 16:
                dec_half(ncb + 1, 0, (stage + 3) % 2)
            bid = stage % 2
            buf = wbuf[bid]
            for hc in range(NT):
                hoff = 256 * (hc % 8)
                hpar = 1 if hc >= 8 else 0
                pd, pdn = (PS[2], PSN[2]) if hc % 2 == 0 else (PS[3], PSN[3])
                for pl8 in range(8):
                    lhs = buf[:, pl8 * 2048 + hoff:pl8 * 2048 + hoff + 256].rearrange(
                        "p (a two) -> p a two", two=2)[:, :, hpar]
                    rkeys = [("wb", bid, pl8 * 4 + q) for q in range(4)]
                    for nh in range(2):
                        P.emit("tensor", lambda t, _l=lhs, _pl=pl8, _nh=nh, _ps=pd, _f=pl8 == 0, _la=pl8 == 7:
                               t.matmul(_ps[:, _nh * 512:_nh * 512 + 512], _l,
                                        inter[:, _pl * T + _nh * 512:_pl * T + _nh * 512 + 512],
                                        start=_f, stop=_la),
                               inc=sem.t if (pl8 == 7 and nh == 1) else None,
                               reads=rkeys + [("int", pl8)], writes=[("P", pdn, nh)])
                if ncb == 0:
                    P.emit("scalar", lambda s, _hc=hc, _ps=pd: s.copy(y[:, _hc * T:(_hc + 1) * T], _ps[:]),
                           inc=sem.s, reads=[("P", pdn, 0), ("P", pdn, 1)], writes=[("y", hc)])
                else:
                    P.emit("vector", lambda v, _hc=hc, _ps=pd: v.tensor_tensor(y[:, _hc * T:(_hc + 1) * T],
                           y[:, _hc * T:(_hc + 1) * T], _ps[:], ALU.add),
                           inc=sem.v, reads=[("y", hc), ("P", pdn, 0), ("P", pdn, 1)], writes=[("y", hc)])
            if ncb + 1 < 16:
                dec_half(ncb + 1, 1, (stage + 4) % 2)
            stage += 3
        return y_keys

    def emit_resid_ln(self, x_hbm_ap, add_aps, add_keys, out_aps, out_keys, r_sb, sq, rows,
                      ps_s1, ps_s2, ps_b1, ps_b2, kn):
        P, sem = self.P, self.sem
        P.dma("sync", self.S("xr"), r_sb[:].rearrange("p (kc t) -> p kc t", kc=16),
              x_hbm_ap.rearrange("(kc p) t -> p kc t", p=128),
              writes=[(kn["r"], kc) for kc in range(NT)])
        r_keys = []
        for kc in range(NT):
            r = r_sb[:, kc * T:(kc + 1) * T]
            P.emit("vector", lambda v, _r=r, _a=add_aps[kc]: v.tensor_tensor(_r, _r, _a, ALU.add),
                   inc=sem.v, reads=[(kn["r"], kc), add_keys[kc]], writes=[(kn["r"], kc)])
            r_keys.append((kn["r"], kc))
        r_aps = [r_sb[:, kc * T:(kc + 1) * T] for kc in range(NT)]
        return self.emit_ln(r_aps, r_keys, out_aps, out_keys, sq, rows, ps_s1, ps_s2, ps_b1, ps_b2, kn)


def alloc_phase_a_scratch(mk, right):
    nc = mk.nc
    a = NS()
    R = lambda *ar: right.enter_context(nc.sbuf_tensor(*ar, side="right"))
    a.q_sb = R("q_sb", [128, 16 * T], BF16)
    a.k_sb = R("k_sb", [128, 4 * S], BF16)
    a.v_tm = R("v_tm", [128, 32 * 520], BF16)
    return a


def build_phase_a(nc, upto='full'):
    es = ExitStack()
    with es:
        P = Prog(nc)
        mk = MK(nc, es, P)
        x1_out = mk.dout("x1_out", [D, T], F32)
        dbg = mk.dout("dbg", [128, 20 * 1024], F32)

        mk.setup_consts()
        attn = mk.sbuf("attn", [128, 16 * T], BF16)
        mem_sb = mk.sbuf("mem_sb", [128, 4 * 512], BF16)
        memn_sb = mk.sbuf("memn_sb", [128, 4], BF16)
        mw_sb = mk.sbuf("mw_sb", [1, 8], F32)

        ps0 = mk.psum("ps0", [128, 1024])
        ps1 = mk.psum("ps1", [128, 1024])
        ps2 = mk.psum("ps2", [128, 1024])
        ps3 = mk.psum("ps3", [128, 1024])

        a = NS()
        a.attn = attn; a.mem_sb = mem_sb; a.memn_sb = memn_sb; a.mw_sb = mw_sb
        a.ps_sc = [ps0, ps1]; a.ps_a = ps2; a.ps_bc = ps3

        def dump_and_finish(tile, ncols, rkeys):
            tok_out = mk.P.dma("gpsimd", mk.S("out"), dbg[:, 0:ncols], tile[:, 0:ncols], reads=rkeys)
            mk.P.raw("gpsimd", lambda e, _t=tok_out: e.wait_ge(_t.sem.h, _t.tgt))
            P.finalize(es.enter_context(nc.Block()), es)

        with ExitStack() as ph:
            R = lambda *ar: ph.enter_context(nc.sbuf_tensor(*ar, side="right"))
            a.q_sb = R("q_sb", [128, 16 * T], BF16)
            a.k_sb = R("k_sb", [128, 4 * S], BF16)
            a.v_tm = R("v_tm", [128, 32 * 520], BF16)
            with ExitStack() as ph12:
                R2 = lambda *ar: ph12.enter_context(nc.sbuf_tensor(*ar, side="right"))
                a.xsl = R2("xsl", [128, 2 * 8192], BF16)
                a.wkv_sb = R2("wkv_sb", [128, 2 * 8192], BF16)
                a.wqb = R2("wqb", [128, 2 * D], BF16)
                mk.emit_l0_attn(a, upto)
            if upto == 'a1':
                dump_and_finish(a.k_sb, 16384, [("k",)])
                return nc
            if upto == 'a1v':
                dump_and_finish(a.v_tm, 16640, [("v",)])
                return nc
            if upto == 'a2':
                dump_and_finish(a.q_sb, 16384, [("q",)])
                return nc
            P.join([("xsl", 0), ("xsl", 1), ("wk",), ("wv",), ("wqb", 0), ("wqb", 1)])
            with ExitStack() as ph34:
                R3 = lambda *ar: ph34.enter_context(nc.sbuf_tensor(*ar, side="right"))
                a.sig_sb = R3("sig_sb", [128, 4 * T], BF16)
                a.sctch = R3("sctch", [128, 2 * T], F32)
                a.pbuf = R3("pbuf", [128, 3 * T], BF16)
                a.tmp64 = R3("tmp64", [64, T], BF16)
                a.rowsA = R3("rowsA", [1, T], F32)
                a.rbf = R3("rbf", [1, T], BF16)
                if upto.startswith('a3:'):
                    _, nqh_s, sub_s = upto.split(':')
                    mk.emit_l0_scores(a, nqh=int(nqh_s), sub=sub_s)
                    dump_and_finish(a.pbuf, 2048, [("pb", 0), ("pb", 1)])
                    return nc
                mk.emit_l0_scores(a)
                if upto == 'a3':
                    dump_and_finish(a.attn, 16384, [("at", i) for i in range(16)])
                    return nc
                mk.emit_a_mem(a)
                if upto == 'a4':
                    dump_and_finish(a.attn, 16384, [("at", i) for i in range(16)])
                    return nc
        P.join([("q",), ("k",), ("v",), ("sig",), ("sct", 0), ("sct", 1), ("pb", 0), ("pb", 1),
                ("tmp",), ("r1",), ("rbf",), ("xsl", 0), ("xsl", 1), ("wk",), ("wv",), ("wqb", 0), ("wqb", 1)])
        with ExitStack() as ph5:
            R5 = lambda *ar: ph5.enter_context(nc.sbuf_tensor(*ar, side="right"))
            r_sb = R5("r_sb", [128, 16 * T], F32)
            sq = R5("sq", [128, 2 * T], F32)
            rows = R5("rows", [1, 5 * T], F32)
            out_aps = [attn[:, kc * T:(kc + 1) * T] for kc in range(NT)]
            kn = dict(r="rA", sq="sqA", rows="rwA", s1=("P", "ps0"), s2=("P", "ps1"),
                      b1=("P", "ps2"), b2=("P", "ps3"))
            mk.emit_resid_ln(mk.ap_xT_my, out_aps, [("at", kc) for kc in range(NT)],
                             out_aps, [("x1", kc) for kc in range(NT)],
                             r_sb, sq, rows, ps0[0:1, :], ps1[0:1, :], ps2, ps3, kn)
            tok_out = None
            for kc in range(NT):
                tok_out = mk.P.dma("gpsimd", mk.S("out"),
                                   x1_out.rearrange("(kc p) t -> kc p t", p=128)[kc],
                                   attn[:, kc * T:(kc + 1) * T], reads=[("x1", kc)])
            mk.P.raw("gpsimd", lambda e, _t=tok_out: e.wait_ge(_t.sem.h, _t.tgt))
        P.finalize(es.enter_context(nc.Block()), es)
    return nc


def build_phase_b(nc, upto='full'):
    es = ExitStack()
    with es:
        P = Prog(nc)
        mk = MK(nc, es, P)
        x2_out = mk.dout("x2_out", [D, T], F32)
        dbg = mk.dout("dbg", [128, 16 * 1024], F32)
        stream_in = mk.din("stream_in", [D, T], F32)

        mk.setup_consts()
        x1_sb = mk.sbuf("x1_sb", [128, 16 * T], BF16)
        ps_g = mk.psum("psg", [128, 1024])
        ps_u = mk.psum("psu", [128, 1024])
        ps_d = mk.psum("psd", [128, 1024])
        ps_3 = mk.psum("ps3", [128, 1024])

        P.dma("gpsimd", mk.S("si"), x1_sb[:].rearrange("p (kc t) -> p kc t", kc=16),
              stream_in.rearrange("(kc p) t -> p kc t", p=128),
              writes=[("x1", kc) for kc in range(NT)])

        b = NS()
        b.x1_aps = [x1_sb[:, kc * T:(kc + 1) * T] for kc in range(NT)]
        b.x1_keys = [("x1", kc) for kc in range(NT)]
        b.ps_g, b.ps_u, b.ps_d, b.ps_3 = ps_g, ps_u, ps_d, ps_3

        def dump_and_finish(tile, ncols, rkeys):
            tok_out = mk.P.dma("gpsimd", mk.S("out"), dbg[:, 0:ncols], tile[:, 0:ncols], reads=rkeys)
            mk.P.raw("gpsimd", lambda e, _t=tok_out: e.wait_ge(_t.sem.h, _t.tgt))
            P.finalize(es.enter_context(nc.Block()), es)

        y_guard = nc.sbuf_tensor("y_acc", [128, 16 * T], F32, side="right")
        b.y_acc = es.enter_context(y_guard)
        with ExitStack() as ph:
            R = lambda *ar: ph.enter_context(nc.sbuf_tensor(*ar, side="right"))
            b.wbuf = [R("wbuf0", [128, 16384], BF16), R("wbuf1", [128, 16384], BF16)]
            b.pkg = R("pkg", [128, 2048], U16)
            b.pku = R("pku", [128, 2048], U16)
            b.scg = R("scg", [128, 2048], FP16)
            b.scu = R("scu", [128, 2048], FP16)
            b.inter = R("inter", [128, 8 * T], BF16)
            b.tmpa = R("tmpa", [128, 512], I32)
            b.tmpb = R("tmpb", [128, 512], I32)
            b.gb = R("gb", [128, 2 * T], BF16)
            b.ub = R("ub", [128, 2 * T], BF16)
            y_keys = mk.emit_l0_ffn(b)
        if upto == 'y':
            dump_and_finish(b.y_acc, 16384, y_keys)
            return nc
        for kc in range(NT):
            P.emit("vector", lambda v, _kc=kc: v.tensor_tensor(
                   b.y_acc[:, _kc * T:(_kc + 1) * T], b.y_acc[:, _kc * T:(_kc + 1) * T],
                   b.x1_aps[_kc], ALU.add),
                   inc=mk.sem.v, reads=[("y", kc), ("x1", kc)], writes=[("y", kc)])
        P.join([("wb", bid, u) for bid in range(2) for u in range(32)] +
               [("int", si) for si in range(8)] +
               [("pkg",), ("pku",), ("scg",), ("scu",), ("gb", 0), ("gb", 1), ("ub", 0), ("ub", 1),
                ("t", "a"), ("t", "b")])
        with ExitStack() as ph2:
            R2 = lambda *ar: ph2.enter_context(nc.sbuf_tensor(*ar, side="right"))
            sq = R2("sq", [128, 2 * T], F32)
            rows = R2("rows", [1, 5 * T], F32)
            r_aps = [b.y_acc[:, kc * T:(kc + 1) * T] for kc in range(NT)]
            kn = dict(sq="sqB", rows="rwB", s1=("P", "ps0"), s2=("P", "ps1"),
                      b1=("P", "ps2"), b2=("P", "ps3"))
            mk.emit_ln(r_aps, y_keys, b.x1_aps, [("x2", kc) for kc in range(NT)],
                       sq, rows, ps_g[0:1, :], ps_u[0:1, :], ps_d, ps_3, kn)
            tok_out = None
            for kc in range(NT):
                tok_out = mk.P.dma("gpsimd", mk.S("out"),
                                   x2_out.rearrange("(kc p) t -> kc p t", p=128)[kc],
                                   x1_sb[:, kc * T:(kc + 1) * T], reads=[("x2", kc)])
            mk.P.raw("gpsimd", lambda e, _t=tok_out: e.wait_ge(_t.sem.h, _t.tgt))
        P.finalize(es.enter_context(nc.Block()), es)
    return nc


DENOM_L1 = float(np.sqrt(2048.0) / 4096.0)


def build_phase_c(nc, upto='full', ncores=NCORES):
    es = ExitStack()
    with es:
        P = Prog(nc)
        mk = MK(nc, es, P)
        sem = mk.sem
        x3_out = mk.dout("x3_out", [D, T], F32)
        dbg = mk.dout("dbg", [128, 16 * 1024], F32)
        stream_in = mk.din("stream_in", [D, T], F32)
        wq = mk.din("l1_wq", [16, 128, D], F32)
        wk = mk.din("l1_wk", [16, 128, D], F32)
        wv = mk.din("l1_wv", [16, 128, D], F32)
        wo = mk.din("l1_wo", [16, 128, D], F32)
        cmask = mk.din("cmask", [128, 8], F32)
        cc_in = nc.dram_tensor("cc_in", [128, 16], F32, kind="Internal")
        cc_out = nc.dram_tensor("cc_out", [128 * ncores, 16], F32, kind="Internal")

        mk.setup_consts()
        x2_sb = mk.sbuf("x2_sb", [128, 16 * T], BF16)
        cm_sb = mk.sbuf("cm_sb", [128, 8], F32)
        cs_sb = mk.sbuf("cs_sb", [128, 16], F32)
        cy_sb = mk.sbuf("cy_sb", [128, 16], F32)
        ag_sb = mk.sbuf("ag_sb", [128, 16 * ncores], F32)
        ps0 = mk.psum("ps0", [128, 1024])
        ps1 = mk.psum("ps1", [128, 1024])
        ps2 = mk.psum("ps2", [128, 1024])
        ps3 = mk.psum("ps3", [128, 1024])

        P.dma("gpsimd", mk.S("si"), x2_sb[:].rearrange("p (kc t) -> p kc t", kc=16),
              stream_in.rearrange("(kc p) t -> p kc t", p=128),
              writes=[("x2", kc) for kc in range(NT)])
        P.dma("sync", mk.S("cm"), cm_sb[:], cmask, writes=[("cm",)])

        def dump_and_finish(tile, ncols, rkeys):
            tok_out = mk.P.dma("gpsimd", mk.S("out"), dbg[:, 0:ncols], tile[:, 0:ncols], reads=rkeys)
            mk.P.raw("gpsimd", lambda e, _t=tok_out: e.wait_ge(_t.sem.h, _t.tgt))
            P.finalize(es.enter_context(nc.Block()), es)

        q2 = es.enter_context(nc.sbuf_tensor("q2", [128, 16 * T], BF16, side="right"))
        wb = es.enter_context(nc.sbuf_tensor("wb", [128, 2 * D], BF16, side="right"))

        X2 = lambda kc: x2_sb[:, kc * T:(kc + 1) * T]

        def proj(dst_writer, w_ap, wkey_pfx, extra=None):
            for oc in range(16):
                bi = oc % 2
                P.dma("gpsimd", mk.S(wkey_pfx + str(bi)), wb[:, bi * D:(bi + 1) * D], w_ap[oc],
                      writes=[(wkey_pfx, bi)])
                for kc in range(NT):
                    for nh in range(2):
                        P.emit("tensor", lambda t, _bi=bi, _kc=kc, _nh=nh, _ps=dst_writer[0]:
                               t.matmul(_ps[:, _nh * 512:_nh * 512 + 512],
                                        wb[:, _bi * D + _kc * 128:_bi * D + _kc * 128 + 128],
                                        X2(_kc)[:, _nh * 512:_nh * 512 + 512],
                                        start=_kc == 0, stop=_kc == NT - 1),
                               inc=sem.t if (kc == NT - 1 and nh == 1) else None,
                               reads=[(wkey_pfx, bi), ("x2", kc)], writes=[dst_writer[1] + (nh,)])
                dst_writer[2](oc)

        with ExitStack() as phkv:
            Rr = lambda *ar: phkv.enter_context(nc.sbuf_tensor(*ar, side="right"))
            kv_sb = Rr("kv_sb", [128, 16 * T], BF16)
            kbuf = Rr("kbuf", [128, T], BF16)
            kvt_t = Rr("kvt_t", [128, T], BF16)

            def q_ev(oc):
                P.emit("scalar", lambda s, _oc=oc: s.copy(q2[:, _oc * T:(_oc + 1) * T], ps0[:]),
                       inc=sem.s, reads=[("P", "ps0", 0), ("P", "ps0", 1)], writes=[("q2", oc)])
            proj((ps0, ("P", "ps0"), q_ev)[0:3], wq, "wA")

            for oc in range(16):
                bi = oc % 2
                P.dma("gpsimd", mk.S("wA" + str(bi)), wb[:, bi * D:(bi + 1) * D], wk[oc], writes=[("wA", bi)])
                for kc in range(NT):
                    for nh in range(2):
                        P.emit("tensor", lambda t, _bi=bi, _kc=kc, _nh=nh:
                               t.matmul(ps0[:, _nh * 512:_nh * 512 + 512],
                                        wb[:, _bi * D + _kc * 128:_bi * D + _kc * 128 + 128],
                                        X2(_kc)[:, _nh * 512:_nh * 512 + 512],
                                        start=_kc == 0, stop=_kc == NT - 1),
                               inc=sem.t if (kc == NT - 1 and nh == 1) else None,
                               reads=[("wA", bi), ("x2", kc)], writes=[("P", "ps0", 0 if nh == 0 else 1)])
                P.emit("scalar", lambda s: s.copy(kbuf[:], ps0[:]),
                       inc=sem.s, reads=[("P", "ps0", 0), ("P", "ps0", 1)], writes=[("kb",)])
                P.dma("gpsimd", mk.S("wB" + str(bi)), wb[:, bi * D:(bi + 1) * D], wv[oc], writes=[("wA", bi)])
                for kc in range(NT):
                    for nh in range(2):
                        P.emit("tensor", lambda t, _bi=bi, _kc=kc, _nh=nh:
                               t.matmul(ps1[:, _nh * 512:_nh * 512 + 512],
                                        wb[:, _bi * D + _kc * 128:_bi * D + _kc * 128 + 128],
                                        X2(_kc)[:, _nh * 512:_nh * 512 + 512],
                                        start=_kc == 0, stop=_kc == NT - 1),
                               inc=sem.t if (kc == NT - 1 and nh == 1) else None,
                               reads=[("wA", bi), ("x2", kc)], writes=[("P", "ps1", nh)])
                P.emit("vector", lambda v, _oc=oc: v.tensor_tensor(kv_sb[:, _oc * T:(_oc + 1) * T], kbuf[:], ps1[:], ALU.mult),
                       inc=sem.v, reads=[("kb",), ("P", "ps1", 0), ("P", "ps1", 1)], writes=[("kv", oc)])
                P.emit("vector", lambda v, _oc=oc: v.tensor_reduce(cs_sb[:, _oc:_oc + 1], kv_sb[:, _oc * T:(_oc + 1) * T], AX.X, ALU.add),
                       inc=sem.v, reads=[("kv", oc)], writes=[("cs",)])

            P.dma("gpsimd", mk.S("ccw"), cc_in.ap(), cs_sb[:], reads=[("cs",)], writes=[("ccin",)])
            P.emit("gpsimd", lambda g: g.collective_compute(
                       "AllGather", ALU.bypass, replica_groups=[list(range(ncores))],
                       ins=[cc_in.ap()], outs=[cc_out.ap()]),
                   inc=sem.cc, reads=[("ccin",)], writes=[("ccout",)])
            P.dma("gpsimd", mk.S("agr"), ag_sb[:, 0:16 * ncores],
                  cc_out.ap().rearrange("(r p) oc -> p r oc", p=128), reads=[("ccout",)], writes=[("ag",)])
            P.emit("vector", lambda v: v.memset(cy_sb[:], 0.0), writes=[("cy",)])
            for r in range(ncores):
                P.emit("vector", lambda v, _r=r: v.scalar_tensor_tensor(
                           cy_sb[:], ag_sb[:, _r * 16:(_r + 1) * 16], cm_sb[:, _r:_r + 1], cy_sb[:],
                           ALU.mult, ALU.add),
                       inc=sem.v if r == ncores - 1 else None,
                       reads=[("ag",), ("cm",), ("cy",)], writes=[("cy",)])

            for oc in range(16):
                P.emit("vector", lambda v, _oc=oc: v.tensor_tensor_scan(
                           kvt_t[:], kv_sb[:, _oc * T:(_oc + 1) * T], kv_sb[:, _oc * T:(_oc + 1) * T],
                           cy_sb[:, _oc:_oc + 1], ALU.add, ALU.bypass),
                       inc=sem.v, reads=[("kv", oc), ("cy",)], writes=[("kvt",)])
                P.emit("vector", lambda v, _oc=oc: v.scalar_tensor_tensor(
                           q2[:, _oc * T:(_oc + 1) * T], q2[:, _oc * T:(_oc + 1) * T], DENOM_L1, kvt_t[:],
                           ALU.mult, ALU.mult),
                       inc=sem.v, reads=[("q2", oc), ("kvt",)], writes=[("q2", oc)])

        if upto == 'vhat':
            dump_and_finish(q2, 16384, [("q2", oc) for oc in range(16)])
            return nc

        P.join([("kv", oc) for oc in range(16)] + [("kb",), ("kvt",)])
        with ExitStack() as phr:
            Rr = lambda *ar: phr.enter_context(nc.sbuf_tensor(*ar, side="right"))
            r_sb = Rr("r_sb", [128, 16 * T], F32)
            sq = Rr("sq", [128, 2 * T], F32)
            rows = Rr("rows", [1, 5 * T], F32)
            for oc in range(16):
                bi = oc % 2
                P.dma("gpsimd", mk.S("wA" + str(bi)), wb[:, bi * D:(bi + 1) * D], wo[oc], writes=[("wA", bi)])
                for kc in range(NT):
                    for nh in range(2):
                        P.emit("tensor", lambda t, _bi=bi, _kc=kc, _nh=nh:
                               t.matmul(ps2[:, _nh * 512:_nh * 512 + 512],
                                        wb[:, _bi * D + _kc * 128:_bi * D + _kc * 128 + 128],
                                        q2[:, _kc * T + _nh * 512:_kc * T + _nh * 512 + 512],
                                        start=_kc == 0, stop=_kc == NT - 1),
                               inc=sem.t if (kc == NT - 1 and nh == 1) else None,
                               reads=[("wA", bi), ("q2", kc)], writes=[("P", "ps2", nh)])
                P.emit("vector", lambda v, _oc=oc: v.tensor_tensor(r_sb[:, _oc * T:(_oc + 1) * T],
                           x2_sb[:, _oc * T:(_oc + 1) * T], ps2[:], ALU.add),
                       inc=sem.v, reads=[("x2", oc), ("P", "ps2", 0), ("P", "ps2", 1)], writes=[("rC", oc)])
            r_aps = [r_sb[:, kc * T:(kc + 1) * T] for kc in range(NT)]
            kn = dict(sq="sqC", rows="rwC", s1=("P", "ps0"), s2=("P", "ps1"),
                      b1=("P", "ps2"), b2=("P", "ps3"))
            P.join([("q2", oc) for oc in range(16)], engines=("tensor",))
            mk.emit_ln(r_aps, [("rC", kc) for kc in range(NT)],
                       [X2(kc) for kc in range(NT)], [("x3", kc) for kc in range(NT)],
                       sq, rows, ps0[0:1, :], ps1[0:1, :], ps2, ps3, kn)
            tok_out = None
            for kc in range(NT):
                tok_out = mk.P.dma("gpsimd", mk.S("out"),
                                   x3_out.rearrange("(kc p) t -> kc p t", p=128)[kc],
                                   x2_sb[:, kc * T:(kc + 1) * T], reads=[("x3", kc)])
            mk.P.raw("gpsimd", lambda e, _t=tok_out: e.wait_ge(_t.sem.h, _t.tgt))
        P.finalize(es.enter_context(nc.Block()), es)
    return nc


L1_CHUNKS = [(0, 2), (2, 2), (4, 2), (6, 2), (8, 2), (10, 1)]


def build_phase_d(nc, upto='full'):
    es = ExitStack()
    with es:
        P = Prog(nc)
        mk = MK(nc, es, P)
        sem = mk.sem
        x4_out = mk.dout("x4_out", [D, T], F32)
        dbg = mk.dout("dbg", [128, 16 * 1024], F32)
        stream_in = mk.din("stream_in", [D, T], F32)
        gpk = mk.din("l1_gpk", [22, 128, 2048], U16)
        dpk = mk.din("l1_dpk", [11, 128, 2048], U16)
        gsc = mk.din("l1_gsc", [22, 4, 4, 512], FP16)
        dsc = mk.din("l1_dsc", [11, 4, 2048], FP16)
        x3d = nc.dram_tensor("x3d", [D, T], BF16, kind="Internal")

        mk.setup_consts()
        x3_sb = mk.sbuf("x3_sb", [128, 16 * T], BF16)
        ps0 = mk.psum("ps0", [128, 1024])
        ps1 = mk.psum("ps1", [128, 1024])
        ps2 = mk.psum("ps2", [128, 1024])
        ps3 = mk.psum("ps3", [128, 1024])

        P.dma("gpsimd", mk.S("si"), x3_sb[:].rearrange("p (kc t) -> p kc t", kc=16),
              stream_in.rearrange("(kc p) t -> p kc t", p=128),
              writes=[("x3", kc) for kc in range(NT)])

        def dump_and_finish(tile, ncols, rkeys):
            tok_out = mk.P.dma("gpsimd", mk.S("out"), dbg[:, 0:ncols], tile[:, 0:ncols], reads=rkeys)
            mk.P.raw("gpsimd", lambda e, _t=tok_out: e.wait_ge(_t.sem.h, _t.tgt))
            P.finalize(es.enter_context(nc.Block()), es)

        for kc in range(NT):
            P.dma("sync", mk.S("bw"), x3d.ap().rearrange("(kc p) t -> kc p t", p=128)[kc],
                  x3_sb[:, kc * T:(kc + 1) * T], reads=[("x3", kc)], writes=[("x3d", kc)])
        xj = es.enter_context(nc.sbuf_tensor("xj", [128, 16 * T], BF16, side="right"))
        y = es.enter_context(nc.sbuf_tensor("y_acc", [128, 16 * T], F32, side="right"))
        x3r = x3d.ap().rearrange("(a two) t -> two a t", two=2)
        for jp in range(4):
            par, hi = jp % 2, jp // 2
            par = 1 if jp in (2, 3) else 0
            hi = 1 if jp in (1, 3) else 0
            for tt in range(4):
                a0 = hi * 512 + tt * 128
                P.dma("sync", mk.S("xj%d" % (jp * 4 + tt)), xj[:, (jp * 4 + tt) * T:(jp * 4 + tt + 1) * T],
                      x3r[par, a0:a0 + 128, :],
                      reads=[("x3d", kc) for kc in range(NT)], writes=[("xj", jp, tt)])

        XJ = lambda jp, tt: xj[:, (jp * 4 + tt) * T:(jp * 4 + tt + 1) * T]

        with ExitStack() as ph:
            Rr = lambda *ar: ph.enter_context(nc.sbuf_tensor(*ar, side="right"))
            wdec = Rr("wdec", [128, 16384], BF16)
            pk0 = Rr("pk0", [128, 2048], U16)
            pk1 = Rr("pk1", [128, 2048], U16)
            sc0 = Rr("sc0", [128, 2048], FP16)
            sc1 = Rr("sc1", [128, 2048], FP16)
            inter = Rr("interD", [128, 8 * T], BF16)
            tmpa = Rr("tmpa", [128, 1024], I32)
            tmpb = Rr("tmpb", [128, 1024], I32)
            gsb = Rr("gsb", [128, T], BF16)

            GUK = lambda ki: ("gu", ki)
            y_keys = [("y", hc) for hc in range(NT)]
            for ci, (cs, nt) in enumerate(L1_CHUNKS):
                for tt in range(nt):
                    for half, span in ((0, cs + tt), (1, 11 + cs + tt)):
                        pkb, pkk = (pk0, ("pk", 0)) if half == 0 else (pk1, ("pk", 1))
                        scb, sck = (sc0, ("sc", 0)) if half == 0 else (sc1, ("sc", 1))
                        P.dma("sync", mk.S("pk%d" % half), pkb[:], gpk[span], writes=[pkk])
                        for tin in range(4):
                            P.dma("sync", mk.S("sc%d%d" % (half, tin)), scb[:, tin * 512:(tin + 1) * 512],
                                  gsc[span, tin].unsqueeze(1).broadcast_to((4, 32, 512)),
                                  writes=[sck + (tin,)])
                        for jin in range(4):
                            for tin in range(4):
                                ki = jin * 4 + tin
                                dst = wdec[:, (half * 16 + ki) * 512:(half * 16 + ki + 1) * 512]
                                self_keys = [("w", half, ki)]
                                mk.emit_decode(pkb[:, tin * 512:(tin + 1) * 512], scb[:, tin * 512:(tin + 1) * 512],
                                               dst, jin, tmpa, tmpb,
                                               [pkk], [sck + (tin,)], self_keys, ("t",))
                    for jo in range(4):
                        for half in range(2):
                            psx, psk = (ps0, ("P", "ps0")) if half == 0 else (ps1, ("P", "ps1"))
                            for jin in range(4):
                                for tin in range(4):
                                    ki = jin * 4 + tin
                                    lhs = wdec[:, (half * 16 + ki) * 512:(half * 16 + ki + 1) * 512].rearrange(
                                        "p (a four) -> p a four", four=4)[:, :, jo]
                                    for nh in range(2):
                                        P.emit("tensor", lambda t, _l=lhs, _jin=jin, _tin=tin, _nh=nh, _ps=psx,
                                               _f=ki == 0, _la=ki == 15:
                                               t.matmul(_ps[:, _nh * 512:_nh * 512 + 512], _l,
                                                        XJ(_jin, _tin)[:, _nh * 512:_nh * 512 + 512],
                                                        start=_f, stop=_la),
                                               inc=sem.t if (ki == 15 and nh == 1) else None,
                                               reads=[("w", half, ki), ("xj", jin, tin)], writes=[psk + (nh,)])
                        it = inter[:, (tt * 4 + jo) * T:(tt * 4 + jo + 1) * T]
                        P.emit("scalar", lambda s: s.activation(gsb[:], ps0[:], AF.Silu, scale=float(2 ** 14)),
                               inc=sem.s, reads=[("P", "ps0", 0), ("P", "ps0", 1)], writes=[("gsb",)])
                        P.emit("vector", lambda v, _it=it: v.scalar_tensor_tensor(_it, gsb[:], float(2 ** 28), ps1[:], ALU.mult, ALU.mult),
                               inc=sem.v, reads=[("gsb",), ("P", "ps1", 0), ("P", "ps1", 1)],
                               writes=[("int", tt * 4 + jo)])
                for tt in range(nt):
                    pkb, pkk = (pk0, ("pk", 0)) if tt % 2 == 0 else (pk1, ("pk", 1))
                    scb, sck = (sc0, ("sc", 0)) if tt % 2 == 0 else (sc1, ("sc", 1))
                    P.dma("sync", mk.S("pk%d" % (tt % 2)), pkb[:], dpk[cs + tt], writes=[pkk])
                    P.dma("sync", mk.S("scd%d" % (tt % 2)), scb[:],
                          dsc[cs + tt].unsqueeze(1).broadcast_to((4, 32, 2048)), writes=[sck + (0,), sck + (1,), sck + (2,), sck + (3,)])
                    for jd in range(4):
                        pl = tt * 4 + jd
                        dst = wdec[:, pl * 2048:(pl + 1) * 2048]
                        alias = [("w", 0, 4 * pl + q) for q in range(4)] if pl < 4 else \
                                [("w", 1, 4 * (pl - 4) + q) for q in range(4)]
                        mk.emit_decode(pkb[:], scb[:], dst, jd, tmpa, tmpb,
                                       [pkk], [sck + (q,) for q in range(4)], [("dw", pl)] + alias, ("t",))
                for hc in range(NT):
                    hoff = 256 * (hc % 8)
                    hpar = 1 if hc >= 8 else 0
                    np_ = nt * 4
                    for pl in range(np_):
                        tt, jd = pl // 4, pl % 4
                        lhs = wdec[:, pl * 2048 + hoff:pl * 2048 + hoff + 256].rearrange(
                            "p (a two) -> p a two", two=2)[:, :, hpar]
                        alias_r = ([("w", 0, 4 * pl + q) for q in range(4)] if pl < 4 else
                                   [("w", 1, 4 * (pl - 4) + q) for q in range(4)])
                        for nh in range(2):
                            P.emit("tensor", lambda t, _l=lhs, _pl=pl, _nh=nh, _f=pl == 0, _la=pl == np_ - 1:
                                   t.matmul(ps2[:, _nh * 512:_nh * 512 + 512], _l,
                                            inter[:, _pl * T + _nh * 512:_pl * T + _nh * 512 + 512],
                                            start=_f, stop=_la),
                                   inc=sem.t if (pl == np_ - 1 and nh == 1) else None,
                                   reads=[("dw", pl), ("int", pl)] + alias_r, writes=[("P", "ps2", nh)])
                    if ci == 0:
                        P.emit("scalar", lambda s, _hc=hc: s.copy(y[:, _hc * T:(_hc + 1) * T], ps2[:]),
                               inc=sem.s, reads=[("P", "ps2", 0), ("P", "ps2", 1)], writes=[("y", hc)])
                    else:
                        P.emit("vector", lambda v, _hc=hc: v.tensor_tensor(y[:, _hc * T:(_hc + 1) * T],
                               y[:, _hc * T:(_hc + 1) * T], ps2[:], ALU.add),
                               inc=sem.v, reads=[("y", hc), ("P", "ps2", 0), ("P", "ps2", 1)], writes=[("y", hc)])
        if upto == 'y':
            dump_and_finish(y, 16384, y_keys)
            return nc

        P.join([("w", 0, k) for k in range(16)] + [("w", 1, k) for k in range(16)] +
               [("dw", pl) for pl in range(8)] + [("int", q) for q in range(8)] +
               [("pk", 0), ("pk", 1), ("sc", 0, 0), ("sc", 0, 1), ("sc", 0, 2), ("sc", 0, 3),
                ("sc", 1, 0), ("sc", 1, 1), ("sc", 1, 2), ("sc", 1, 3), ("gsb",), ("t", "a"), ("t", "b")])
        with ExitStack() as ph2:
            Rr = lambda *ar: ph2.enter_context(nc.sbuf_tensor(*ar, side="right"))
            sq = Rr("sq", [128, 2 * T], F32)
            rows = Rr("rows", [1, 5 * T], F32)
            RW = [("rwD",)]
            for hc in range(NT):
                buf = sq[:, (hc % 2) * T:(hc % 2) * T + T]
                P.emit("scalar", lambda s, _b=buf, _hc=hc: s.activation(_b, y[:, _hc * T:(_hc + 1) * T], AF.Square),
                       inc=sem.s, reads=[("y", hc)], writes=[("sqD", hc % 2)])
                for nh in range(2):
                    P.emit("tensor", lambda t, _b=buf, _nh=nh, _f=hc == 0, _la=hc == NT - 1:
                           t.matmul(ps0[0:1, _nh * 512:_nh * 512 + 512], mk.ones_col,
                                    _b[:, _nh * 512:_nh * 512 + 512], start=_f, stop=_la),
                           inc=sem.t if (hc == NT - 1 and nh == 1) else None,
                           reads=[("sqD", hc % 2), ("cst",)], writes=[("P", "ps0", nh)])
            msq = rows[0:1, 0:T]
            P.emit("vector", lambda v: v.tensor_scalar(msq, ps0[0:1, :], 1.0 / D, None, ALU.mult),
                   inc=sem.v, reads=[("P", "ps0", 0), ("P", "ps0", 1)], writes=RW)
            P.emit("scalar", lambda s: s.activation(msq, msq, AF.Ln, bias=mk.eps_rms),
                   reads=RW + [("cst",)], writes=RW)
            P.emit("scalar", lambda s: s.activation(msq, msq, AF.Exp, scale=-0.5), inc=sem.s,
                   reads=RW, writes=RW)
            for nh in range(2):
                P.emit("tensor", lambda t, _nh=nh: t.matmul(ps1[:, _nh * 512:_nh * 512 + 512],
                       mk.orow[0:1, 0:128], msq[0:1, _nh * 512:_nh * 512 + 512], start=True, stop=True),
                       inc=sem.t if nh == 1 else None,
                       reads=RW + [("orow",)], writes=[("P", "ps1", nh)])
            for kc in range(NT):
                P.emit("vector", lambda v, _kc=kc: v.tensor_tensor(y[:, _kc * T:(_kc + 1) * T],
                       y[:, _kc * T:(_kc + 1) * T], ps1[:], ALU.mult),
                       reads=[("y", kc), ("P", "ps1", 0), ("P", "ps1", 1)], writes=[("y", kc)])
                P.emit("vector", lambda v, _kc=kc: v.tensor_tensor(y[:, _kc * T:(_kc + 1) * T],
                       y[:, _kc * T:(_kc + 1) * T], x3_sb[:, _kc * T:(_kc + 1) * T], ALU.add),
                       inc=sem.v, reads=[("y", kc), ("x3", kc)], writes=[("y", kc)])
            r_aps = [y[:, kc * T:(kc + 1) * T] for kc in range(NT)]
            kn = dict(sq="sqD2", rows="rwD2", s1=("P", "ps0"), s2=("P", "ps2"),
                      b1=("P", "ps1"), b2=("P", "ps3"))
            mk.emit_ln(r_aps, [("y", kc) for kc in range(NT)],
                       [x3_sb[:, kc * T:(kc + 1) * T] for kc in range(NT)],
                       [("x4", kc) for kc in range(NT)],
                       sq, rows, ps0[0:1, :], ps2[0:1, :], ps1, ps3, kn)
            tok_out = None
            for kc in range(NT):
                tok_out = mk.P.dma("gpsimd", mk.S("out"),
                                   x4_out.rearrange("(kc p) t -> kc p t", p=128)[kc],
                                   x3_sb[:, kc * T:(kc + 1) * T], reads=[("x4", kc)])
            mk.P.raw("gpsimd", lambda e, _t=tok_out: e.wait_ge(_t.sem.h, _t.tgt))
        P.finalize(es.enter_context(nc.Block()), es)
    return nc



def build_fused(nc, ncores=NCORES):
    es = ExitStack()
    with es:
        P = Prog(nc)
        mk = MK(nc, es, P)
        sem = mk.sem
        out_ap = mk.dout("x4_out", [D, T], F32)

        mk.setup_consts()
        stream = mk.sbuf("stream", [128, 16 * T], BF16)
        ps0 = mk.psum("ps0", [128, 1024])
        ps1 = mk.psum("ps1", [128, 1024])
        ps2 = mk.psum("ps2", [128, 1024])
        ps3 = mk.psum("ps3", [128, 1024])
        ST = lambda kc: stream[:, kc * T:(kc + 1) * T]

        a = NS()
        a.attn = stream
        a.ps_sc = [ps0, ps1]; a.ps_a = ps2; a.ps_bc = ps3
        with ExitStack() as ph:
            R = lambda *ar: ph.enter_context(nc.sbuf_tensor(*ar, side="right"))
            a.mem_sb = R("mem_sb", [128, 4 * 512], BF16)
            a.memn_sb = R("memn_sb", [128, 4], BF16)
            a.mw_sb = R("mw_sb", [1, 8], F32)
            a.q_sb = R("q_sb", [128, 16 * T], BF16)
            a.k_sb = R("k_sb", [128, 4 * S], BF16)
            a.v_tm = R("v_tm", [128, 32 * 520], BF16)
            with ExitStack() as ph12:
                R2 = lambda *ar: ph12.enter_context(nc.sbuf_tensor(*ar, side="right"))
                a.xsl = R2("xsl", [128, 2 * 8192], BF16)
                a.wkv_sb = R2("wkv_sb", [128, 2 * 8192], BF16)
                a.wqb = R2("wqb", [128, 2 * D], BF16)
                mk.emit_l0_attn(a, 'full')
            P.join([("xsl", 0), ("xsl", 1), ("wk",), ("wv",), ("wqb", 0), ("wqb", 1)])
            with ExitStack() as ph34:
                R3 = lambda *ar: ph34.enter_context(nc.sbuf_tensor(*ar, side="right"))
                a.sig_sb = R3("sig_sb", [128, 4 * T], BF16)
                a.sctch = R3("sctch", [128, 2 * T], F32)
                a.pbuf = R3("pbuf", [128, 3 * T], BF16)
                a.tmp64 = R3("tmp64", [64, T], BF16)
                a.rowsA = R3("rowsA", [1, T], F32)
                a.rbf = R3("rbf", [1, T], BF16)
                mk.emit_l0_scores(a)
                mk.emit_a_mem(a)
        P.join([("q",), ("k",), ("v",), ("sig",), ("sct", 0), ("sct", 1), ("pb", 0), ("pb", 1),
                ("tmp",), ("r1",), ("rbf",), ("xsl", 0), ("xsl", 1), ("wk",), ("wv",),
                ("wqb", 0), ("wqb", 1), ("mem",), ("memn",), ("mw",)])
        with ExitStack() as ph5:
            R5 = lambda *ar: ph5.enter_context(nc.sbuf_tensor(*ar, side="right"))
            r_sb = R5("r_sbA", [128, 16 * T], F32)
            sq = R5("sqA", [128, 2 * T], F32)
            rows = R5("rowsA5", [1, 5 * T], F32)
            out_aps = [ST(kc) for kc in range(NT)]
            kn = dict(r="rA", sq="sqA", rows="rwA", s1=("P", "ps0"), s2=("P", "ps1"),
                      b1=("P", "ps2"), b2=("P", "ps3"))
            mk.emit_resid_ln(mk.ap_xT_my, out_aps, [("at", kc) for kc in range(NT)],
                             out_aps, [("x1", kc) for kc in range(NT)],
                             r_sb, sq, rows, ps0[0:1, :], ps1[0:1, :], ps2, ps3, kn)
        P.join([("rA", kc) for kc in range(NT)] + [("sqA", 0), ("sqA", 1), ("rwA",)])

        b = NS()
        b.x1_aps = [ST(kc) for kc in range(NT)]
        b.x1_keys = [("x1", kc) for kc in range(NT)]
        b.ps_g, b.ps_u, b.ps_d, b.ps_3 = ps0, ps1, ps2, ps3
        b.y_acc = es.enter_context(nc.sbuf_tensor("y_accB", [128, 16 * T], F32, side="right"))
        with ExitStack() as ph:
            R = lambda *ar: ph.enter_context(nc.sbuf_tensor(*ar, side="right"))
            b.wbuf = [R("wbuf0B", [128, 16384], BF16), R("wbuf1B", [128, 16384], BF16)]
            b.pkg = R("pkgB", [128, 2048], U16)
            b.pku = R("pkuB", [128, 2048], U16)
            b.scg = R("scgB", [128, 2048], FP16)
            b.scu = R("scuB", [128, 2048], FP16)
            b.inter = R("interB", [128, 8 * T], BF16)
            b.tmpa = R("tmpaB", [128, 512], I32)
            b.tmpb = R("tmpbB", [128, 512], I32)
            b.gb = R("gbB", [128, 2 * T], BF16)
            b.ub = R("ubB", [128, 2 * T], BF16)
            y_keys = mk.emit_l0_ffn(b)
        for kc in range(NT):
            P.emit("vector", lambda v, _kc=kc: v.tensor_tensor(
                   b.y_acc[:, _kc * T:(_kc + 1) * T], b.y_acc[:, _kc * T:(_kc + 1) * T],
                   ST(_kc), ALU.add),
                   inc=sem.v, reads=[("y", kc), ("x1", kc)], writes=[("y", kc)])
        P.join([("wb", bid, u) for bid in range(2) for u in range(32)] +
               [("int", si) for si in range(8)] +
               [("pkg",), ("pku",), ("scg",), ("scu",), ("gb", 0), ("gb", 1), ("ub", 0), ("ub", 1),
                ("t", "a"), ("t", "b")])
        with ExitStack() as ph2:
            R2 = lambda *ar: ph2.enter_context(nc.sbuf_tensor(*ar, side="right"))
            sq = R2("sqB", [128, 2 * T], F32)
            rows = R2("rowsB", [1, 5 * T], F32)
            r_aps = [b.y_acc[:, kc * T:(kc + 1) * T] for kc in range(NT)]
            kn = dict(sq="sqB", rows="rwB", s1=("P", "ps0"), s2=("P", "ps1"),
                      b1=("P", "ps2"), b2=("P", "ps3"))
            mk.emit_ln(r_aps, y_keys, b.x1_aps, [("x2", kc) for kc in range(NT)],
                       sq, rows, ps0[0:1, :], ps1[0:1, :], ps2, ps3, kn)
        P.join([("y", kc) for kc in range(NT)] + [("sqB", 0), ("sqB", 1), ("rwB",)])

        cmask = mk.din("cmask", [128, 8], F32)
        wq1 = mk.din("l1_wq", [16, 128, D], F32)
        wk1 = mk.din("l1_wk", [16, 128, D], F32)
        wv1 = mk.din("l1_wv", [16, 128, D], F32)
        wo1 = mk.din("l1_wo", [16, 128, D], F32)
        cc_in = nc.dram_tensor("cc_in", [128, 16], F32, kind="Internal")
        cc_out = nc.dram_tensor("cc_out", [128 * ncores, 16], F32, kind="Internal")
        cm_sb = mk.sbuf("cm_sb", [128, 8], F32)
        cs_sb = mk.sbuf("cs_sb", [128, 16], F32)
        cy_sb = mk.sbuf("cy_sb", [128, 16], F32)
        ag_sb = mk.sbuf("ag_sb", [128, 16 * ncores], F32)
        P.dma("sync", mk.S("cm"), cm_sb[:], cmask, writes=[("cm",)])

        with ExitStack() as phC:
            RC = lambda *ar: phC.enter_context(nc.sbuf_tensor(*ar, side="right"))
            q2 = RC("q2", [128, 16 * T], BF16)
            wb = RC("wbC", [128, 4 * D], BF16)
            with ExitStack() as phkv:
                Rr = lambda *ar: phkv.enter_context(nc.sbuf_tensor(*ar, side="right"))
                kv_sb = Rr("kv_sb", [128, 16 * T], BF16)
                kbuf = Rr("kbuf", [128, T], BF16)
                kvt_t = Rr("kvt_t", [128, T], BF16)
                for oc in range(16):
                    bi = (2 * oc) % 4
                    P.dma("gpsimd", mk.S("wA" + str(bi)), wb[:, bi * D:(bi + 1) * D], wk1[oc], writes=[("wA", bi)])
                    for kc in range(NT):
                        for nh in range(2):
                            P.emit("tensor", lambda t, _bi=bi, _kc=kc, _nh=nh:
                                   t.matmul(ps0[:, _nh * 512:_nh * 512 + 512],
                                            wb[:, _bi * D + _kc * 128:_bi * D + _kc * 128 + 128],
                                            ST(_kc)[:, _nh * 512:_nh * 512 + 512],
                                            start=_kc == 0, stop=_kc == NT - 1),
                                   inc=sem.t if (kc == NT - 1 and nh == 1) else None,
                                   reads=[("wA", bi), ("x2", kc)], writes=[("P", "ps0", nh)])
                    P.emit("scalar", lambda s: s.copy(kbuf[:], ps0[:]),
                           inc=sem.s, reads=[("P", "ps0", 0), ("P", "ps0", 1)], writes=[("kb",)])
                    bi2 = (2 * oc + 1) % 4
                    P.dma("gpsimd", mk.S("wB" + str(bi2)), wb[:, bi2 * D:(bi2 + 1) * D], wv1[oc], writes=[("wA", bi2)])
                    vp, vpn = (ps1, "ps1") if oc % 2 == 0 else (ps3, "ps3")
                    for kc in range(NT):
                        for nh in range(2):
                            P.emit("tensor", lambda t, _bi=bi2, _kc=kc, _nh=nh, _ps=vp:
                                   t.matmul(_ps[:, _nh * 512:_nh * 512 + 512],
                                            wb[:, _bi * D + _kc * 128:_bi * D + _kc * 128 + 128],
                                            ST(_kc)[:, _nh * 512:_nh * 512 + 512],
                                            start=_kc == 0, stop=_kc == NT - 1),
                                   inc=sem.t if (kc == NT - 1 and nh == 1) else None,
                                   reads=[("wA", bi2), ("x2", kc)], writes=[("P", vpn, nh)])
                    P.emit("vector", lambda v, _oc=oc, _ps=vp: v.tensor_tensor(kv_sb[:, _oc * T:(_oc + 1) * T], kbuf[:], _ps[:], ALU.mult),
                           inc=sem.v, reads=[("kb",), ("P", vpn, 0), ("P", vpn, 1)], writes=[("kv", oc)])
                    P.emit("vector", lambda v, _oc=oc: v.tensor_reduce(cs_sb[:, _oc:_oc + 1], kv_sb[:, _oc * T:(_oc + 1) * T], AX.X, ALU.add),
                           inc=sem.v, reads=[("kv", oc)], writes=[("cs",)])
                P.dma("gpsimd", mk.S("ccw"), cc_in.ap(), cs_sb[:], reads=[("cs",)], writes=[("ccin",)])
                P.emit("gpsimd", lambda g: g.collective_compute(
                           "AllGather", ALU.bypass, replica_groups=[list(range(ncores))],
                           ins=[cc_in.ap()], outs=[cc_out.ap()]),
                       inc=sem.cc, reads=[("ccin",)], writes=[("ccout",)])
                P.dma("gpsimd", mk.S("agr"), ag_sb[:, 0:16 * ncores],
                      cc_out.ap().rearrange("(r p) oc -> p r oc", p=128), reads=[("ccout",)], writes=[("ag",)])
                for oc in range(16):
                    bi = oc % 4
                    qp, qpn = (ps0, "ps0") if oc % 2 == 0 else (ps1, "ps1")
                    P.dma("gpsimd", mk.S("wA" + str(bi)), wb[:, bi * D:(bi + 1) * D], wq1[oc],
                          writes=[("wA", bi)])
                    for kc in range(NT):
                        for nh in range(2):
                            P.emit("tensor", lambda t, _bi=bi, _kc=kc, _nh=nh, _ps=qp:
                                   t.matmul(_ps[:, _nh * 512:_nh * 512 + 512],
                                            wb[:, _bi * D + _kc * 128:_bi * D + _kc * 128 + 128],
                                            ST(_kc)[:, _nh * 512:_nh * 512 + 512],
                                            start=_kc == 0, stop=_kc == NT - 1),
                                   inc=sem.t if (kc == NT - 1 and nh == 1) else None,
                                   reads=[("wA", bi), ("x2", kc)], writes=[("P", qpn, nh)])
                    P.emit("scalar", lambda s, _oc=oc, _ps=qp: s.copy(q2[:, _oc * T:(_oc + 1) * T], _ps[:]),
                           inc=sem.s, reads=[("P", qpn, 0), ("P", qpn, 1)], writes=[("q2", oc)])
                P.emit("vector", lambda v: v.memset(cy_sb[:], 0.0), writes=[("cy",)])
                for r in range(ncores):
                    P.emit("vector", lambda v, _r=r: v.scalar_tensor_tensor(
                               cy_sb[:], ag_sb[:, _r * 16:(_r + 1) * 16], cm_sb[:, _r:_r + 1], cy_sb[:],
                               ALU.mult, ALU.add),
                           inc=sem.v if r == ncores - 1 else None,
                           reads=[("ag",), ("cm",), ("cy",)], writes=[("cy",)])
                for oc in range(16):
                    P.emit("vector", lambda v, _oc=oc: v.tensor_tensor_scan(
                               kvt_t[:], kv_sb[:, _oc * T:(_oc + 1) * T], kv_sb[:, _oc * T:(_oc + 1) * T],
                               cy_sb[:, _oc:_oc + 1], ALU.add, ALU.bypass),
                           inc=sem.v, reads=[("kv", oc), ("cy",)], writes=[("kvt",)])
                    P.emit("vector", lambda v, _oc=oc: v.scalar_tensor_tensor(
                               q2[:, _oc * T:(_oc + 1) * T], q2[:, _oc * T:(_oc + 1) * T], DENOM_L1, kvt_t[:],
                               ALU.mult, ALU.mult),
                           inc=sem.v, reads=[("q2", oc), ("kvt",)], writes=[("q2", oc)])
            P.join([("kv", oc) for oc in range(16)] + [("kb",), ("kvt",)])
            with ExitStack() as phr:
                Rr = lambda *ar: phr.enter_context(nc.sbuf_tensor(*ar, side="right"))
                r_sb = b.y_acc
                sq = Rr("sqC", [128, 2 * T], F32)
                rows = Rr("rowsC", [1, 5 * T], F32)
                for oc in range(16):
                    bi = oc % 4
                    op_, opn = (ps2, "ps2") if oc % 2 == 0 else (ps3, "ps3")
                    P.dma("gpsimd", mk.S("wA" + str(bi)), wb[:, bi * D:(bi + 1) * D], wo1[oc], writes=[("wA", bi)])
                    for kc in range(NT):
                        for nh in range(2):
                            P.emit("tensor", lambda t, _bi=bi, _kc=kc, _nh=nh, _ps=op_:
                                   t.matmul(_ps[:, _nh * 512:_nh * 512 + 512],
                                            wb[:, _bi * D + _kc * 128:_bi * D + _kc * 128 + 128],
                                            q2[:, _kc * T + _nh * 512:_kc * T + _nh * 512 + 512],
                                            start=_kc == 0, stop=_kc == NT - 1),
                                   inc=sem.t if (kc == NT - 1 and nh == 1) else None,
                                   reads=[("wA", bi), ("q2", kc)], writes=[("P", opn, nh)])
                    P.emit("vector", lambda v, _oc=oc, _ps=op_: v.tensor_tensor(r_sb[:, _oc * T:(_oc + 1) * T],
                               ST(_oc), _ps[:], ALU.add),
                           inc=sem.v, reads=[("x2", oc), ("P", opn, 0), ("P", opn, 1)], writes=[("rC", oc)])
                r_aps = [r_sb[:, kc * T:(kc + 1) * T] for kc in range(NT)]
                kn = dict(sq="sqC", rows="rwC", s1=("P", "ps0"), s2=("P", "ps1"),
                          b1=("P", "ps2"), b2=("P", "ps3"))
                mk.emit_ln(r_aps, [("rC", kc) for kc in range(NT)],
                           [ST(kc) for kc in range(NT)], [("x3", kc) for kc in range(NT)],
                           sq, rows, ps0[0:1, :], ps1[0:1, :], ps2, ps3, kn)
        P.join([("q2", oc) for oc in range(16)] + [("wA", 0), ("wA", 1)] +
               [("rC", kc) for kc in range(NT)] + [("sqC", 0), ("sqC", 1), ("rwC",)])

        gpk1 = mk.din("l1_gpk", [22, 128, 2048], U16)
        dpk1 = mk.din("l1_dpk", [11, 128, 2048], U16)
        gsc1 = mk.din("l1_gsc", [22, 4, 4, 512], FP16)
        dsc1 = mk.din("l1_dsc", [11, 4, 2048], FP16)
        x3d = nc.dram_tensor("x3d", [D, T], BF16, kind="Internal")
        for kc in range(NT):
            P.dma("sync", mk.S("bw"), x3d.ap().rearrange("(kc p) t -> kc p t", p=128)[kc],
                  ST(kc), reads=[("x3", kc)], writes=[("x3d", kc)])
        y = b.y_acc
        with ExitStack() as phD:
            RD = lambda *ar: phD.enter_context(nc.sbuf_tensor(*ar, side="right"))
            xj = RD("xj", [128, 16 * T], BF16)
            x3r = x3d.ap().rearrange("(a two) t -> two a t", two=2)
            for jp in range(4):
                par = 1 if jp in (2, 3) else 0
                hi = 1 if jp in (1, 3) else 0
                for tt in range(4):
                    a0 = hi * 512 + tt * 128
                    P.dma("sync", mk.S("xj%d" % (jp * 4 + tt)), xj[:, (jp * 4 + tt) * T:(jp * 4 + tt + 1) * T],
                          x3r[par, a0:a0 + 128, :],
                          reads=[("x3d", kc) for kc in range(NT)], writes=[("xj", jp, tt)])
            XJ = lambda jp, tt: xj[:, (jp * 4 + tt) * T:(jp * 4 + tt + 1) * T]
            with ExitStack() as ph:
                Rr = lambda *ar: ph.enter_context(nc.sbuf_tensor(*ar, side="right"))
                wdec = Rr("wdecD", [128, 16384], BF16)
                pk0 = Rr("pk0D", [128, 2048], U16)
                pk1 = Rr("pk1D", [128, 2048], U16)
                sc0 = Rr("sc0D", [128, 2048], FP16)
                sc1 = Rr("sc1D", [128, 2048], FP16)
                inter = Rr("interD", [128, 8 * T], BF16)
                tmpa = Rr("tmpaD", [128, 1024], I32)
                tmpb = Rr("tmpbD", [128, 1024], I32)
                gsb = Rr("gsbD", [128, T], BF16)
                y_keys = [("yD", hc) for hc in range(NT)]
                for ci, (cs, nt) in enumerate(L1_CHUNKS):
                    for tt in range(nt):
                        for half, span in ((0, cs + tt), (1, 11 + cs + tt)):
                            pkb, pkk = (pk0, ("pkD", 0)) if half == 0 else (pk1, ("pkD", 1))
                            scb, sck = (sc0, ("scD", 0)) if half == 0 else (sc1, ("scD", 1))
                            P.dma("sync", mk.S("pkD%d" % half), pkb[:], gpk1[span], writes=[pkk])
                            for tin in range(4):
                                P.dma("sync", mk.S("scD%d%d" % (half, tin)), scb[:, tin * 512:(tin + 1) * 512],
                                      gsc1[span, tin].unsqueeze(1).broadcast_to((4, 32, 512)),
                                      writes=[sck + (tin,)])
                            for jin in range(4):
                                for tin in range(4):
                                    ki = jin * 4 + tin
                                    dst = wdec[:, (half * 16 + ki) * 512:(half * 16 + ki + 1) * 512]
                                    mk.emit_decode(pkb[:, tin * 512:(tin + 1) * 512], scb[:, tin * 512:(tin + 1) * 512],
                                                   dst, jin, tmpa, tmpb,
                                                   [pkk], [sck + (tin,)], [("w", half, ki)], ("t",))
                        for jo in range(4):
                            for half in range(2):
                                if jo % 2 == 0:
                                    psx, psk = (ps0, ("P", "ps0")) if half == 0 else (ps1, ("P", "ps1"))
                                else:
                                    psx, psk = (ps2, ("P", "ps2")) if half == 0 else (ps3, ("P", "ps3"))
                                for jin in range(4):
                                    for tin in range(4):
                                        ki = jin * 4 + tin
                                        lhs = wdec[:, (half * 16 + ki) * 512:(half * 16 + ki + 1) * 512].rearrange(
                                            "p (a four) -> p a four", four=4)[:, :, jo]
                                        for nh in range(2):
                                            P.emit("tensor", lambda t, _l=lhs, _jin=jin, _tin=tin, _nh=nh, _ps=psx,
                                                   _f=ki == 0, _la=ki == 15:
                                                   t.matmul(_ps[:, _nh * 512:_nh * 512 + 512], _l,
                                                            XJ(_jin, _tin)[:, _nh * 512:_nh * 512 + 512],
                                                            start=_f, stop=_la),
                                                   inc=sem.t if (ki == 15 and nh == 1) else None,
                                                   reads=[("w", half, ki), ("xj", jin, tin)], writes=[psk + (nh,)])
                            it = inter[:, (tt * 4 + jo) * T:(tt * 4 + jo + 1) * T]
                            pg_, pu_ = (ps0, ps1) if jo % 2 == 0 else (ps2, ps3)
                            pgn_, pun_ = ("ps0", "ps1") if jo % 2 == 0 else ("ps2", "ps3")
                            P.emit("scalar", lambda s, _pg=pg_: s.activation(gsb[:], _pg[:], AF.Silu, scale=float(2 ** 14)),
                                   inc=sem.s, reads=[("P", pgn_, 0), ("P", pgn_, 1)], writes=[("gsb",)])
                            P.emit("vector", lambda v, _it=it, _pu=pu_: v.scalar_tensor_tensor(_it, gsb[:], float(2 ** 28), _pu[:], ALU.mult, ALU.mult),
                                   inc=sem.v, reads=[("gsb",), ("P", pun_, 0), ("P", pun_, 1)],
                                   writes=[("intD", tt * 4 + jo)])
                    for tt in range(nt):
                        pkb, pkk = (pk0, ("pkD", 0)) if tt % 2 == 0 else (pk1, ("pkD", 1))
                        scb, sck = (sc0, ("scD", 0)) if tt % 2 == 0 else (sc1, ("scD", 1))
                        P.dma("sync", mk.S("pkD%d" % (tt % 2)), pkb[:], dpk1[cs + tt], writes=[pkk])
                        P.dma("sync", mk.S("scDd%d" % (tt % 2)), scb[:],
                              dsc1[cs + tt].unsqueeze(1).broadcast_to((4, 32, 2048)),
                              writes=[sck + (0,), sck + (1,), sck + (2,), sck + (3,)])
                        for jd in range(4):
                            pl = tt * 4 + jd
                            dst = wdec[:, pl * 2048:(pl + 1) * 2048]
                            alias = ([("w", 0, 4 * pl + q) for q in range(4)] if pl < 4 else
                                     [("w", 1, 4 * (pl - 4) + q) for q in range(4)])
                            mk.emit_decode(pkb[:], scb[:], dst, jd, tmpa, tmpb,
                                           [pkk], [sck + (q,) for q in range(4)], [("dw", pl)] + alias, ("t",))
                    for hc in range(NT):
                        hoff = 256 * (hc % 8)
                        hpar = 1 if hc >= 8 else 0
                        np_ = nt * 4
                        pd_, pdn_ = (ps2, "ps2") if hc % 2 == 0 else (ps3, "ps3")
                        for pl in range(np_):
                            lhs = wdec[:, pl * 2048 + hoff:pl * 2048 + hoff + 256].rearrange(
                                "p (a two) -> p a two", two=2)[:, :, hpar]
                            alias_r = ([("w", 0, 4 * pl + q) for q in range(4)] if pl < 4 else
                                       [("w", 1, 4 * (pl - 4) + q) for q in range(4)])
                            for nh in range(2):
                                P.emit("tensor", lambda t, _l=lhs, _pl=pl, _nh=nh, _ps=pd_, _f=pl == 0, _la=pl == np_ - 1:
                                       t.matmul(_ps[:, _nh * 512:_nh * 512 + 512], _l,
                                                inter[:, _pl * T + _nh * 512:_pl * T + _nh * 512 + 512],
                                                start=_f, stop=_la),
                                       inc=sem.t if (pl == np_ - 1 and nh == 1) else None,
                                       reads=[("dw", pl), ("intD", pl)] + alias_r, writes=[("P", pdn_, nh)])
                        if ci == 0:
                            P.emit("scalar", lambda s, _hc=hc, _ps=pd_: s.copy(y[:, _hc * T:(_hc + 1) * T], _ps[:]),
                                   inc=sem.s, reads=[("P", pdn_, 0), ("P", pdn_, 1)], writes=[("yD", hc)])
                        else:
                            P.emit("vector", lambda v, _hc=hc, _ps=pd_: v.tensor_tensor(y[:, _hc * T:(_hc + 1) * T],
                                   y[:, _hc * T:(_hc + 1) * T], _ps[:], ALU.add),
                                   inc=sem.v, reads=[("yD", hc), ("P", pdn_, 0), ("P", pdn_, 1)], writes=[("yD", hc)])
            P.join([("w", 0, k) for k in range(16)] + [("w", 1, k) for k in range(16)] +
                   [("dw", pl) for pl in range(8)] + [("intD", q) for q in range(8)] +
                   [("pkD", 0), ("pkD", 1)] +
                   [("scD", hq, q) for hq in range(2) for q in range(4)] +
                   [("gsb",), ("t", "a"), ("t", "b")] +
                   [("xj", jp, tt) for jp in range(4) for tt in range(4)])
            with ExitStack() as ph2:
                Rr = lambda *ar: ph2.enter_context(nc.sbuf_tensor(*ar, side="right"))
                sq = Rr("sqD", [128, 2 * T], F32)
                rows = Rr("rowsD", [1, 5 * T], F32)
                RW = [("rwD",)]
                for hc in range(NT):
                    buf = sq[:, (hc % 2) * T:(hc % 2) * T + T]
                    P.emit("scalar", lambda s, _b=buf, _hc=hc: s.activation(_b, y[:, _hc * T:(_hc + 1) * T], AF.Square),
                           inc=sem.s, reads=[("yD", hc)], writes=[("sqD", hc % 2)])
                    for nh in range(2):
                        P.emit("tensor", lambda t, _b=buf, _nh=nh, _f=hc == 0, _la=hc == NT - 1:
                               t.matmul(ps0[0:1, _nh * 512:_nh * 512 + 512], mk.ones_col,
                                        _b[:, _nh * 512:_nh * 512 + 512], start=_f, stop=_la),
                               inc=sem.t if (hc == NT - 1 and nh == 1) else None,
                               reads=[("sqD", hc % 2), ("cst",)], writes=[("P", "ps0", nh)])
                msq = rows[0:1, 0:T]
                P.emit("vector", lambda v: v.tensor_scalar(msq, ps0[0:1, :], 1.0 / D, None, ALU.mult),
                       inc=sem.v, reads=[("P", "ps0", 0), ("P", "ps0", 1)], writes=RW)
                P.emit("scalar", lambda s: s.activation(msq, msq, AF.Ln, bias=mk.eps_rms),
                       reads=RW + [("cst",)], writes=RW)
                P.emit("scalar", lambda s: s.activation(msq, msq, AF.Exp, scale=-0.5), inc=sem.s,
                       reads=RW, writes=RW)
                for nh in range(2):
                    P.emit("tensor", lambda t, _nh=nh: t.matmul(ps1[:, _nh * 512:_nh * 512 + 512],
                           mk.orow[0:1, 0:128], msq[0:1, _nh * 512:_nh * 512 + 512], start=True, stop=True),
                           inc=sem.t if nh == 1 else None,
                           reads=RW + [("orow",)], writes=[("P", "ps1", nh)])
                for kc in range(NT):
                    P.emit("vector", lambda v, _kc=kc: v.tensor_tensor(y[:, _kc * T:(_kc + 1) * T],
                           y[:, _kc * T:(_kc + 1) * T], ps1[:], ALU.mult),
                           reads=[("yD", kc), ("P", "ps1", 0), ("P", "ps1", 1)], writes=[("yD", kc)])
                    P.emit("vector", lambda v, _kc=kc: v.tensor_tensor(y[:, _kc * T:(_kc + 1) * T],
                           y[:, _kc * T:(_kc + 1) * T], ST(_kc), ALU.add),
                           inc=sem.v, reads=[("yD", kc), ("x3", kc)], writes=[("yD", kc)])
                r_aps = [y[:, kc * T:(kc + 1) * T] for kc in range(NT)]
                kn = dict(sq="sqD2", rows="rwD2", s1=("P", "ps0"), s2=("P", "ps2"),
                          b1=("P", "ps1"), b2=("P", "ps3"))
                mk.emit_ln(r_aps, [("yD", kc) for kc in range(NT)],
                           [ST(kc) for kc in range(NT)], [("x4", kc) for kc in range(NT)],
                           sq, rows, ps0[0:1, :], ps2[0:1, :], ps1, ps3, kn)
                tok_out = None
                for kc in range(NT):
                    tok_out = mk.P.dma("gpsimd", mk.S("outF"),
                                       out_ap.rearrange("(kc p) t -> kc p t", p=128)[kc],
                                       ST(kc), reads=[("x4", kc)])
                mk.P.raw("gpsimd", lambda e, _t=tok_out: e.wait_ge(_t.sem.h, _t.tgt))
        P.finalize(es.enter_context(nc.Block()), es)
    return nc




from concourse.bass_utils import run_bass_kernel_spmd


def kernel(**inputs):
    g, percore = host_prep(inputs)
    nc = bass.Bass("TRN2", target_bir_lowering=False)
    build_fused(nc)
    in_maps = []
    for c in range(NCORES):
        m = dict(percore[c])
        m.update(consts=g["consts"], onesrow=g["onesrow"],
                 l0_wq=g["l0_wq"], l0_wk_nat=g["l0_wk_nat"], l0_wv_nat=g["l0_wv_nat"],
                 l0_mem=g["l0_mem"], l0_memn=g["l0_memn"], l0_memw=g["l0_memw"],
                 l0_gpk=g["l0_gpk"], l0_upk=g["l0_upk"], l0_dpk=g["l0_dpk"],
                 l0_gsc=g["l0_gsc"], l0_usc=g["l0_usc"], l0_dsc=g["l0_dsc"],
                 l1_wq=g["l1_wq"], l1_wk=g["l1_wk"], l1_wv=g["l1_wv"], l1_wo=g["l1_wo"],
                 l1_gpk=g["l1_gpk"], l1_dpk=g["l1_dpk"], l1_gsc=g["l1_gsc"], l1_dsc=g["l1_dsc"])
        in_maps.append(m)
    res = run_bass_kernel_spmd(nc, in_maps, core_ids=list(range(NCORES)))
    return host_finish([np.asarray(res.results[c]["x4_out"], np.float32) for c in range(NCORES)])


# version 09
# speedup vs baseline: 1.1346x; 1.0116x over previous
import numpy as np
from contextlib import ExitStack

import concourse.bass as bass
import concourse.mybir as mybir

F32 = mybir.dt.float32
BF16 = mybir.dt.bfloat16
I32 = mybir.dt.int32
U16 = mybir.dt.uint16
FP16 = mybir.dt.float16
AF = mybir.ActivationFunctionType
ALU = mybir.AluOpType
AX = mybir.AxisListType

D = 2048
NT = 16
T = 1024
S = 4096
NCORES = 8

PI = np.concatenate([np.arange(0, D, 2), np.arange(1, D, 2)])



def prep_w(A):
    A = np.ascontiguousarray(A)
    K, O = A.shape
    KT = K // 128
    out = np.empty((O // 128, 128, K), A.dtype)
    for oc in range(O // 128):
        out[oc] = A[:, oc * 128:(oc + 1) * 128].reshape(KT, 128, 128).transpose(1, 0, 2).reshape(128, K)
    return out


def pack_u16(arr_i32, dense):
    v = np.ascontiguousarray(arr_i32).view(np.uint16)
    if not dense:
        v = np.ascontiguousarray(v[:, 0::2])
    return v


def prep_packed_pm(u16_T, rows_per_chunk):
    Wr, O = u16_T.shape
    n = Wr // rows_per_chunk
    sub = rows_per_chunk // 128
    out = np.empty((n, 128, sub * O), np.uint16)
    for c in range(n):
        blk = u16_T[c * rows_per_chunk:(c + 1) * rows_per_chunk]
        out[c] = blk.reshape(sub, 128, O).transpose(1, 0, 2).reshape(128, sub * O)
    return out


def host_prep(inp):
    g = {}
    x = np.asarray(inp["L0_attn_x"], np.float32)
    xT = np.ascontiguousarray(x.reshape(2 * S, D).T)
    xT_pi = np.ascontiguousarray(xT[PI])

    def pm_tiles(A):
        K, O = A.shape
        return np.ascontiguousarray(A.reshape(K // 128, 128, O).transpose(1, 0, 2).reshape(128, (K // 128) * O))

    g["l0_wq"] = prep_w(np.asarray(inp["L0_attn_Wq"], np.float32).T[PI])
    g["l0_wk_nat"] = pm_tiles(np.asarray(inp["L0_attn_Wk"], np.float32).T[PI])
    g["l0_wv_nat"] = pm_tiles(np.asarray(inp["L0_attn_Wv"], np.float32).T[PI])
    g["l0_mem"] = np.ascontiguousarray(np.asarray(inp["L0_attn_memory"], np.float32))
    g["l0_memn"] = np.ascontiguousarray(np.asarray(inp["L0_attn_memory_norm"], np.float32).reshape(4, 128).T)
    g["l0_memw"] = np.asarray(inp["L0_attn_memory_weight"], np.float32).reshape(1, 1)

    gte = pack_u16(np.asarray(inp["L0_ffn_gate_W_packed"]), dense=False).T
    upe = pack_u16(np.asarray(inp["L0_ffn_up_W_packed"]), dense=False).T
    dne = pack_u16(np.asarray(inp["L0_ffn_down_W_packed"]), dense=False).T
    def tile_gu(A):
        out = np.empty((16, 128, 8192), np.uint16)
        for ncb in range(16):
            blk = A[:, ncb * 1024:(ncb + 1) * 1024]
            out[ncb] = blk.reshape(8, 128, 1024).transpose(1, 0, 2).reshape(128, 8192)
        return out

    def tile_dn(A):
        out = np.empty((16, 128, 8192), np.uint16)
        for ncb in range(16):
            blk = A[ncb * 512:(ncb + 1) * 512]
            out[ncb] = blk.reshape(4, 128, 2048).transpose(1, 0, 2).reshape(128, 8192)
        return out

    g["l0_gpk"] = tile_gu(gte)
    g["l0_upk"] = tile_gu(upe)
    g["l0_dpk"] = tile_dn(dne)
    gscT = np.asarray(inp["L0_ffn_gate_scales"], np.float32).T
    uscT = np.asarray(inp["L0_ffn_up_scales"], np.float32).T
    dscT = np.asarray(inp["L0_ffn_down_scales"], np.float32).T

    def tile_sc_gu(A):
        out = np.empty((16, 2, 8, 4096), np.float32)
        for ncb in range(16):
            for h in range(2):
                for r in range(8):
                    for wc in range(8):
                        out[ncb, h, r, wc * 512:(wc + 1) * 512] = A[8 * wc + r, ncb * 1024 + h * 512:ncb * 1024 + h * 512 + 512]
        return np.ascontiguousarray(out)

    def tile_sc_d(A):
        out = np.empty((16, 2, 8, 4096), np.float32)
        for ncb in range(16):
            for dh in range(2):
                for r in range(8):
                    for ws in range(2):
                        out[ncb, dh, r, ws * 2048:(ws + 1) * 2048] = A[32 * ncb + 16 * dh + 8 * ws + r]
        return np.ascontiguousarray(out)

    g["l0_gsc"] = tile_sc_gu(gscT)
    g["l0_usc"] = tile_sc_gu(uscT)
    g["l0_dsc"] = tile_sc_d(dscT)

    g["l1_wq"] = prep_w(np.asarray(inp["L1_attn_Wq"], np.float32).T[PI])
    g["l1_wk"] = prep_w(np.asarray(inp["L1_attn_Wk"], np.float32).T[PI])
    g["l1_wv"] = prep_w(np.asarray(inp["L1_attn_Wv"], np.float32).T[PI])
    g["l1_wo"] = prep_w(np.ascontiguousarray(np.asarray(inp["L1_attn_Wo"], np.float32)[PI].T))

    gupT = pack_u16(np.asarray(inp["L1_ffn_gate_up_packed"]), dense=True).T
    dnT = pack_u16(np.asarray(inp["L1_ffn_down_packed"]), dense=True).T
    gpk1 = np.empty((22, 128, 2048), np.uint16)
    for s_ in range(22):
        gpk1[s_] = gupT[:, 512 * s_:512 * (s_ + 1)].reshape(4, 128, 512).transpose(1, 0, 2).reshape(128, 2048)
    g["l1_gpk"] = gpk1
    g["l1_dpk"] = np.ascontiguousarray(dnT.reshape(11, 128, 2048))
    gsc1 = np.asarray(inp["L1_ffn_gate_up_scales"])
    gsc_t = np.empty((22, 4, 4, 512), np.float16)
    for s_ in range(22):
        for tin in range(4):
            for r in range(4):
                gsc_t[s_, tin, r] = gsc1[4 * tin + r, 512 * s_:512 * (s_ + 1)]
    g["l1_gsc"] = gsc_t
    g["l1_dsc"] = np.ascontiguousarray(np.asarray(inp["L1_ffn_down_scales"]).reshape(11, 4, 2048))

    consts = np.zeros((128, 8), np.float32)
    consts[:, 0] = 1.0
    consts[:, 1] = -20.0
    consts[:, 2] = 1e-5
    consts[:, 3] = 1e-6
    g["consts"] = consts
    g["onesrow"] = np.ones((1, 128), np.float32)

    percore = []
    for c in range(NCORES):
        b, j = c // 4, c % 4
        pc = dict(
            xT_my=np.ascontiguousarray(xT_pi[:, c * T:(c + 1) * T]),
            xT_b=np.ascontiguousarray(xT_pi[:, b * S:(b + 1) * S]),
        )
        cm = np.zeros((128, 8), np.float32)
        for r in range(b * 4, b * 4 + j):
            cm[:, r] = 1.0
        pc["cmask"] = cm
        percore.append(pc)
    return g, percore


def host_finish(per_core_outs):
    full = np.concatenate(per_core_outs, axis=1)
    nat = np.empty_like(full)
    nat[PI] = full
    return np.ascontiguousarray(nat.T).reshape(2, S, D)



ENGINES = ("sync", "gpsimd", "vector", "scalar", "tensor")


class Sem:
    def __init__(self, handle, name):
        self.h = handle
        self.name = name
        self.n = 0

    def bump(self, k):
        self.n += k
        return self.n


class Tok:
    __slots__ = ("sem", "tgt", "_op")

    def __init__(self, sem, tgt):
        self.sem = sem
        self.tgt = tgt
        self._op = None


class OpRec:
    __slots__ = ("eng", "fn", "exp_sem", "exp_by", "auto", "auto_val", "prewaits", "idx", "_tok", "raw_fn")

    def __init__(self, eng, fn, exp_sem, exp_by, idx):
        self.eng = eng
        self.fn = fn
        self.exp_sem = exp_sem
        self.exp_by = exp_by
        self.auto = False
        self.auto_val = None
        self.prewaits = []
        self.idx = idx
        self._tok = None
        self.raw_fn = False


class Prog:

    def __init__(self, nc):
        self.nc = nc
        self.ops = {e: [] for e in ENGINES}
        self._waited = {e: {} for e in ENGINES}
        self._auto_hw = {e: {} for e in ENGINES}
        self._store = {}
        self._auto_sems = None

    def _edge(self, producer, consumer):
        if producer is consumer:
            return
        if isinstance(consumer, Tok):
            consumer = consumer._op
        if isinstance(producer, Tok):
            if getattr(producer, "_op", None) is consumer:
                return
            consumer.prewaits.append(("tok", producer))
            return
        if producer.eng == "tensor" and consumer.eng == "tensor":
            return
        producer.auto = True
        consumer.prewaits.append(("auto", producer))

    def _note(self, op_or_tok, reads, writes):
        for k in reads:
            d = self._store.get(k)
            if d is not None and d[0] is not None:
                self._edge(d[0], op_or_tok)
            if d is None:
                d = self._store[k] = [None, []]
            d[1].append(op_or_tok)
        for k in writes:
            d = self._store.get(k)
            if d is not None:
                if d[0] is not None:
                    self._edge(d[0], op_or_tok)
                for r in d[1]:
                    if r is not op_or_tok:
                        self._edge(r, op_or_tok)
            self._store[k] = [op_or_tok, []]

    def emit(self, eng, fn, inc=None, inc_by=1, reads=(), writes=()):
        op = OpRec(eng, fn, inc, inc_by, len(self.ops[eng]))
        tok = None
        if inc is not None:
            tgt = inc.bump(inc_by)
            tok = Tok(inc, tgt)
            tok._op = op
        op._tok = tok
        if reads or writes:
            self._note(tok if tok is not None else op, reads, writes)
        self.ops[eng].append(op)
        return tok if inc is not None else op

    def dma(self, eng, sem, out, in_, reads=(), writes=()):
        tgt = sem.bump(16)
        tok = Tok(sem, tgt)
        op = OpRec(eng, lambda e, _o=out, _i=in_: e.dma_start(out=_o, in_=_i), sem, 16, len(self.ops[eng]))
        op._tok = tok
        tok._op = op
        if reads or writes:
            for k in reads:
                d = self._store.get(k)
                if d is not None and d[0] is not None:
                    self._edge(d[0], op)
                if d is None:
                    d = self._store[k] = [None, []]
                d[1].append(tok)
            for k in writes:
                d = self._store.get(k)
                if d is not None:
                    if d[0] is not None:
                        self._edge(d[0], op)
                    for r in d[1]:
                        self._edge(r, op)
                self._store[k] = [tok, []]
        self.ops[eng].append(op)
        return tok

    def join(self, keys, engines=ENGINES):
        for eng in engines:
            marker = OpRec(eng, None, None, 0, len(self.ops[eng]))
            for k in keys:
                d = self._store.get(k)
                if d is None:
                    continue
                if d[0] is not None:
                    self._edge(d[0], marker)
                for r in d[1]:
                    self._edge(r, marker)
            if marker.prewaits:
                self.ops[eng].append(marker)

    def wait(self, eng, tok):
        if tok is None:
            return
        if isinstance(tok, OpRec):
            tok.auto = True
            marker = OpRec(eng, None, None, 0, len(self.ops[eng]))
            marker.prewaits.append(("auto", tok))
            self.ops[eng].append(marker)
            return
        w = self._waited[eng]
        if w.get(tok.sem.name, -1) >= tok.tgt:
            return
        w[tok.sem.name] = tok.tgt
        marker = OpRec(eng, None, None, 0, len(self.ops[eng]))
        marker.prewaits.append(("tok", tok))
        self.ops[eng].append(marker)

    def raw(self, eng, fn):
        op = OpRec(eng, fn, None, 0, len(self.ops[eng]))
        op.raw_fn = True
        self.ops[eng].append(op)

    def finalize(self, block, es):
        auto_sems = {}
        for e in ENGINES:
            if any(getattr(o, "auto", False) for o in self.ops[e]):
                auto_sems[e] = Sem(es.enter_context(self.nc.semaphore("au_" + e)), "au_" + e)
        for e, lst in self.ops.items():
            v = 0
            for o in lst:
                if getattr(o, "auto", False):
                    v += 1
                    o.auto_val = v
        waited_tok = {e: {} for e in ENGINES}
        waited_auto = {e: {} for e in ENGINES}

        for e in ENGINES:
            lst = self.ops[e]
            if not lst:
                continue

            def body(eng_obj, _lst=tuple(lst), _e=e):
                for o in _lst:
                    tok_max = {}
                    auto_max = {}
                    for kind, p in o.prewaits:
                        if kind == "tok":
                            if tok_max.get(p.sem.name, (None, -1))[1] < p.tgt:
                                tok_max[p.sem.name] = (p.sem.h, p.tgt)
                        else:
                            if auto_max.get(p.eng, (None, -1))[1] < p.auto_val:
                                auto_max[p.eng] = (auto_sems[p.eng].h, p.auto_val)
                    for name, (h, tgt) in tok_max.items():
                        wt = waited_tok[_e]
                        if wt.get(name, -1) < tgt:
                            wt[name] = tgt
                            eng_obj.wait_ge(h, tgt)
                    for peng, (h, tgt) in auto_max.items():
                        wa = waited_auto[_e]
                        if wa.get(peng, -1) < tgt:
                            wa[peng] = tgt
                            eng_obj.wait_ge(h, tgt)
                    if o.fn is None:
                        continue
                    inst = o.fn(eng_obj)
                    if getattr(o, "raw_fn", False):
                        continue
                    if o.exp_sem is not None:
                        inst.then_inc(o.exp_sem.h, o.exp_by)
                    if getattr(o, "auto", False):
                        inst.then_inc(auto_sems[o.eng].h, 1)
            getattr(block, e)(body)



class NS(dict):
    __getattr__ = dict.__getitem__
    __setattr__ = dict.__setitem__


class MK:
    def __init__(self, nc, es, P):
        self.nc = nc
        self.es = es
        self.P = P
        self.sem = NS()
        for name in ("d", "g", "v", "s", "t", "cc"):
            self.sem[name] = Sem(es.enter_context(nc.semaphore("sem_" + name)), name)
        self._dsems = {}
        self.t = NS()
        self.ps = NS()
        self.ap = NS()
        self._ln_ps_free = None
        self._ln_bc_free = None

    def S(self, name):
        if name not in self._dsems:
            self._dsems[name] = Sem(self.es.enter_context(self.nc.semaphore("ds_" + name)), "ds_" + name)
        return self._dsems[name]

    def sbuf(self, name, shape, dtype):
        h = self.es.enter_context(self.nc.sbuf_tensor(name, shape, dtype))
        self.t[name] = h
        return h

    def psum(self, name, shape, dtype=F32):
        h = self.es.enter_context(self.nc.psum_tensor(name, shape, dtype))
        self.ps[name] = h
        return h

    def din(self, name, shape, dtype):
        ap = self.nc.dram_tensor(name, shape, dtype, kind="ExternalInput").ap()
        self.ap[name] = ap
        return ap

    def dout(self, name, shape, dtype):
        ap = self.nc.dram_tensor(name, shape, dtype, kind="ExternalOutput").ap()
        self.ap[name] = ap
        return ap

    def setup_consts(self):
        P, sem = self.P, self.sem
        self.din("consts", [128, 8], F32)
        self.din("onesrow", [1, 128], F32)
        cst = self.sbuf("cst", [128, 8], F32)
        orow = self.sbuf("orow", [1, 128], F32)
        orow_b = self.sbuf("orow_b", [1, 128], BF16)
        P.dma("sync", self.S("c0"), cst[:], self.ap.consts, writes=[("cst",)])
        P.dma("sync", self.S("c1"), orow[:], self.ap.onesrow, writes=[("orow",)])
        P.emit("vector", lambda v: v.tensor_copy(orow_b[:], orow[:]), inc=sem.v,
               reads=[("orow",)], writes=[("orowb",)])
        self.ones_col = cst[:, 0:1]
        self.bias_exp = cst[:, 1:2]
        self.eps_ln = cst[0:1, 2:3]
        self.eps_rms = cst[0:1, 3:4]
        self.orow = orow
        self.orow_b = orow_b

    def emit_ln(self, r_aps, r_keys, out_aps, out_keys, sq, rows, ps_s1, ps_s2, ps_b1, ps_b2,
                kn):
        P, sem = self.P, self.sem
        nT = len(r_aps)
        K = lambda n, *i: (kn[n] + i) if isinstance(kn[n], tuple) else ((kn[n],) + i)
        for i, r in enumerate(r_aps):
            buf = sq[:, (i % 2) * T:(i % 2) * T + T]
            P.emit("scalar", lambda s, _b=buf, _r=r: s.activation(_b, _r, AF.Square), inc=sem.s,
                   reads=[r_keys[i], ("cst",)], writes=[K("sq", i % 2)])
            for nh in range(T // 512):
                st, sp = (i == 0), (i == nT - 1)
                P.emit("tensor", lambda t, _r=r, _nh=nh, _st=st, _sp=sp:
                       t.matmul(ps_s1[0:1, _nh * 512:_nh * 512 + 512], self.ones_col,
                                _r[:, _nh * 512:_nh * 512 + 512], start=_st, stop=_sp),
                       inc=sem.t if (sp and nh == 1) else None,
                       reads=[r_keys[i], ("cst",)], writes=[K("s1", nh)])
                P.emit("tensor", lambda t, _b=buf, _nh=nh, _st=st, _sp=sp:
                       t.matmul(ps_s2[0:1, _nh * 512:_nh * 512 + 512], self.ones_col,
                                _b[:, _nh * 512:_nh * 512 + 512], start=_st, stop=_sp),
                       inc=sem.t if (sp and nh == 1) else None,
                       reads=[K("sq", i % 2), ("cst",)], writes=[K("s2", nh)])
        m = rows[0:1, 0:T]; msq = rows[0:1, T:2 * T]; var = rows[0:1, 2 * T:3 * T]
        rstd = rows[0:1, 3 * T:4 * T]; mr = rows[0:1, 4 * T:5 * T]
        RW = [K("rows")]
        P.emit("vector", lambda v: v.tensor_scalar(m, ps_s1[0:1, :], 1.0 / D, None, ALU.mult),
               reads=[K("s1", 0), K("s1", 1)], writes=RW)
        P.emit("vector", lambda v: v.tensor_scalar(msq, ps_s2[0:1, :], 1.0 / D, None, ALU.mult),
               reads=[K("s2", 0), K("s2", 1)], writes=RW)
        P.emit("vector", lambda v: v.scalar_tensor_tensor(var, m, -1.0, m, ALU.mult, ALU.mult),
               reads=RW, writes=RW)
        P.emit("vector", lambda v: v.tensor_tensor(var, msq, var, ALU.add), inc=sem.v,
               reads=RW, writes=RW)
        P.emit("scalar", lambda s: s.activation(var, var, AF.Ln, bias=self.eps_ln),
               reads=RW + [("cst",)], writes=RW)
        P.emit("scalar", lambda s: s.activation(rstd, var, AF.Exp, scale=-0.5), inc=sem.s,
               reads=RW, writes=RW)
        for nh in range(T // 512):
            P.emit("tensor", lambda t, _nh=nh: t.matmul(ps_b1[:, _nh * 512:_nh * 512 + 512],
                   self.orow[0:1, 0:128], rstd[0:1, _nh * 512:_nh * 512 + 512], start=True, stop=True),
                   reads=RW + [("orow",)], writes=[K("b1", nh)])
            P.emit("tensor", lambda t, _nh=nh: t.matmul(ps_b2[:, _nh * 512:_nh * 512 + 512],
                   self.orow[0:1, 0:128], m[0:1, _nh * 512:_nh * 512 + 512], start=True, stop=True),
                   inc=sem.t if nh == 1 else None,
                   reads=RW + [("orow",)], writes=[K("b2", nh)])
        out_toks = []
        for i, (r, o) in enumerate(zip(r_aps, out_aps)):
            buf = sq[:, (i % 2) * T:(i % 2) * T + T]
            P.emit("vector", lambda v, _r=r, _b=buf: v.tensor_tensor(_b, _r, ps_b2[:], ALU.subtract),
                   reads=[r_keys[i], K("b2", 0), K("b2", 1)], writes=[K("sq", i % 2)])
            out_toks.append(P.emit("vector", lambda v, _b=buf, _o=o: v.tensor_tensor(_o, _b, ps_b1[:], ALU.mult),
                                   inc=sem.v,
                                   reads=[K("sq", i % 2), K("b1", 0), K("b1", 1)], writes=[out_keys[i]]))
        return out_toks

    def emit_l0_attn(self, a, upto='full'):
        P, sem = self.P, self.sem
        xT_my = self.din("xT_my", [D, T], F32)
        xT_b = self.din("xT_b", [D, S], F32)
        wq = self.din("l0_wq", [16, 128, D], F32)
        wk = self.din("l0_wk_nat", [128, 16 * 512], F32)
        wv = self.din("l0_wv_nat", [128, 16 * 512], F32)
        mem = self.din("l0_mem", [512, 512], F32)
        memn = self.din("l0_memn", [128, 4], F32)
        memw = self.din("l0_memw", [1, 1], F32)
        self.ap_xT_my = xT_my

        q_sb, k_sb, v_tm, attn = a.q_sb, a.k_sb, a.v_tm, a.attn
        wkv_sb, xsl, wqb = a.wkv_sb, a.xsl, a.wqb
        mw_sb = a.mw_sb
        mem_sb, memn_sb = a.mem_sb, a.memn_sb
        ps_sc, ps_a, ps_bc = a.ps_sc, a.ps_a, a.ps_bc

        P.dma("gpsimd", self.S("wk"), wkv_sb[:, 0:8192], wk, writes=[("wk",)])
        P.dma("gpsimd", self.S("wv"), wkv_sb[:, 8192:], wv, writes=[("wv",)])
        P.dma("gpsimd", self.S("mem"), mem_sb[:], mem.rearrange("(ic p) j -> p ic j", p=128), writes=[("mem",)])
        P.dma("gpsimd", self.S("memn"), memn_sb[:], memn, writes=[("memn",)])
        P.dma("sync", self.S("mw"), mw_sb[0:1, 0:1], memw, writes=[("mw",)])

        for tkc in range(8):
            bi = tkc % 2
            sl = xsl[:, bi * 8192:(bi + 1) * 8192]
            P.dma("gpsimd", self.S("x%d" % bi), sl,
                  xT_b.rearrange("(kc p) s -> p kc s", p=128)[:, :, tkc * 512:(tkc + 1) * 512],
                  writes=[("xsl", bi)])
            for oc in range(4):
                kp, kpn = (a.ps_sc[0], "ps0") if oc % 2 == 0 else (a.ps_a, "ps2")
                for kc in range(NT):
                    P.emit("tensor", lambda t, _oc=oc, _kc=kc, _sl=sl, _ps=kp:
                           t.matmul(_ps[:, 0:512],
                                    wkv_sb[:, _kc * 512 + _oc * 128:_kc * 512 + _oc * 128 + 128],
                                    _sl[:, _kc * 512:_kc * 512 + 512],
                                    start=_kc == 0, stop=_kc == NT - 1),
                           inc=sem.t if kc == NT - 1 else None,
                           reads=[("wk",), ("xsl", bi)], writes=[("P", kpn, 0)])
                P.emit("scalar", lambda s, _oc=oc, _tkc=tkc, _ps=kp:
                       s.copy(k_sb[:, _oc * S + _tkc * 512:_oc * S + _tkc * 512 + 512], _ps[:, 0:512]),
                       inc=sem.s, reads=[("P", kpn, 0)], writes=[("k",)])
            for tsub in range(4):
                vt = tkc * 4 + tsub
                vp, vpn = (a.ps_sc[1], "ps1") if tsub % 2 == 0 else (a.ps_bc, "ps3")
                for kc in range(NT):
                    P.emit("tensor", lambda t, _kc=kc, _tsub=tsub, _sl=sl, _ps=vp:
                           t.matmul(_ps[:, 0:512],
                                    _sl[:, _kc * 512 + _tsub * 128:_kc * 512 + _tsub * 128 + 128],
                                    wkv_sb[:, 8192 + _kc * 512:8192 + (_kc + 1) * 512],
                                    start=_kc == 0, stop=_kc == NT - 1),
                           inc=sem.t if kc == NT - 1 else None,
                           reads=[("wv",), ("xsl", bi)], writes=[("P", vpn, 0)])
                src = vp[:, 0:512].rearrange("p (h dd par) -> p h par dd", h=8, dd=32)
                dst = v_tm[:, vt * 520:(vt + 1) * 520].rearrange("p (h x) -> p h x", h=8)[:, :, 0:64].rearrange(
                    "p h (par dd) -> p h par dd", par=2)
                P.emit("vector", lambda v, _s=src, _d=dst: v.tensor_copy(_d, _s), inc=sem.v,
                       reads=[("P", vpn, 0)], writes=[("v",)])
                ones_ap = v_tm[:, vt * 520:(vt + 1) * 520].rearrange("p (h x) -> p h x", h=8)[:, :, 64:65]
                P.emit("vector", lambda v, _a=ones_ap: v.memset(_a, 1.0), writes=[("v",)])

        if upto in ('a1', 'a1v'):
            self._phase_tail_toks = [Tok(sem.s, sem.s.n), Tok(sem.v, sem.v.n), Tok(sem.t, sem.t.n)]
            return a
        tok_xmy = P.dma("gpsimd", self.S("x0"), xsl[:, 0:16 * T].rearrange("p (kc t) -> p kc t", kc=16),
                        xT_my.rearrange("(kc p) t -> p kc t", p=128),
                        writes=[("xsl", 0), ("xsl", 1)])
        a.tok_xmy = tok_xmy
        q_evac = []
        for oc in range(16):
            bi = oc % 2
            psn = "ps0" if bi == 0 else "ps1"
            psx = ps_sc[bi]
            P.dma("gpsimd", self.S("w%d" % bi), wqb[:, bi * D:(bi + 1) * D], wq[oc], writes=[("wqb", bi)])
            for kc in range(NT):
                for nh in range(2):
                    P.emit("tensor", lambda t, _bi=bi, _kc=kc, _nh=nh, _ps=psx:
                           t.matmul(_ps[:, _nh * 512:_nh * 512 + 512],
                                    wqb[:, _bi * D + _kc * 128:_bi * D + _kc * 128 + 128],
                                    xsl[:, _kc * T + _nh * 512:_kc * T + _nh * 512 + 512],
                                    start=_kc == 0, stop=_kc == NT - 1),
                           inc=sem.t if (kc == NT - 1 and nh == 1) else None,
                           reads=[("wqb", bi), ("xsl", 0), ("xsl", 1)], writes=[("P", psn, nh)])
            tok_ev = P.emit("scalar", lambda s, _oc=oc, _ps=psx: s.copy(q_sb[:, _oc * T:(_oc + 1) * T], _ps[:]),
                            inc=sem.s, reads=[("P", psn, 0), ("P", psn, 1)], writes=[("q",)])
            q_evac.append(tok_ev)

        MW = [("mw",)]
        P.emit("scalar", lambda s: s.activation(mw_sb[0:1, 1:2], mw_sb[0:1, 0:1], AF.Exp, scale=-1.0),
               inc=sem.s, reads=MW, writes=MW)
        P.emit("vector", lambda v: v.tensor_scalar(mw_sb[0:1, 2:3], mw_sb[0:1, 1:2], 1.0, None, ALU.add),
               reads=MW, writes=MW)
        P.emit("vector", lambda v: v.reciprocal(mw_sb[0:1, 2:3], mw_sb[0:1, 2:3]), reads=MW, writes=MW)
        tok_w = P.emit("vector", lambda v: v.tensor_tensor(mw_sb[0:1, 3:4], mw_sb[0:1, 1:2], mw_sb[0:1, 2:3], ALU.mult),
                       inc=sem.v, reads=MW, writes=MW)
        a.tok_w = tok_w
        a.q_evac = q_evac
        return a

    def emit_l0_scores(self, a, nqh=32, sub='full'):
        P, sem = self.P, self.sem
        q_sb, k_sb, v_tm, attn = a.q_sb, a.k_sb, a.v_tm, a.attn
        pbuf, tmp64, rowsA, rbf, mw_sb = a.pbuf, a.tmp64, a.rowsA, a.rbf, a.mw_sb
        ps_sc, ps_a, ps_bc = a.ps_sc, a.ps_a, a.ps_bc

        attn_toks = [None] * 16
        items = [(qh, tkc) for qh in range(nqh) for tkc in range(32)]
        PSN = ["ps0", "ps1"]

        def emit_scores(idx):
            qh, tkc = items[idx]
            h = qh % 8
            koc, koff = h // 2, (h % 2) * 64
            qoc, qoff = qh // 2, (qh % 2) * 64
            bi = idx % 2
            pi3 = idx % 3
            psn = PSN[bi]
            for nh in range(2):
                P.emit("tensor", lambda t, _bi=bi, _koc=koc, _koff=koff, _qoc=qoc, _qoff=qoff, _tkc=tkc, _nh=nh:
                       t.matmul(ps_sc[_bi][:, _nh * 512:_nh * 512 + 512],
                                k_sb[_koff:_koff + 64, _koc * S + _tkc * 128:_koc * S + (_tkc + 1) * 128],
                                q_sb[_qoff:_qoff + 64, _qoc * T + _nh * 512:_qoc * T + _nh * 512 + 512],
                                start=True, stop=True),
                       inc=sem.t if nh == 1 else None,
                       reads=[("k",), ("q",)], writes=[("P", psn, nh)])
            P.emit("scalar", lambda s, _bi=bi, _pi=pi3:
                   s.activation(pbuf[:, _pi * T:(_pi + 1) * T], ps_sc[_bi][:], AF.Exp,
                                bias=self.bias_exp, scale=0.125), inc=sem.s,
                   reads=[("P", psn, 0), ("P", psn, 1), ("cst",)], writes=[("pb", pi3)])

        def emit_pv(idx):
            qh, tkc = items[idx]
            h = qh % 8
            pi3 = idx % 3
            for nh in range(2):
                P.emit("tensor", lambda t, _pi=pi3, _h=h, _tkc=tkc, _nh=nh:
                       t.matmul(ps_a[0:65, _nh * 512:_nh * 512 + 512],
                                v_tm[:, _tkc * 520 + _h * 65:_tkc * 520 + _h * 65 + 65],
                                pbuf[:, _pi * T + _nh * 512:_pi * T + _nh * 512 + 512],
                                start=_tkc == 0, stop=_tkc == 31),
                       inc=sem.t if nh == 1 else None,
                       reads=[("v",), ("pb", pi3)], writes=[("P", "ps2", nh)])

        def emit_tail(qh):
            r1 = rowsA[0:1, 0:T]
            P.emit("vector", lambda v, _r=r1: v.reciprocal(_r, ps_a[64:65, :]),
                   reads=[("P", "ps2", 0), ("P", "ps2", 1)], writes=[("r1",)])
            P.emit("vector", lambda v, _r=r1: v.tensor_scalar(_r, _r, mw_sb[0:1, 3:4], None, ALU.mult),
                   reads=[("r1",), ("mw",)], writes=[("r1",)])
            P.emit("vector", lambda v, _r=r1: v.tensor_copy(rbf[:], _r), inc=sem.v,
                   reads=[("r1",)], writes=[("rbf",)])
            for nh in range(2):
                P.emit("tensor", lambda t, _nh=nh:
                       t.matmul(ps_bc[0:64, _nh * 512:_nh * 512 + 512], self.orow_b[0:1, 0:64],
                                rbf[0:1, _nh * 512:_nh * 512 + 512], start=True, stop=True),
                       inc=sem.t if nh == 1 else None,
                       reads=[("rbf",), ("orowb",)], writes=[("P", "ps3", nh)])
            P.emit("scalar", lambda s: s.copy(tmp64[:], ps_a[0:64, :]), inc=sem.s,
                   reads=[("P", "ps2", 0), ("P", "ps2", 1)], writes=[("tmp",)])
            d_ev, o_ev = (qh * 32) // 128, (qh * 32) % 128
            attn_toks[d_ev] = P.emit("vector", lambda v, _d=d_ev, _o=o_ev:
                   v.tensor_tensor(attn[_o:_o + 32, _d * T:(_d + 1) * T], tmp64[0:32, :], ps_bc[0:32, :], ALU.mult),
                   inc=sem.v, reads=[("tmp",), ("P", "ps3", 0), ("P", "ps3", 1)], writes=[("at", d_ev)])
            attn_toks[d_ev + 8] = P.emit("vector", lambda v, _d=d_ev + 8, _o=o_ev:
                   v.tensor_tensor(attn[_o:_o + 32, _d * T:(_d + 1) * T], tmp64[32:64, :], ps_bc[32:64, :], ALU.mult),
                   inc=sem.v, reads=[("tmp",), ("P", "ps3", 0), ("P", "ps3", 1)], writes=[("at", d_ev + 8)])

        LAG = 2
        if sub == 'scores':
            for idx in range(len(items)):
                emit_scores(idx)
        else:
            for idx in range(len(items) + LAG):
                if idx < len(items):
                    emit_scores(idx)
                j = idx - LAG
                if j >= 0:
                    emit_pv(j)
                    if items[j][1] == 31:
                        emit_tail(items[j][0])
        a.attn_toks = attn_toks
        return a

    def emit_a_mem(self, a):
        P, sem = self.P, self.sem
        q_sb, sig_sb, attn = a.q_sb, a.sig_sb, a.attn
        mem_sb, memn_sb, mw_sb = a.mem_sb, a.memn_sb, a.mw_sb
        rowsA, rbf, sctch = a.rowsA, a.rbf, a.sctch
        ps_m, ps_d, ps_bc = a.ps_sc[0], a.ps_a, a.ps_bc
        attn_toks = a.attn_toks
        for qq in range(4):
            for ic in range(4):
                oc = qq * 4 + ic
                qa = q_sb[:, oc * T:(oc + 1) * T]
                mbuf = sctch[:, 0:T]
                pbuf2 = sctch[:, T:2 * T]
                P.emit("vector", lambda v, _q=qa, _m=mbuf: v.tensor_scalar(_m, _q, 0.0, None, ALU.min),
                       reads=[("q",)], writes=[("sct", 0)])
                P.emit("vector", lambda v, _q=qa, _p=pbuf2: v.tensor_scalar(_p, _q, 0.0, None, ALU.max),
                       inc=sem.v, reads=[("q",)], writes=[("sct", 1)])
                P.emit("scalar", lambda s, _m=mbuf: s.activation(_m, _m, AF.Exp), inc=sem.s,
                       reads=[("sct", 0)], writes=[("sct", 0)])
                P.emit("vector", lambda v, _ic=ic, _m=mbuf, _p=pbuf2:
                       v.scalar_tensor_tensor(sig_sb[:, _ic * T:(_ic + 1) * T], _m, -1.0, _p, ALU.add, ALU.add),
                       inc=sem.v, reads=[("sct", 0), ("sct", 1)], writes=[("sig",)])
            for ic in range(4):
                for nh in range(2):
                    P.emit("tensor", lambda t, _ic=ic, _nh=nh:
                           t.matmul(ps_d[64:65, _nh * 512:_nh * 512 + 512], memn_sb[:, _ic:_ic + 1],
                                    sig_sb[:, _ic * T + _nh * 512:_ic * T + _nh * 512 + 512],
                                    start=_ic == 0, stop=_ic == 3),
                           inc=sem.t if (ic == 3 and nh == 1) else None,
                           reads=[("memn",), ("sig",)], writes=[("P", "ps2", nh)])
            r2 = rowsA[0:1, 0:T]
            P.emit("vector", lambda v, _r=r2: v.reciprocal(_r, ps_d[64:65, :]),
                   reads=[("P", "ps2", 0), ("P", "ps2", 1)], writes=[("r1",)])
            P.emit("vector", lambda v, _r=r2: v.tensor_scalar(_r, _r, mw_sb[0:1, 2:3], None, ALU.mult),
                   reads=[("r1",), ("mw",)], writes=[("r1",)])
            P.emit("vector", lambda v, _r=r2: v.tensor_copy(rbf[:], _r), inc=sem.v,
                   reads=[("r1",)], writes=[("rbf",)])
            for nh in range(2):
                P.emit("tensor", lambda t, _nh=nh:
                       t.matmul(ps_bc[:, _nh * 512:_nh * 512 + 512], self.orow_b[0:1, 0:128],
                                rbf[0:1, _nh * 512:_nh * 512 + 512], start=True, stop=True),
                       inc=sem.t if nh == 1 else None,
                       reads=[("rbf",), ("orowb",)], writes=[("P", "ps3", nh)])
            for js in range(4):
                par = 0 if js < 2 else 1
                for ic in range(4):
                    for nh in range(2):
                        P.emit("tensor", lambda t, _ic=ic, _js=js, _par=par, _nh=nh:
                               t.matmul(ps_m[:, _nh * 512:_nh * 512 + 512],
                                        mem_sb[:, _ic * 512:(_ic + 1) * 512].rearrange(
                                            "p (a two) -> p a two", two=2)[:, (_js % 2) * 128:(_js % 2) * 128 + 128, _par],
                                        sig_sb[:, _ic * T + _nh * 512:_ic * T + _nh * 512 + 512],
                                        start=_ic == 0, stop=_ic == 3),
                               inc=sem.t if (ic == 3 and nh == 1) else None,
                               reads=[("mem",), ("sig",)], writes=[("P", "ps0", nh)])
                P.emit("scalar", lambda s: s.copy(sctch[:, 0:T], ps_m[:]), inc=sem.s,
                       reads=[("P", "ps0", 0), ("P", "ps0", 1)], writes=[("sct", 0)])
                dt_i = qq * 2 + (js % 2) + (8 if par else 0)
                P.emit("vector", lambda v: v.tensor_tensor(sctch[:, T:2 * T], sctch[:, 0:T], ps_bc[:], ALU.mult),
                       reads=[("sct", 0), ("P", "ps3", 0), ("P", "ps3", 1)], writes=[("sct", 1)])
                attn_toks[dt_i] = P.emit("vector", lambda v, _d=dt_i:
                       v.tensor_tensor(attn[:, _d * T:(_d + 1) * T], attn[:, _d * T:(_d + 1) * T],
                                       sctch[:, T:2 * T], ALU.add),
                       inc=sem.v, reads=[("at", dt_i), ("sct", 1)], writes=[("at", dt_i)])
        a.attn_toks = attn_toks
        return a

    GELU_TANH_OK = True

    def emit_decode(self, pk_ap, sc_ap, out_ap, dense_shift, tmpa, tmpb, keys_in, keys_sc, keys_out, key_tmp):
        P, sem = self.P, self.sem
        pk32 = pk_ap.bitcast(I32)
        W2 = pk32.shape[-1]
        a32 = tmpa[:, 0:W2]
        j = dense_shift
        mag = (0x00070007 << (4 * j)) & 0xFFFFFFFF
        sgn = (0x00080008 << (4 * j)) & 0xFFFFFFFF
        if sgn >= 1 << 31:
            sgn = sgn - (1 << 32)
        if j <= 2:
            P.emit("vector", lambda v: v.tensor_scalar(a32, pk32, mag, 9 - 4 * j, ALU.bitwise_and, ALU.logical_shift_left),
                   reads=keys_in, writes=[key_tmp + ("a",)])
        else:
            P.emit("vector", lambda v: v.tensor_scalar(a32, pk32, mag, 4 * j - 9, ALU.bitwise_and, ALU.logical_shift_right),
                   reads=keys_in, writes=[key_tmp + ("a",)])
        nb = tmpb.shape[-1]
        for o in range(0, W2, nb):
            e = min(W2, o + nb)
            b32 = tmpb[:, 0:e - o]
            P.emit("vector", lambda v, _b=b32, _p=pk32[:, o:e]: v.tensor_scalar(_b, _p, sgn, 12 - 4 * j, ALU.bitwise_and, ALU.logical_shift_left),
                   reads=keys_in, writes=[key_tmp + ("b",)])
            P.emit("vector", lambda v, _b=b32, _a=a32[:, o:e]: v.tensor_tensor(_a, _a, _b, ALU.bitwise_or),
                   reads=[key_tmp + ("a",), key_tmp + ("b",)], writes=[key_tmp + ("a",)])
        P.emit("vector", lambda v: v.tensor_tensor(out_ap, a32.bitcast(FP16), sc_ap, ALU.mult),
               inc=sem.v, reads=[key_tmp + ("a",)] + keys_sc, writes=keys_out)

    def emit_l0_ffn(self, b):
        P, sem = self.P, self.sem
        gpk = self.din("l0_gpk", [16, 128, 8192], U16)
        upk = self.din("l0_upk", [16, 128, 8192], U16)
        dpk = self.din("l0_dpk", [16, 128, 8192], U16)
        gsc = self.din("l0_gsc", [16, 2, 8, 4096], F32)
        usc = self.din("l0_usc", [16, 2, 8, 4096], F32)
        dsc = self.din("l0_dsc", [16, 2, 8, 4096], F32)

        x1 = b.x1_aps
        x1k = b.x1_keys
        wbuf = b.wbuf
        pkg, pku = b.pkg, b.pku
        scg, scu = b.scg, b.scu
        inter = b.inter
        y = b.y_acc
        tmpa, tmpb = b.tmpa, b.tmpb
        gb, ub = b.gb, b.ub
        PS = [b.ps_g, b.ps_u, b.ps_d, b.ps_3]
        PSN = ["ps0", "ps1", "ps2", "ps3"]

        stage = 0
        y_keys = [("y", hc) for hc in range(NT)]

        def dec_half(ncb, half, bid):
            buf = wbuf[bid]
            for pc in range(2):
                P.dma("sync", self.S("pkg"), pkg[:],
                      gpk[ncb].rearrange("p (wc o) -> p wc o", wc=8)[:, pc * 4:pc * 4 + 4, half * 512:half * 512 + 512],
                      writes=[("pkg",)])
                P.dma("sync", self.S("pku"), pku[:],
                      upk[ncb].rearrange("p (wc o) -> p wc o", wc=8)[:, pc * 4:pc * 4 + 4, half * 512:half * 512 + 512],
                      writes=[("pku",)])
                P.dma("gpsimd", self.S("scg"), scg[:],
                      gsc[ncb, half, :, pc * 2048:(pc + 1) * 2048].unsqueeze(1).broadcast_to((8, 16, 2048)),
                      writes=[("scg",)])
                P.dma("gpsimd", self.S("scu"), scu[:],
                      usc[ncb, half, :, pc * 2048:(pc + 1) * 2048].unsqueeze(1).broadcast_to((8, 16, 2048)),
                      writes=[("scu",)])
                for nib in range(2):
                    ki = nib * 8 + pc * 4
                    self.emit_decode(pkg[:], scg[:],
                                     buf[:, ki * 512:(ki + 4) * 512], nib, tmpa, tmpb,
                                     [("pkg",)], [("scg",)],
                                     [("wb", bid, ki + q) for q in range(4)], ("t",))
                    self.emit_decode(pku[:], scu[:],
                                     buf[:, (16 + ki) * 512:(20 + ki) * 512], nib, tmpa, tmpb,
                                     [("pku",)], [("scu",)],
                                     [("wb", bid, 16 + ki + q) for q in range(4)], ("t",))

        for ncb in range(16):
            half_bid = {0: stage % 2, 1: (stage + 1) % 2}
            if ncb == 0:
                dec_half(0, 0, half_bid[0])
                dec_half(0, 1, half_bid[1])
            def emit_mm_half(half):
                bid = half_bid[half]
                buf = wbuf[bid]
                for si in range(4):
                    wsub = 2 * half + si // 2
                    nib_o = si % 2
                    off = 256 * (wsub % 2)
                    sub_idx = wsub * 2 + nib_o
                    pg, pgn = (PS[0], PSN[0]) if si % 2 == 0 else (PS[2], PSN[2])
                    pu, pun = (PS[1], PSN[1]) if si % 2 == 0 else (PS[3], PSN[3])
                    for ki in range(16):
                        lhs_g = buf[:, ki * 512 + off:ki * 512 + off + 256].rearrange(
                            "p (a two) -> p a two", two=2)[:, :, nib_o]
                        lhs_u = buf[:, (16 + ki) * 512 + off:(16 + ki) * 512 + off + 256].rearrange(
                            "p (a two) -> p a two", two=2)[:, :, nib_o]
                        for nh in range(2):
                            P.emit("tensor", lambda t, _l=lhs_g, _ki=ki, _nh=nh, _ps=pg:
                                   t.matmul(_ps[:, _nh * 512:_nh * 512 + 512], _l,
                                            x1[_ki][:, _nh * 512:_nh * 512 + 512],
                                            start=_ki == 0, stop=_ki == 15),
                                   inc=sem.t if (ki == 15 and nh == 1) else None,
                                   reads=[("wb", bid, ki), x1k[ki]], writes=[("P", pgn, nh)])
                            P.emit("tensor", lambda t, _l=lhs_u, _ki=ki, _nh=nh, _ps=pu:
                                   t.matmul(_ps[:, _nh * 512:_nh * 512 + 512], _l,
                                            x1[_ki][:, _nh * 512:_nh * 512 + 512],
                                            start=_ki == 0, stop=_ki == 15),
                                   inc=sem.t if (ki == 15 and nh == 1) else None,
                                   reads=[("wb", bid, 16 + ki), x1k[ki]], writes=[("P", pun, nh)])
                    it = inter[:, sub_idx * T:(sub_idx + 1) * T]
                    gk = ("gb", si % 2)
                    uk = ("ub", si % 2)
                    gbx = gb[:, (si % 2) * T:(si % 2) * T + T]
                    ubx = ub[:, (si % 2) * T:(si % 2) * T + T]
                    P.emit("scalar", lambda s, _g=gbx, _ps=pg: s.activation(_g, _ps[:], AF.Gelu_apprx_tanh, scale=float(2 ** 14)),
                           inc=sem.s, reads=[("P", pgn, 0), ("P", pgn, 1)], writes=[gk])
                    P.emit("scalar", lambda s, _u=ubx, _ps=pu: s.activation(_u, _ps[:], AF.Copy, scale=float(2 ** 14)),
                           inc=sem.s, reads=[("P", pun, 0), ("P", pun, 1)], writes=[uk])
                    P.emit("vector", lambda v, _it=it, _g=gbx, _u=ubx: v.scalar_tensor_tensor(_it, _g, float(2 ** 14), _u, ALU.mult, ALU.mult),
                           inc=sem.v, reads=[gk, uk], writes=[("int", sub_idx)])

            def emit_down_dec(bid):
                buf = wbuf[bid]
                for ws4 in range(4):
                    pkb, pkk = (pkg, "pkg") if ws4 % 2 == 0 else (pku, "pku")
                    scb, sck = (scg, "scg") if ws4 % 2 == 0 else (scu, "scu")
                    P.dma("sync", self.S(pkk), pkb[:],
                          dpk[ncb].rearrange("p (ws h) -> p ws h", ws=4)[:, ws4, :],
                          writes=[(pkk,)])
                    P.dma("gpsimd", self.S(sck), scb[:],
                          dsc[ncb, ws4 // 2, :, (ws4 % 2) * 2048:(ws4 % 2) * 2048 + 2048].unsqueeze(1).broadcast_to((8, 16, 2048)),
                          writes=[(sck,)])
                    for nib in range(2):
                        pl8 = ws4 * 2 + nib
                        self.emit_decode(pkb[:], scb[:],
                                         buf[:, pl8 * 2048:pl8 * 2048 + 2048],
                                         nib, tmpa, tmpb,
                                         [(pkk,)], [(sck,)],
                                         [("wb", bid, pl8 * 4 + q) for q in range(4)], ("t",))

            emit_mm_half(0)
            down_bid = stage % 2
            emit_down_dec(down_bid)
            emit_mm_half(1)
            if ncb + 1 < 16:
                dec_half(ncb + 1, 0, (stage + 3) % 2)
            bid = stage % 2
            buf = wbuf[bid]
            for hc in range(NT):
                hoff = 256 * (hc % 8)
                hpar = 1 if hc >= 8 else 0
                pd, pdn = (PS[2], PSN[2]) if hc % 2 == 0 else (PS[3], PSN[3])
                for pl8 in range(8):
                    lhs = buf[:, pl8 * 2048 + hoff:pl8 * 2048 + hoff + 256].rearrange(
                        "p (a two) -> p a two", two=2)[:, :, hpar]
                    rkeys = [("wb", bid, pl8 * 4 + q) for q in range(4)]
                    for nh in range(2):
                        P.emit("tensor", lambda t, _l=lhs, _pl=pl8, _nh=nh, _ps=pd, _f=pl8 == 0, _la=pl8 == 7:
                               t.matmul(_ps[:, _nh * 512:_nh * 512 + 512], _l,
                                        inter[:, _pl * T + _nh * 512:_pl * T + _nh * 512 + 512],
                                        start=_f, stop=_la),
                               inc=sem.t if (pl8 == 7 and nh == 1) else None,
                               reads=rkeys + [("int", pl8)], writes=[("P", pdn, nh)])
                if ncb == 0:
                    P.emit("scalar", lambda s, _hc=hc, _ps=pd: s.copy(y[:, _hc * T:(_hc + 1) * T], _ps[:]),
                           inc=sem.s, reads=[("P", pdn, 0), ("P", pdn, 1)], writes=[("y", hc)])
                else:
                    P.emit("vector", lambda v, _hc=hc, _ps=pd: v.tensor_tensor(y[:, _hc * T:(_hc + 1) * T],
                           y[:, _hc * T:(_hc + 1) * T], _ps[:], ALU.add),
                           inc=sem.v, reads=[("y", hc), ("P", pdn, 0), ("P", pdn, 1)], writes=[("y", hc)])
            if ncb + 1 < 16:
                dec_half(ncb + 1, 1, (stage + 4) % 2)
            stage += 3
        return y_keys

    def emit_resid_ln(self, x_hbm_ap, add_aps, add_keys, out_aps, out_keys, r_sb, sq, rows,
                      ps_s1, ps_s2, ps_b1, ps_b2, kn):
        P, sem = self.P, self.sem
        P.dma("sync", self.S("xr"), r_sb[:].rearrange("p (kc t) -> p kc t", kc=16),
              x_hbm_ap.rearrange("(kc p) t -> p kc t", p=128),
              writes=[(kn["r"], kc) for kc in range(NT)])
        r_keys = []
        for kc in range(NT):
            r = r_sb[:, kc * T:(kc + 1) * T]
            P.emit("vector", lambda v, _r=r, _a=add_aps[kc]: v.tensor_tensor(_r, _r, _a, ALU.add),
                   inc=sem.v, reads=[(kn["r"], kc), add_keys[kc]], writes=[(kn["r"], kc)])
            r_keys.append((kn["r"], kc))
        r_aps = [r_sb[:, kc * T:(kc + 1) * T] for kc in range(NT)]
        return self.emit_ln(r_aps, r_keys, out_aps, out_keys, sq, rows, ps_s1, ps_s2, ps_b1, ps_b2, kn)


def alloc_phase_a_scratch(mk, right):
    nc = mk.nc
    a = NS()
    R = lambda *ar: right.enter_context(nc.sbuf_tensor(*ar, side="right"))
    a.q_sb = R("q_sb", [128, 16 * T], BF16)
    a.k_sb = R("k_sb", [128, 4 * S], BF16)
    a.v_tm = R("v_tm", [128, 32 * 520], BF16)
    return a


def build_phase_a(nc, upto='full'):
    es = ExitStack()
    with es:
        P = Prog(nc)
        mk = MK(nc, es, P)
        x1_out = mk.dout("x1_out", [D, T], F32)
        dbg = mk.dout("dbg", [128, 20 * 1024], F32)

        mk.setup_consts()
        attn = mk.sbuf("attn", [128, 16 * T], BF16)
        mem_sb = mk.sbuf("mem_sb", [128, 4 * 512], BF16)
        memn_sb = mk.sbuf("memn_sb", [128, 4], BF16)
        mw_sb = mk.sbuf("mw_sb", [1, 8], F32)

        ps0 = mk.psum("ps0", [128, 1024])
        ps1 = mk.psum("ps1", [128, 1024])
        ps2 = mk.psum("ps2", [128, 1024])
        ps3 = mk.psum("ps3", [128, 1024])

        a = NS()
        a.attn = attn; a.mem_sb = mem_sb; a.memn_sb = memn_sb; a.mw_sb = mw_sb
        a.ps_sc = [ps0, ps1]; a.ps_a = ps2; a.ps_bc = ps3

        def dump_and_finish(tile, ncols, rkeys):
            tok_out = mk.P.dma("gpsimd", mk.S("out"), dbg[:, 0:ncols], tile[:, 0:ncols], reads=rkeys)
            mk.P.raw("gpsimd", lambda e, _t=tok_out: e.wait_ge(_t.sem.h, _t.tgt))
            P.finalize(es.enter_context(nc.Block()), es)

        with ExitStack() as ph:
            R = lambda *ar: ph.enter_context(nc.sbuf_tensor(*ar, side="right"))
            a.q_sb = R("q_sb", [128, 16 * T], BF16)
            a.k_sb = R("k_sb", [128, 4 * S], BF16)
            a.v_tm = R("v_tm", [128, 32 * 520], BF16)
            with ExitStack() as ph12:
                R2 = lambda *ar: ph12.enter_context(nc.sbuf_tensor(*ar, side="right"))
                a.xsl = R2("xsl", [128, 2 * 8192], BF16)
                a.wkv_sb = R2("wkv_sb", [128, 2 * 8192], BF16)
                a.wqb = R2("wqb", [128, 2 * D], BF16)
                mk.emit_l0_attn(a, upto)
            if upto == 'a1':
                dump_and_finish(a.k_sb, 16384, [("k",)])
                return nc
            if upto == 'a1v':
                dump_and_finish(a.v_tm, 16640, [("v",)])
                return nc
            if upto == 'a2':
                dump_and_finish(a.q_sb, 16384, [("q",)])
                return nc
            P.join([("xsl", 0), ("xsl", 1), ("wk",), ("wv",), ("wqb", 0), ("wqb", 1)])
            with ExitStack() as ph34:
                R3 = lambda *ar: ph34.enter_context(nc.sbuf_tensor(*ar, side="right"))
                a.sig_sb = R3("sig_sb", [128, 4 * T], BF16)
                a.sctch = R3("sctch", [128, 2 * T], F32)
                a.pbuf = R3("pbuf", [128, 3 * T], BF16)
                a.tmp64 = R3("tmp64", [64, T], BF16)
                a.rowsA = R3("rowsA", [1, T], F32)
                a.rbf = R3("rbf", [1, T], BF16)
                if upto.startswith('a3:'):
                    _, nqh_s, sub_s = upto.split(':')
                    mk.emit_l0_scores(a, nqh=int(nqh_s), sub=sub_s)
                    dump_and_finish(a.pbuf, 2048, [("pb", 0), ("pb", 1)])
                    return nc
                mk.emit_l0_scores(a)
                if upto == 'a3':
                    dump_and_finish(a.attn, 16384, [("at", i) for i in range(16)])
                    return nc
                mk.emit_a_mem(a)
                if upto == 'a4':
                    dump_and_finish(a.attn, 16384, [("at", i) for i in range(16)])
                    return nc
        P.join([("q",), ("k",), ("v",), ("sig",), ("sct", 0), ("sct", 1), ("pb", 0), ("pb", 1),
                ("tmp",), ("r1",), ("rbf",), ("xsl", 0), ("xsl", 1), ("wk",), ("wv",), ("wqb", 0), ("wqb", 1)])
        with ExitStack() as ph5:
            R5 = lambda *ar: ph5.enter_context(nc.sbuf_tensor(*ar, side="right"))
            r_sb = R5("r_sb", [128, 16 * T], F32)
            sq = R5("sq", [128, 2 * T], F32)
            rows = R5("rows", [1, 5 * T], F32)
            out_aps = [attn[:, kc * T:(kc + 1) * T] for kc in range(NT)]
            kn = dict(r="rA", sq="sqA", rows="rwA", s1=("P", "ps0"), s2=("P", "ps1"),
                      b1=("P", "ps2"), b2=("P", "ps3"))
            mk.emit_resid_ln(mk.ap_xT_my, out_aps, [("at", kc) for kc in range(NT)],
                             out_aps, [("x1", kc) for kc in range(NT)],
                             r_sb, sq, rows, ps0[0:1, :], ps1[0:1, :], ps2, ps3, kn)
            tok_out = None
            for kc in range(NT):
                tok_out = mk.P.dma("gpsimd", mk.S("out"),
                                   x1_out.rearrange("(kc p) t -> kc p t", p=128)[kc],
                                   attn[:, kc * T:(kc + 1) * T], reads=[("x1", kc)])
            mk.P.raw("gpsimd", lambda e, _t=tok_out: e.wait_ge(_t.sem.h, _t.tgt))
        P.finalize(es.enter_context(nc.Block()), es)
    return nc


def build_phase_b(nc, upto='full'):
    es = ExitStack()
    with es:
        P = Prog(nc)
        mk = MK(nc, es, P)
        x2_out = mk.dout("x2_out", [D, T], F32)
        dbg = mk.dout("dbg", [128, 16 * 1024], F32)
        stream_in = mk.din("stream_in", [D, T], F32)

        mk.setup_consts()
        x1_sb = mk.sbuf("x1_sb", [128, 16 * T], BF16)
        ps_g = mk.psum("psg", [128, 1024])
        ps_u = mk.psum("psu", [128, 1024])
        ps_d = mk.psum("psd", [128, 1024])
        ps_3 = mk.psum("ps3", [128, 1024])

        P.dma("gpsimd", mk.S("si"), x1_sb[:].rearrange("p (kc t) -> p kc t", kc=16),
              stream_in.rearrange("(kc p) t -> p kc t", p=128),
              writes=[("x1", kc) for kc in range(NT)])

        b = NS()
        b.x1_aps = [x1_sb[:, kc * T:(kc + 1) * T] for kc in range(NT)]
        b.x1_keys = [("x1", kc) for kc in range(NT)]
        b.ps_g, b.ps_u, b.ps_d, b.ps_3 = ps_g, ps_u, ps_d, ps_3

        def dump_and_finish(tile, ncols, rkeys):
            tok_out = mk.P.dma("gpsimd", mk.S("out"), dbg[:, 0:ncols], tile[:, 0:ncols], reads=rkeys)
            mk.P.raw("gpsimd", lambda e, _t=tok_out: e.wait_ge(_t.sem.h, _t.tgt))
            P.finalize(es.enter_context(nc.Block()), es)

        y_guard = nc.sbuf_tensor("y_acc", [128, 16 * T], F32, side="right")
        b.y_acc = es.enter_context(y_guard)
        with ExitStack() as ph:
            R = lambda *ar: ph.enter_context(nc.sbuf_tensor(*ar, side="right"))
            b.wbuf = [R("wbuf0", [128, 16384], BF16), R("wbuf1", [128, 16384], BF16)]
            b.pkg = R("pkg", [128, 2048], U16)
            b.pku = R("pku", [128, 2048], U16)
            b.scg = R("scg", [128, 2048], FP16)
            b.scu = R("scu", [128, 2048], FP16)
            b.inter = R("inter", [128, 8 * T], BF16)
            b.tmpa = R("tmpa", [128, 1024], I32)
            b.tmpb = R("tmpb", [128, 512], I32)
            b.gb = R("gb", [128, 2 * T], BF16)
            b.ub = R("ub", [128, 2 * T], BF16)
            y_keys = mk.emit_l0_ffn(b)
        if upto == 'y':
            dump_and_finish(b.y_acc, 16384, y_keys)
            return nc
        for kc in range(NT):
            P.emit("vector", lambda v, _kc=kc: v.tensor_tensor(
                   b.y_acc[:, _kc * T:(_kc + 1) * T], b.y_acc[:, _kc * T:(_kc + 1) * T],
                   b.x1_aps[_kc], ALU.add),
                   inc=mk.sem.v, reads=[("y", kc), ("x1", kc)], writes=[("y", kc)])
        P.join([("wb", bid, u) for bid in range(2) for u in range(32)] +
               [("int", si) for si in range(8)] +
               [("pkg",), ("pku",), ("scg",), ("scu",), ("gb", 0), ("gb", 1), ("ub", 0), ("ub", 1),
                ("t", "a"), ("t", "b")])
        with ExitStack() as ph2:
            R2 = lambda *ar: ph2.enter_context(nc.sbuf_tensor(*ar, side="right"))
            sq = R2("sq", [128, 2 * T], F32)
            rows = R2("rows", [1, 5 * T], F32)
            r_aps = [b.y_acc[:, kc * T:(kc + 1) * T] for kc in range(NT)]
            kn = dict(sq="sqB", rows="rwB", s1=("P", "ps0"), s2=("P", "ps1"),
                      b1=("P", "ps2"), b2=("P", "ps3"))
            mk.emit_ln(r_aps, y_keys, b.x1_aps, [("x2", kc) for kc in range(NT)],
                       sq, rows, ps_g[0:1, :], ps_u[0:1, :], ps_d, ps_3, kn)
            tok_out = None
            for kc in range(NT):
                tok_out = mk.P.dma("gpsimd", mk.S("out"),
                                   x2_out.rearrange("(kc p) t -> kc p t", p=128)[kc],
                                   x1_sb[:, kc * T:(kc + 1) * T], reads=[("x2", kc)])
            mk.P.raw("gpsimd", lambda e, _t=tok_out: e.wait_ge(_t.sem.h, _t.tgt))
        P.finalize(es.enter_context(nc.Block()), es)
    return nc


DENOM_L1 = float(np.sqrt(2048.0) / 4096.0)


def build_phase_c(nc, upto='full', ncores=NCORES):
    es = ExitStack()
    with es:
        P = Prog(nc)
        mk = MK(nc, es, P)
        sem = mk.sem
        x3_out = mk.dout("x3_out", [D, T], F32)
        dbg = mk.dout("dbg", [128, 16 * 1024], F32)
        stream_in = mk.din("stream_in", [D, T], F32)
        wq = mk.din("l1_wq", [16, 128, D], F32)
        wk = mk.din("l1_wk", [16, 128, D], F32)
        wv = mk.din("l1_wv", [16, 128, D], F32)
        wo = mk.din("l1_wo", [16, 128, D], F32)
        cmask = mk.din("cmask", [128, 8], F32)
        cc_in = nc.dram_tensor("cc_in", [128, 16], F32, kind="Internal")
        cc_out = nc.dram_tensor("cc_out", [128 * ncores, 16], F32, kind="Internal")

        mk.setup_consts()
        x2_sb = mk.sbuf("x2_sb", [128, 16 * T], BF16)
        cm_sb = mk.sbuf("cm_sb", [128, 8], F32)
        cs_sb = mk.sbuf("cs_sb", [128, 16], F32)
        cy_sb = mk.sbuf("cy_sb", [128, 16], F32)
        ag_sb = mk.sbuf("ag_sb", [128, 16 * ncores], F32)
        ps0 = mk.psum("ps0", [128, 1024])
        ps1 = mk.psum("ps1", [128, 1024])
        ps2 = mk.psum("ps2", [128, 1024])
        ps3 = mk.psum("ps3", [128, 1024])

        P.dma("gpsimd", mk.S("si"), x2_sb[:].rearrange("p (kc t) -> p kc t", kc=16),
              stream_in.rearrange("(kc p) t -> p kc t", p=128),
              writes=[("x2", kc) for kc in range(NT)])
        P.dma("sync", mk.S("cm"), cm_sb[:], cmask, writes=[("cm",)])

        def dump_and_finish(tile, ncols, rkeys):
            tok_out = mk.P.dma("gpsimd", mk.S("out"), dbg[:, 0:ncols], tile[:, 0:ncols], reads=rkeys)
            mk.P.raw("gpsimd", lambda e, _t=tok_out: e.wait_ge(_t.sem.h, _t.tgt))
            P.finalize(es.enter_context(nc.Block()), es)

        q2 = es.enter_context(nc.sbuf_tensor("q2", [128, 16 * T], BF16, side="right"))
        wb = es.enter_context(nc.sbuf_tensor("wb", [128, 2 * D], BF16, side="right"))

        X2 = lambda kc: x2_sb[:, kc * T:(kc + 1) * T]

        def proj(dst_writer, w_ap, wkey_pfx, extra=None):
            for oc in range(16):
                bi = oc % 2
                P.dma("gpsimd", mk.S(wkey_pfx + str(bi)), wb[:, bi * D:(bi + 1) * D], w_ap[oc],
                      writes=[(wkey_pfx, bi)])
                for kc in range(NT):
                    for nh in range(2):
                        P.emit("tensor", lambda t, _bi=bi, _kc=kc, _nh=nh, _ps=dst_writer[0]:
                               t.matmul(_ps[:, _nh * 512:_nh * 512 + 512],
                                        wb[:, _bi * D + _kc * 128:_bi * D + _kc * 128 + 128],
                                        X2(_kc)[:, _nh * 512:_nh * 512 + 512],
                                        start=_kc == 0, stop=_kc == NT - 1),
                               inc=sem.t if (kc == NT - 1 and nh == 1) else None,
                               reads=[(wkey_pfx, bi), ("x2", kc)], writes=[dst_writer[1] + (nh,)])
                dst_writer[2](oc)

        with ExitStack() as phkv:
            Rr = lambda *ar: phkv.enter_context(nc.sbuf_tensor(*ar, side="right"))
            kv_sb = Rr("kv_sb", [128, 16 * T], BF16)
            kbuf = Rr("kbuf", [128, T], BF16)
            kvt_t = Rr("kvt_t", [128, T], BF16)

            def q_ev(oc):
                P.emit("scalar", lambda s, _oc=oc: s.copy(q2[:, _oc * T:(_oc + 1) * T], ps0[:]),
                       inc=sem.s, reads=[("P", "ps0", 0), ("P", "ps0", 1)], writes=[("q2", oc)])
            proj((ps0, ("P", "ps0"), q_ev)[0:3], wq, "wA")

            for oc in range(16):
                bi = oc % 2
                P.dma("gpsimd", mk.S("wA" + str(bi)), wb[:, bi * D:(bi + 1) * D], wk[oc], writes=[("wA", bi)])
                for kc in range(NT):
                    for nh in range(2):
                        P.emit("tensor", lambda t, _bi=bi, _kc=kc, _nh=nh:
                               t.matmul(ps0[:, _nh * 512:_nh * 512 + 512],
                                        wb[:, _bi * D + _kc * 128:_bi * D + _kc * 128 + 128],
                                        X2(_kc)[:, _nh * 512:_nh * 512 + 512],
                                        start=_kc == 0, stop=_kc == NT - 1),
                               inc=sem.t if (kc == NT - 1 and nh == 1) else None,
                               reads=[("wA", bi), ("x2", kc)], writes=[("P", "ps0", 0 if nh == 0 else 1)])
                P.emit("scalar", lambda s: s.copy(kbuf[:], ps0[:]),
                       inc=sem.s, reads=[("P", "ps0", 0), ("P", "ps0", 1)], writes=[("kb",)])
                P.dma("gpsimd", mk.S("wB" + str(bi)), wb[:, bi * D:(bi + 1) * D], wv[oc], writes=[("wA", bi)])
                for kc in range(NT):
                    for nh in range(2):
                        P.emit("tensor", lambda t, _bi=bi, _kc=kc, _nh=nh:
                               t.matmul(ps1[:, _nh * 512:_nh * 512 + 512],
                                        wb[:, _bi * D + _kc * 128:_bi * D + _kc * 128 + 128],
                                        X2(_kc)[:, _nh * 512:_nh * 512 + 512],
                                        start=_kc == 0, stop=_kc == NT - 1),
                               inc=sem.t if (kc == NT - 1 and nh == 1) else None,
                               reads=[("wA", bi), ("x2", kc)], writes=[("P", "ps1", nh)])
                P.emit("vector", lambda v, _oc=oc: v.tensor_tensor(kv_sb[:, _oc * T:(_oc + 1) * T], kbuf[:], ps1[:], ALU.mult),
                       inc=sem.v, reads=[("kb",), ("P", "ps1", 0), ("P", "ps1", 1)], writes=[("kv", oc)])
                P.emit("vector", lambda v, _oc=oc: v.tensor_reduce(cs_sb[:, _oc:_oc + 1], kv_sb[:, _oc * T:(_oc + 1) * T], AX.X, ALU.add),
                       inc=sem.v, reads=[("kv", oc)], writes=[("cs",)])

            P.dma("gpsimd", mk.S("ccw"), cc_in.ap(), cs_sb[:], reads=[("cs",)], writes=[("ccin",)])
            P.emit("gpsimd", lambda g: g.collective_compute(
                       "AllGather", ALU.bypass, replica_groups=[list(range(ncores))],
                       ins=[cc_in.ap()], outs=[cc_out.ap()]),
                   inc=sem.cc, reads=[("ccin",)], writes=[("ccout",)])
            P.dma("gpsimd", mk.S("agr"), ag_sb[:, 0:16 * ncores],
                  cc_out.ap().rearrange("(r p) oc -> p r oc", p=128), reads=[("ccout",)], writes=[("ag",)])
            P.emit("vector", lambda v: v.memset(cy_sb[:], 0.0), writes=[("cy",)])
            for r in range(ncores):
                P.emit("vector", lambda v, _r=r: v.scalar_tensor_tensor(
                           cy_sb[:], ag_sb[:, _r * 16:(_r + 1) * 16], cm_sb[:, _r:_r + 1], cy_sb[:],
                           ALU.mult, ALU.add),
                       inc=sem.v if r == ncores - 1 else None,
                       reads=[("ag",), ("cm",), ("cy",)], writes=[("cy",)])

            for oc in range(16):
                P.emit("vector", lambda v, _oc=oc: v.tensor_tensor_scan(
                           kvt_t[:], kv_sb[:, _oc * T:(_oc + 1) * T], kv_sb[:, _oc * T:(_oc + 1) * T],
                           cy_sb[:, _oc:_oc + 1], ALU.add, ALU.bypass),
                       inc=sem.v, reads=[("kv", oc), ("cy",)], writes=[("kvt",)])
                P.emit("vector", lambda v, _oc=oc: v.scalar_tensor_tensor(
                           q2[:, _oc * T:(_oc + 1) * T], q2[:, _oc * T:(_oc + 1) * T], DENOM_L1, kvt_t[:],
                           ALU.mult, ALU.mult),
                       inc=sem.v, reads=[("q2", oc), ("kvt",)], writes=[("q2", oc)])

        if upto == 'vhat':
            dump_and_finish(q2, 16384, [("q2", oc) for oc in range(16)])
            return nc

        P.join([("kv", oc) for oc in range(16)] + [("kb",), ("kvt",)])
        with ExitStack() as phr:
            Rr = lambda *ar: phr.enter_context(nc.sbuf_tensor(*ar, side="right"))
            r_sb = Rr("r_sb", [128, 16 * T], F32)
            sq = Rr("sq", [128, 2 * T], F32)
            rows = Rr("rows", [1, 5 * T], F32)
            for oc in range(16):
                bi = oc % 2
                P.dma("gpsimd", mk.S("wA" + str(bi)), wb[:, bi * D:(bi + 1) * D], wo[oc], writes=[("wA", bi)])
                for kc in range(NT):
                    for nh in range(2):
                        P.emit("tensor", lambda t, _bi=bi, _kc=kc, _nh=nh:
                               t.matmul(ps2[:, _nh * 512:_nh * 512 + 512],
                                        wb[:, _bi * D + _kc * 128:_bi * D + _kc * 128 + 128],
                                        q2[:, _kc * T + _nh * 512:_kc * T + _nh * 512 + 512],
                                        start=_kc == 0, stop=_kc == NT - 1),
                               inc=sem.t if (kc == NT - 1 and nh == 1) else None,
                               reads=[("wA", bi), ("q2", kc)], writes=[("P", "ps2", nh)])
                P.emit("vector", lambda v, _oc=oc: v.tensor_tensor(r_sb[:, _oc * T:(_oc + 1) * T],
                           x2_sb[:, _oc * T:(_oc + 1) * T], ps2[:], ALU.add),
                       inc=sem.v, reads=[("x2", oc), ("P", "ps2", 0), ("P", "ps2", 1)], writes=[("rC", oc)])
            r_aps = [r_sb[:, kc * T:(kc + 1) * T] for kc in range(NT)]
            kn = dict(sq="sqC", rows="rwC", s1=("P", "ps0"), s2=("P", "ps1"),
                      b1=("P", "ps2"), b2=("P", "ps3"))
            P.join([("q2", oc) for oc in range(16)], engines=("tensor",))
            mk.emit_ln(r_aps, [("rC", kc) for kc in range(NT)],
                       [X2(kc) for kc in range(NT)], [("x3", kc) for kc in range(NT)],
                       sq, rows, ps0[0:1, :], ps1[0:1, :], ps2, ps3, kn)
            tok_out = None
            for kc in range(NT):
                tok_out = mk.P.dma("gpsimd", mk.S("out"),
                                   x3_out.rearrange("(kc p) t -> kc p t", p=128)[kc],
                                   x2_sb[:, kc * T:(kc + 1) * T], reads=[("x3", kc)])
            mk.P.raw("gpsimd", lambda e, _t=tok_out: e.wait_ge(_t.sem.h, _t.tgt))
        P.finalize(es.enter_context(nc.Block()), es)
    return nc


L1_CHUNKS = [(0, 2), (2, 2), (4, 2), (6, 2), (8, 2), (10, 1)]


def build_phase_d(nc, upto='full'):
    es = ExitStack()
    with es:
        P = Prog(nc)
        mk = MK(nc, es, P)
        sem = mk.sem
        x4_out = mk.dout("x4_out", [D, T], F32)
        dbg = mk.dout("dbg", [128, 16 * 1024], F32)
        stream_in = mk.din("stream_in", [D, T], F32)
        gpk = mk.din("l1_gpk", [22, 128, 2048], U16)
        dpk = mk.din("l1_dpk", [11, 128, 2048], U16)
        gsc = mk.din("l1_gsc", [22, 4, 4, 512], FP16)
        dsc = mk.din("l1_dsc", [11, 4, 2048], FP16)
        x3d = nc.dram_tensor("x3d", [D, T], BF16, kind="Internal")

        mk.setup_consts()
        x3_sb = mk.sbuf("x3_sb", [128, 16 * T], BF16)
        ps0 = mk.psum("ps0", [128, 1024])
        ps1 = mk.psum("ps1", [128, 1024])
        ps2 = mk.psum("ps2", [128, 1024])
        ps3 = mk.psum("ps3", [128, 1024])

        P.dma("gpsimd", mk.S("si"), x3_sb[:].rearrange("p (kc t) -> p kc t", kc=16),
              stream_in.rearrange("(kc p) t -> p kc t", p=128),
              writes=[("x3", kc) for kc in range(NT)])

        def dump_and_finish(tile, ncols, rkeys):
            tok_out = mk.P.dma("gpsimd", mk.S("out"), dbg[:, 0:ncols], tile[:, 0:ncols], reads=rkeys)
            mk.P.raw("gpsimd", lambda e, _t=tok_out: e.wait_ge(_t.sem.h, _t.tgt))
            P.finalize(es.enter_context(nc.Block()), es)

        for kc in range(NT):
            P.dma("sync", mk.S("bw"), x3d.ap().rearrange("(kc p) t -> kc p t", p=128)[kc],
                  x3_sb[:, kc * T:(kc + 1) * T], reads=[("x3", kc)], writes=[("x3d", kc)])
        xj = es.enter_context(nc.sbuf_tensor("xj", [128, 16 * T], BF16, side="right"))
        y = es.enter_context(nc.sbuf_tensor("y_acc", [128, 16 * T], F32, side="right"))
        x3r = x3d.ap().rearrange("(a two) t -> two a t", two=2)
        for jp in range(4):
            par, hi = jp % 2, jp // 2
            par = 1 if jp in (2, 3) else 0
            hi = 1 if jp in (1, 3) else 0
            for tt in range(4):
                a0 = hi * 512 + tt * 128
                P.dma("sync", mk.S("xj%d" % (jp * 4 + tt)), xj[:, (jp * 4 + tt) * T:(jp * 4 + tt + 1) * T],
                      x3r[par, a0:a0 + 128, :],
                      reads=[("x3d", kc) for kc in range(NT)], writes=[("xj", jp, tt)])

        XJ = lambda jp, tt: xj[:, (jp * 4 + tt) * T:(jp * 4 + tt + 1) * T]

        with ExitStack() as ph:
            Rr = lambda *ar: ph.enter_context(nc.sbuf_tensor(*ar, side="right"))
            wdec = Rr("wdec", [128, 16384], BF16)
            pk0 = Rr("pk0", [128, 2048], U16)
            pk1 = Rr("pk1", [128, 2048], U16)
            sc0 = Rr("sc0", [128, 2048], FP16)
            sc1 = Rr("sc1", [128, 2048], FP16)
            inter = Rr("interD", [128, 8 * T], BF16)
            tmpa = Rr("tmpa", [128, 1024], I32)
            tmpb = Rr("tmpb", [128, 1024], I32)
            gsb = Rr("gsb", [128, T], BF16)

            GUK = lambda ki: ("gu", ki)
            y_keys = [("y", hc) for hc in range(NT)]
            for ci, (cs, nt) in enumerate(L1_CHUNKS):
                for tt in range(nt):
                    for half, span in ((0, cs + tt), (1, 11 + cs + tt)):
                        pkb, pkk = (pk0, ("pk", 0)) if half == 0 else (pk1, ("pk", 1))
                        scb, sck = (sc0, ("sc", 0)) if half == 0 else (sc1, ("sc", 1))
                        P.dma("sync", mk.S("pk%d" % half), pkb[:], gpk[span], writes=[pkk])
                        for tin in range(4):
                            P.dma("sync", mk.S("sc%d%d" % (half, tin)), scb[:, tin * 512:(tin + 1) * 512],
                                  gsc[span, tin].unsqueeze(1).broadcast_to((4, 32, 512)),
                                  writes=[sck + (tin,)])
                        for jin in range(4):
                            for tin in range(4):
                                ki = jin * 4 + tin
                                dst = wdec[:, (half * 16 + ki) * 512:(half * 16 + ki + 1) * 512]
                                self_keys = [("w", half, ki)]
                                mk.emit_decode(pkb[:, tin * 512:(tin + 1) * 512], scb[:, tin * 512:(tin + 1) * 512],
                                               dst, jin, tmpa, tmpb,
                                               [pkk], [sck + (tin,)], self_keys, ("t",))
                    for jo in range(4):
                        for half in range(2):
                            psx, psk = (ps0, ("P", "ps0")) if half == 0 else (ps1, ("P", "ps1"))
                            for jin in range(4):
                                for tin in range(4):
                                    ki = jin * 4 + tin
                                    lhs = wdec[:, (half * 16 + ki) * 512:(half * 16 + ki + 1) * 512].rearrange(
                                        "p (a four) -> p a four", four=4)[:, :, jo]
                                    for nh in range(2):
                                        P.emit("tensor", lambda t, _l=lhs, _jin=jin, _tin=tin, _nh=nh, _ps=psx,
                                               _f=ki == 0, _la=ki == 15:
                                               t.matmul(_ps[:, _nh * 512:_nh * 512 + 512], _l,
                                                        XJ(_jin, _tin)[:, _nh * 512:_nh * 512 + 512],
                                                        start=_f, stop=_la),
                                               inc=sem.t if (ki == 15 and nh == 1) else None,
                                               reads=[("w", half, ki), ("xj", jin, tin)], writes=[psk + (nh,)])
                        it = inter[:, (tt * 4 + jo) * T:(tt * 4 + jo + 1) * T]
                        P.emit("scalar", lambda s: s.activation(gsb[:], ps0[:], AF.Silu, scale=float(2 ** 14)),
                               inc=sem.s, reads=[("P", "ps0", 0), ("P", "ps0", 1)], writes=[("gsb",)])
                        P.emit("vector", lambda v, _it=it: v.scalar_tensor_tensor(_it, gsb[:], float(2 ** 28), ps1[:], ALU.mult, ALU.mult),
                               inc=sem.v, reads=[("gsb",), ("P", "ps1", 0), ("P", "ps1", 1)],
                               writes=[("int", tt * 4 + jo)])
                for tt in range(nt):
                    pkb, pkk = (pk0, ("pk", 0)) if tt % 2 == 0 else (pk1, ("pk", 1))
                    scb, sck = (sc0, ("sc", 0)) if tt % 2 == 0 else (sc1, ("sc", 1))
                    P.dma("sync", mk.S("pk%d" % (tt % 2)), pkb[:], dpk[cs + tt], writes=[pkk])
                    P.dma("sync", mk.S("scd%d" % (tt % 2)), scb[:],
                          dsc[cs + tt].unsqueeze(1).broadcast_to((4, 32, 2048)), writes=[sck + (0,), sck + (1,), sck + (2,), sck + (3,)])
                    for jd in range(4):
                        pl = tt * 4 + jd
                        dst = wdec[:, pl * 2048:(pl + 1) * 2048]
                        alias = [("w", 0, 4 * pl + q) for q in range(4)] if pl < 4 else \
                                [("w", 1, 4 * (pl - 4) + q) for q in range(4)]
                        mk.emit_decode(pkb[:], scb[:], dst, jd, tmpa, tmpb,
                                       [pkk], [sck + (q,) for q in range(4)], [("dw", pl)] + alias, ("t",))
                for hc in range(NT):
                    hoff = 256 * (hc % 8)
                    hpar = 1 if hc >= 8 else 0
                    np_ = nt * 4
                    for pl in range(np_):
                        tt, jd = pl // 4, pl % 4
                        lhs = wdec[:, pl * 2048 + hoff:pl * 2048 + hoff + 256].rearrange(
                            "p (a two) -> p a two", two=2)[:, :, hpar]
                        alias_r = ([("w", 0, 4 * pl + q) for q in range(4)] if pl < 4 else
                                   [("w", 1, 4 * (pl - 4) + q) for q in range(4)])
                        for nh in range(2):
                            P.emit("tensor", lambda t, _l=lhs, _pl=pl, _nh=nh, _f=pl == 0, _la=pl == np_ - 1:
                                   t.matmul(ps2[:, _nh * 512:_nh * 512 + 512], _l,
                                            inter[:, _pl * T + _nh * 512:_pl * T + _nh * 512 + 512],
                                            start=_f, stop=_la),
                                   inc=sem.t if (pl == np_ - 1 and nh == 1) else None,
                                   reads=[("dw", pl), ("int", pl)] + alias_r, writes=[("P", "ps2", nh)])
                    if ci == 0:
                        P.emit("scalar", lambda s, _hc=hc: s.copy(y[:, _hc * T:(_hc + 1) * T], ps2[:]),
                               inc=sem.s, reads=[("P", "ps2", 0), ("P", "ps2", 1)], writes=[("y", hc)])
                    else:
                        P.emit("vector", lambda v, _hc=hc: v.tensor_tensor(y[:, _hc * T:(_hc + 1) * T],
                               y[:, _hc * T:(_hc + 1) * T], ps2[:], ALU.add),
                               inc=sem.v, reads=[("y", hc), ("P", "ps2", 0), ("P", "ps2", 1)], writes=[("y", hc)])
        if upto == 'y':
            dump_and_finish(y, 16384, y_keys)
            return nc

        P.join([("w", 0, k) for k in range(16)] + [("w", 1, k) for k in range(16)] +
               [("dw", pl) for pl in range(8)] + [("int", q) for q in range(8)] +
               [("pk", 0), ("pk", 1), ("sc", 0, 0), ("sc", 0, 1), ("sc", 0, 2), ("sc", 0, 3),
                ("sc", 1, 0), ("sc", 1, 1), ("sc", 1, 2), ("sc", 1, 3), ("gsb",), ("t", "a"), ("t", "b")])
        with ExitStack() as ph2:
            Rr = lambda *ar: ph2.enter_context(nc.sbuf_tensor(*ar, side="right"))
            sq = Rr("sq", [128, 2 * T], F32)
            rows = Rr("rows", [1, 5 * T], F32)
            RW = [("rwD",)]
            for hc in range(NT):
                buf = sq[:, (hc % 2) * T:(hc % 2) * T + T]
                P.emit("scalar", lambda s, _b=buf, _hc=hc: s.activation(_b, y[:, _hc * T:(_hc + 1) * T], AF.Square),
                       inc=sem.s, reads=[("y", hc)], writes=[("sqD", hc % 2)])
                for nh in range(2):
                    P.emit("tensor", lambda t, _b=buf, _nh=nh, _f=hc == 0, _la=hc == NT - 1:
                           t.matmul(ps0[0:1, _nh * 512:_nh * 512 + 512], mk.ones_col,
                                    _b[:, _nh * 512:_nh * 512 + 512], start=_f, stop=_la),
                           inc=sem.t if (hc == NT - 1 and nh == 1) else None,
                           reads=[("sqD", hc % 2), ("cst",)], writes=[("P", "ps0", nh)])
            msq = rows[0:1, 0:T]
            P.emit("vector", lambda v: v.tensor_scalar(msq, ps0[0:1, :], 1.0 / D, None, ALU.mult),
                   inc=sem.v, reads=[("P", "ps0", 0), ("P", "ps0", 1)], writes=RW)
            P.emit("scalar", lambda s: s.activation(msq, msq, AF.Ln, bias=mk.eps_rms),
                   reads=RW + [("cst",)], writes=RW)
            P.emit("scalar", lambda s: s.activation(msq, msq, AF.Exp, scale=-0.5), inc=sem.s,
                   reads=RW, writes=RW)
            for nh in range(2):
                P.emit("tensor", lambda t, _nh=nh: t.matmul(ps1[:, _nh * 512:_nh * 512 + 512],
                       mk.orow[0:1, 0:128], msq[0:1, _nh * 512:_nh * 512 + 512], start=True, stop=True),
                       inc=sem.t if nh == 1 else None,
                       reads=RW + [("orow",)], writes=[("P", "ps1", nh)])
            for kc in range(NT):
                P.emit("vector", lambda v, _kc=kc: v.tensor_tensor(y[:, _kc * T:(_kc + 1) * T],
                       y[:, _kc * T:(_kc + 1) * T], ps1[:], ALU.mult),
                       reads=[("y", kc), ("P", "ps1", 0), ("P", "ps1", 1)], writes=[("y", kc)])
                P.emit("vector", lambda v, _kc=kc: v.tensor_tensor(y[:, _kc * T:(_kc + 1) * T],
                       y[:, _kc * T:(_kc + 1) * T], x3_sb[:, _kc * T:(_kc + 1) * T], ALU.add),
                       inc=sem.v, reads=[("y", kc), ("x3", kc)], writes=[("y", kc)])
            r_aps = [y[:, kc * T:(kc + 1) * T] for kc in range(NT)]
            kn = dict(sq="sqD2", rows="rwD2", s1=("P", "ps0"), s2=("P", "ps2"),
                      b1=("P", "ps1"), b2=("P", "ps3"))
            mk.emit_ln(r_aps, [("y", kc) for kc in range(NT)],
                       [x3_sb[:, kc * T:(kc + 1) * T] for kc in range(NT)],
                       [("x4", kc) for kc in range(NT)],
                       sq, rows, ps0[0:1, :], ps2[0:1, :], ps1, ps3, kn)
            tok_out = None
            for kc in range(NT):
                tok_out = mk.P.dma("gpsimd", mk.S("out"),
                                   x4_out.rearrange("(kc p) t -> kc p t", p=128)[kc],
                                   x3_sb[:, kc * T:(kc + 1) * T], reads=[("x4", kc)])
            mk.P.raw("gpsimd", lambda e, _t=tok_out: e.wait_ge(_t.sem.h, _t.tgt))
        P.finalize(es.enter_context(nc.Block()), es)
    return nc



def build_fused(nc, ncores=NCORES):
    es = ExitStack()
    with es:
        P = Prog(nc)
        mk = MK(nc, es, P)
        sem = mk.sem
        out_ap = mk.dout("x4_out", [D, T], F32)

        mk.setup_consts()
        stream = mk.sbuf("stream", [128, 16 * T], BF16)
        ps0 = mk.psum("ps0", [128, 1024])
        ps1 = mk.psum("ps1", [128, 1024])
        ps2 = mk.psum("ps2", [128, 1024])
        ps3 = mk.psum("ps3", [128, 1024])
        ST = lambda kc: stream[:, kc * T:(kc + 1) * T]

        a = NS()
        a.attn = stream
        a.ps_sc = [ps0, ps1]; a.ps_a = ps2; a.ps_bc = ps3
        with ExitStack() as ph:
            R = lambda *ar: ph.enter_context(nc.sbuf_tensor(*ar, side="right"))
            a.mem_sb = R("mem_sb", [128, 4 * 512], BF16)
            a.memn_sb = R("memn_sb", [128, 4], BF16)
            a.mw_sb = R("mw_sb", [1, 8], F32)
            a.q_sb = R("q_sb", [128, 16 * T], BF16)
            a.k_sb = R("k_sb", [128, 4 * S], BF16)
            a.v_tm = R("v_tm", [128, 32 * 520], BF16)
            with ExitStack() as ph12:
                R2 = lambda *ar: ph12.enter_context(nc.sbuf_tensor(*ar, side="right"))
                a.xsl = R2("xsl", [128, 2 * 8192], BF16)
                a.wkv_sb = R2("wkv_sb", [128, 2 * 8192], BF16)
                a.wqb = R2("wqb", [128, 2 * D], BF16)
                mk.emit_l0_attn(a, 'full')
            P.join([("xsl", 0), ("xsl", 1), ("wk",), ("wv",), ("wqb", 0), ("wqb", 1)])
            with ExitStack() as ph34:
                R3 = lambda *ar: ph34.enter_context(nc.sbuf_tensor(*ar, side="right"))
                a.sig_sb = R3("sig_sb", [128, 4 * T], BF16)
                a.sctch = R3("sctch", [128, 2 * T], F32)
                a.pbuf = R3("pbuf", [128, 3 * T], BF16)
                a.tmp64 = R3("tmp64", [64, T], BF16)
                a.rowsA = R3("rowsA", [1, T], F32)
                a.rbf = R3("rbf", [1, T], BF16)
                mk.emit_l0_scores(a)
                mk.emit_a_mem(a)
        P.join([("q",), ("k",), ("v",), ("sig",), ("sct", 0), ("sct", 1), ("pb", 0), ("pb", 1),
                ("tmp",), ("r1",), ("rbf",), ("xsl", 0), ("xsl", 1), ("wk",), ("wv",),
                ("wqb", 0), ("wqb", 1), ("mem",), ("memn",), ("mw",)])
        with ExitStack() as ph5:
            R5 = lambda *ar: ph5.enter_context(nc.sbuf_tensor(*ar, side="right"))
            r_sb = R5("r_sbA", [128, 16 * T], F32)
            sq = R5("sqA", [128, 2 * T], F32)
            rows = R5("rowsA5", [1, 5 * T], F32)
            out_aps = [ST(kc) for kc in range(NT)]
            kn = dict(r="rA", sq="sqA", rows="rwA", s1=("P", "ps0"), s2=("P", "ps1"),
                      b1=("P", "ps2"), b2=("P", "ps3"))
            mk.emit_resid_ln(mk.ap_xT_my, out_aps, [("at", kc) for kc in range(NT)],
                             out_aps, [("x1", kc) for kc in range(NT)],
                             r_sb, sq, rows, ps0[0:1, :], ps1[0:1, :], ps2, ps3, kn)
        P.join([("rA", kc) for kc in range(NT)] + [("sqA", 0), ("sqA", 1), ("rwA",)])

        b = NS()
        b.x1_aps = [ST(kc) for kc in range(NT)]
        b.x1_keys = [("x1", kc) for kc in range(NT)]
        b.ps_g, b.ps_u, b.ps_d, b.ps_3 = ps0, ps1, ps2, ps3
        b.y_acc = es.enter_context(nc.sbuf_tensor("y_accB", [128, 16 * T], F32, side="right"))
        with ExitStack() as ph:
            R = lambda *ar: ph.enter_context(nc.sbuf_tensor(*ar, side="right"))
            b.wbuf = [R("wbuf0B", [128, 16384], BF16), R("wbuf1B", [128, 16384], BF16)]
            b.pkg = R("pkgB", [128, 2048], U16)
            b.pku = R("pkuB", [128, 2048], U16)
            b.scg = R("scgB", [128, 2048], FP16)
            b.scu = R("scuB", [128, 2048], FP16)
            b.inter = R("interB", [128, 8 * T], BF16)
            b.tmpa = R("tmpaB", [128, 1024], I32)
            b.tmpb = R("tmpbB", [128, 512], I32)
            b.gb = R("gbB", [128, 2 * T], BF16)
            b.ub = R("ubB", [128, 2 * T], BF16)
            y_keys = mk.emit_l0_ffn(b)
        for kc in range(NT):
            P.emit("vector", lambda v, _kc=kc: v.tensor_tensor(
                   b.y_acc[:, _kc * T:(_kc + 1) * T], b.y_acc[:, _kc * T:(_kc + 1) * T],
                   ST(_kc), ALU.add),
                   inc=sem.v, reads=[("y", kc), ("x1", kc)], writes=[("y", kc)])
        P.join([("wb", bid, u) for bid in range(2) for u in range(32)] +
               [("int", si) for si in range(8)] +
               [("pkg",), ("pku",), ("scg",), ("scu",), ("gb", 0), ("gb", 1), ("ub", 0), ("ub", 1),
                ("t", "a"), ("t", "b")])
        with ExitStack() as ph2:
            R2 = lambda *ar: ph2.enter_context(nc.sbuf_tensor(*ar, side="right"))
            sq = R2("sqB", [128, 2 * T], F32)
            rows = R2("rowsB", [1, 5 * T], F32)
            r_aps = [b.y_acc[:, kc * T:(kc + 1) * T] for kc in range(NT)]
            kn = dict(sq="sqB", rows="rwB", s1=("P", "ps0"), s2=("P", "ps1"),
                      b1=("P", "ps2"), b2=("P", "ps3"))
            mk.emit_ln(r_aps, y_keys, b.x1_aps, [("x2", kc) for kc in range(NT)],
                       sq, rows, ps0[0:1, :], ps1[0:1, :], ps2, ps3, kn)
        P.join([("y", kc) for kc in range(NT)] + [("sqB", 0), ("sqB", 1), ("rwB",)])

        cmask = mk.din("cmask", [128, 8], F32)
        wq1 = mk.din("l1_wq", [16, 128, D], F32)
        wk1 = mk.din("l1_wk", [16, 128, D], F32)
        wv1 = mk.din("l1_wv", [16, 128, D], F32)
        wo1 = mk.din("l1_wo", [16, 128, D], F32)
        cc_in = nc.dram_tensor("cc_in", [128, 16], F32, kind="Internal")
        cc_out = nc.dram_tensor("cc_out", [128 * ncores, 16], F32, kind="Internal")
        cm_sb = mk.sbuf("cm_sb", [128, 8], F32)
        cs_sb = mk.sbuf("cs_sb", [128, 16], F32)
        cy_sb = mk.sbuf("cy_sb", [128, 16], F32)
        ag_sb = mk.sbuf("ag_sb", [128, 16 * ncores], F32)
        P.dma("sync", mk.S("cm"), cm_sb[:], cmask, writes=[("cm",)])

        with ExitStack() as phC:
            RC = lambda *ar: phC.enter_context(nc.sbuf_tensor(*ar, side="right"))
            q2 = RC("q2", [128, 16 * T], BF16)
            wb = RC("wbC", [128, 4 * D], BF16)
            with ExitStack() as phkv:
                Rr = lambda *ar: phkv.enter_context(nc.sbuf_tensor(*ar, side="right"))
                kv_sb = Rr("kv_sb", [128, 16 * T], BF16)
                kbuf = Rr("kbuf", [128, T], BF16)
                kvt_t = Rr("kvt_t", [128, T], BF16)
                for oc in range(16):
                    bi = (2 * oc) % 4
                    P.dma("gpsimd", mk.S("wA" + str(bi)), wb[:, bi * D:(bi + 1) * D], wk1[oc], writes=[("wA", bi)])
                    for kc in range(NT):
                        for nh in range(2):
                            P.emit("tensor", lambda t, _bi=bi, _kc=kc, _nh=nh:
                                   t.matmul(ps0[:, _nh * 512:_nh * 512 + 512],
                                            wb[:, _bi * D + _kc * 128:_bi * D + _kc * 128 + 128],
                                            ST(_kc)[:, _nh * 512:_nh * 512 + 512],
                                            start=_kc == 0, stop=_kc == NT - 1),
                                   inc=sem.t if (kc == NT - 1 and nh == 1) else None,
                                   reads=[("wA", bi), ("x2", kc)], writes=[("P", "ps0", nh)])
                    P.emit("scalar", lambda s: s.copy(kbuf[:], ps0[:]),
                           inc=sem.s, reads=[("P", "ps0", 0), ("P", "ps0", 1)], writes=[("kb",)])
                    bi2 = (2 * oc + 1) % 4
                    P.dma("gpsimd", mk.S("wB" + str(bi2)), wb[:, bi2 * D:(bi2 + 1) * D], wv1[oc], writes=[("wA", bi2)])
                    vp, vpn = (ps1, "ps1") if oc % 2 == 0 else (ps3, "ps3")
                    for kc in range(NT):
                        for nh in range(2):
                            P.emit("tensor", lambda t, _bi=bi2, _kc=kc, _nh=nh, _ps=vp:
                                   t.matmul(_ps[:, _nh * 512:_nh * 512 + 512],
                                            wb[:, _bi * D + _kc * 128:_bi * D + _kc * 128 + 128],
                                            ST(_kc)[:, _nh * 512:_nh * 512 + 512],
                                            start=_kc == 0, stop=_kc == NT - 1),
                                   inc=sem.t if (kc == NT - 1 and nh == 1) else None,
                                   reads=[("wA", bi2), ("x2", kc)], writes=[("P", vpn, nh)])
                    P.emit("vector", lambda v, _oc=oc, _ps=vp: v.tensor_tensor(kv_sb[:, _oc * T:(_oc + 1) * T], kbuf[:], _ps[:], ALU.mult),
                           inc=sem.v, reads=[("kb",), ("P", vpn, 0), ("P", vpn, 1)], writes=[("kv", oc)])
                    P.emit("vector", lambda v, _oc=oc: v.tensor_reduce(cs_sb[:, _oc:_oc + 1], kv_sb[:, _oc * T:(_oc + 1) * T], AX.X, ALU.add),
                           inc=sem.v, reads=[("kv", oc)], writes=[("cs",)])
                P.dma("gpsimd", mk.S("ccw"), cc_in.ap(), cs_sb[:], reads=[("cs",)], writes=[("ccin",)])
                P.emit("gpsimd", lambda g: g.collective_compute(
                           "AllGather", ALU.bypass, replica_groups=[list(range(ncores))],
                           ins=[cc_in.ap()], outs=[cc_out.ap()]),
                       inc=sem.cc, reads=[("ccin",)], writes=[("ccout",)])
                P.dma("gpsimd", mk.S("agr"), ag_sb[:, 0:16 * ncores],
                      cc_out.ap().rearrange("(r p) oc -> p r oc", p=128), reads=[("ccout",)], writes=[("ag",)])
                for oc in range(16):
                    bi = oc % 4
                    qp, qpn = (ps0, "ps0") if oc % 2 == 0 else (ps1, "ps1")
                    P.dma("gpsimd", mk.S("wA" + str(bi)), wb[:, bi * D:(bi + 1) * D], wq1[oc],
                          writes=[("wA", bi)])
                    for kc in range(NT):
                        for nh in range(2):
                            P.emit("tensor", lambda t, _bi=bi, _kc=kc, _nh=nh, _ps=qp:
                                   t.matmul(_ps[:, _nh * 512:_nh * 512 + 512],
                                            wb[:, _bi * D + _kc * 128:_bi * D + _kc * 128 + 128],
                                            ST(_kc)[:, _nh * 512:_nh * 512 + 512],
                                            start=_kc == 0, stop=_kc == NT - 1),
                                   inc=sem.t if (kc == NT - 1 and nh == 1) else None,
                                   reads=[("wA", bi), ("x2", kc)], writes=[("P", qpn, nh)])
                    P.emit("scalar", lambda s, _oc=oc, _ps=qp: s.copy(q2[:, _oc * T:(_oc + 1) * T], _ps[:]),
                           inc=sem.s, reads=[("P", qpn, 0), ("P", qpn, 1)], writes=[("q2", oc)])
                P.emit("vector", lambda v: v.memset(cy_sb[:], 0.0), writes=[("cy",)])
                for r in range(ncores):
                    P.emit("vector", lambda v, _r=r: v.scalar_tensor_tensor(
                               cy_sb[:], ag_sb[:, _r * 16:(_r + 1) * 16], cm_sb[:, _r:_r + 1], cy_sb[:],
                               ALU.mult, ALU.add),
                           inc=sem.v if r == ncores - 1 else None,
                           reads=[("ag",), ("cm",), ("cy",)], writes=[("cy",)])
                for oc in range(16):
                    P.emit("vector", lambda v, _oc=oc: v.tensor_tensor_scan(
                               kvt_t[:], kv_sb[:, _oc * T:(_oc + 1) * T], kv_sb[:, _oc * T:(_oc + 1) * T],
                               cy_sb[:, _oc:_oc + 1], ALU.add, ALU.bypass),
                           inc=sem.v, reads=[("kv", oc), ("cy",)], writes=[("kvt",)])
                    P.emit("vector", lambda v, _oc=oc: v.scalar_tensor_tensor(
                               q2[:, _oc * T:(_oc + 1) * T], q2[:, _oc * T:(_oc + 1) * T], DENOM_L1, kvt_t[:],
                               ALU.mult, ALU.mult),
                           inc=sem.v, reads=[("q2", oc), ("kvt",)], writes=[("q2", oc)])
            P.join([("kv", oc) for oc in range(16)] + [("kb",), ("kvt",)])
            with ExitStack() as phr:
                Rr = lambda *ar: phr.enter_context(nc.sbuf_tensor(*ar, side="right"))
                r_sb = b.y_acc
                sq = Rr("sqC", [128, 2 * T], F32)
                rows = Rr("rowsC", [1, 5 * T], F32)
                for oc in range(16):
                    bi = oc % 4
                    op_, opn = (ps2, "ps2") if oc % 2 == 0 else (ps3, "ps3")
                    P.dma("gpsimd", mk.S("wA" + str(bi)), wb[:, bi * D:(bi + 1) * D], wo1[oc], writes=[("wA", bi)])
                    for kc in range(NT):
                        for nh in range(2):
                            P.emit("tensor", lambda t, _bi=bi, _kc=kc, _nh=nh, _ps=op_:
                                   t.matmul(_ps[:, _nh * 512:_nh * 512 + 512],
                                            wb[:, _bi * D + _kc * 128:_bi * D + _kc * 128 + 128],
                                            q2[:, _kc * T + _nh * 512:_kc * T + _nh * 512 + 512],
                                            start=_kc == 0, stop=_kc == NT - 1),
                                   inc=sem.t if (kc == NT - 1 and nh == 1) else None,
                                   reads=[("wA", bi), ("q2", kc)], writes=[("P", opn, nh)])
                    P.emit("vector", lambda v, _oc=oc, _ps=op_: v.tensor_tensor(r_sb[:, _oc * T:(_oc + 1) * T],
                               ST(_oc), _ps[:], ALU.add),
                           inc=sem.v, reads=[("x2", oc), ("P", opn, 0), ("P", opn, 1)], writes=[("rC", oc)])
                r_aps = [r_sb[:, kc * T:(kc + 1) * T] for kc in range(NT)]
                kn = dict(sq="sqC", rows="rwC", s1=("P", "ps0"), s2=("P", "ps1"),
                          b1=("P", "ps2"), b2=("P", "ps3"))
                mk.emit_ln(r_aps, [("rC", kc) for kc in range(NT)],
                           [ST(kc) for kc in range(NT)], [("x3", kc) for kc in range(NT)],
                           sq, rows, ps0[0:1, :], ps1[0:1, :], ps2, ps3, kn)
        P.join([("q2", oc) for oc in range(16)] + [("wA", 0), ("wA", 1)] +
               [("rC", kc) for kc in range(NT)] + [("sqC", 0), ("sqC", 1), ("rwC",)])

        gpk1 = mk.din("l1_gpk", [22, 128, 2048], U16)
        dpk1 = mk.din("l1_dpk", [11, 128, 2048], U16)
        gsc1 = mk.din("l1_gsc", [22, 4, 4, 512], FP16)
        dsc1 = mk.din("l1_dsc", [11, 4, 2048], FP16)
        x3d = nc.dram_tensor("x3d", [D, T], BF16, kind="Internal")
        for kc in range(NT):
            P.dma("sync", mk.S("bw"), x3d.ap().rearrange("(kc p) t -> kc p t", p=128)[kc],
                  ST(kc), reads=[("x3", kc)], writes=[("x3d", kc)])
        y = b.y_acc
        with ExitStack() as phD:
            RD = lambda *ar: phD.enter_context(nc.sbuf_tensor(*ar, side="right"))
            xj = RD("xj", [128, 16 * T], BF16)
            x3r = x3d.ap().rearrange("(a two) t -> two a t", two=2)
            for jp in range(4):
                par = 1 if jp in (2, 3) else 0
                hi = 1 if jp in (1, 3) else 0
                for tt in range(4):
                    a0 = hi * 512 + tt * 128
                    P.dma("sync", mk.S("xj%d" % (jp * 4 + tt)), xj[:, (jp * 4 + tt) * T:(jp * 4 + tt + 1) * T],
                          x3r[par, a0:a0 + 128, :],
                          reads=[("x3d", kc) for kc in range(NT)], writes=[("xj", jp, tt)])
            XJ = lambda jp, tt: xj[:, (jp * 4 + tt) * T:(jp * 4 + tt + 1) * T]
            with ExitStack() as ph:
                Rr = lambda *ar: ph.enter_context(nc.sbuf_tensor(*ar, side="right"))
                wdec = Rr("wdecD", [128, 16384], BF16)
                pk0 = Rr("pk0D", [128, 2048], U16)
                pk1 = Rr("pk1D", [128, 2048], U16)
                sc0 = Rr("sc0D", [128, 2048], FP16)
                sc1 = Rr("sc1D", [128, 2048], FP16)
                inter = Rr("interD", [128, 8 * T], BF16)
                tmpa = Rr("tmpaD", [128, 1024], I32)
                tmpb = Rr("tmpbD", [128, 1024], I32)
                gsb = Rr("gsbD", [128, T], BF16)
                y_keys = [("yD", hc) for hc in range(NT)]
                for ci, (cs, nt) in enumerate(L1_CHUNKS):
                    for tt in range(nt):
                        for half, span in ((0, cs + tt), (1, 11 + cs + tt)):
                            pkb, pkk = (pk0, ("pkD", 0)) if half == 0 else (pk1, ("pkD", 1))
                            scb, sck = (sc0, ("scD", 0)) if half == 0 else (sc1, ("scD", 1))
                            P.dma("sync", mk.S("pkD%d" % half), pkb[:], gpk1[span], writes=[pkk])
                            for tin in range(4):
                                P.dma("sync", mk.S("scD%d%d" % (half, tin)), scb[:, tin * 512:(tin + 1) * 512],
                                      gsc1[span, tin].unsqueeze(1).broadcast_to((4, 32, 512)),
                                      writes=[sck + (tin,)])
                            for jin in range(4):
                                for tin in range(4):
                                    ki = jin * 4 + tin
                                    dst = wdec[:, (half * 16 + ki) * 512:(half * 16 + ki + 1) * 512]
                                    mk.emit_decode(pkb[:, tin * 512:(tin + 1) * 512], scb[:, tin * 512:(tin + 1) * 512],
                                                   dst, jin, tmpa, tmpb,
                                                   [pkk], [sck + (tin,)], [("w", half, ki)], ("t",))
                        for jo in range(4):
                            for half in range(2):
                                if jo % 2 == 0:
                                    psx, psk = (ps0, ("P", "ps0")) if half == 0 else (ps1, ("P", "ps1"))
                                else:
                                    psx, psk = (ps2, ("P", "ps2")) if half == 0 else (ps3, ("P", "ps3"))
                                for jin in range(4):
                                    for tin in range(4):
                                        ki = jin * 4 + tin
                                        lhs = wdec[:, (half * 16 + ki) * 512:(half * 16 + ki + 1) * 512].rearrange(
                                            "p (a four) -> p a four", four=4)[:, :, jo]
                                        for nh in range(2):
                                            P.emit("tensor", lambda t, _l=lhs, _jin=jin, _tin=tin, _nh=nh, _ps=psx,
                                                   _f=ki == 0, _la=ki == 15:
                                                   t.matmul(_ps[:, _nh * 512:_nh * 512 + 512], _l,
                                                            XJ(_jin, _tin)[:, _nh * 512:_nh * 512 + 512],
                                                            start=_f, stop=_la),
                                                   inc=sem.t if (ki == 15 and nh == 1) else None,
                                                   reads=[("w", half, ki), ("xj", jin, tin)], writes=[psk + (nh,)])
                            it = inter[:, (tt * 4 + jo) * T:(tt * 4 + jo + 1) * T]
                            pg_, pu_ = (ps0, ps1) if jo % 2 == 0 else (ps2, ps3)
                            pgn_, pun_ = ("ps0", "ps1") if jo % 2 == 0 else ("ps2", "ps3")
                            P.emit("scalar", lambda s, _pg=pg_: s.activation(gsb[:], _pg[:], AF.Silu, scale=float(2 ** 14)),
                                   inc=sem.s, reads=[("P", pgn_, 0), ("P", pgn_, 1)], writes=[("gsb",)])
                            P.emit("vector", lambda v, _it=it, _pu=pu_: v.scalar_tensor_tensor(_it, gsb[:], float(2 ** 28), _pu[:], ALU.mult, ALU.mult),
                                   inc=sem.v, reads=[("gsb",), ("P", pun_, 0), ("P", pun_, 1)],
                                   writes=[("intD", tt * 4 + jo)])
                    for tt in range(nt):
                        pkb, pkk = (pk0, ("pkD", 0)) if tt % 2 == 0 else (pk1, ("pkD", 1))
                        scb, sck = (sc0, ("scD", 0)) if tt % 2 == 0 else (sc1, ("scD", 1))
                        P.dma("sync", mk.S("pkD%d" % (tt % 2)), pkb[:], dpk1[cs + tt], writes=[pkk])
                        P.dma("sync", mk.S("scDd%d" % (tt % 2)), scb[:],
                              dsc1[cs + tt].unsqueeze(1).broadcast_to((4, 32, 2048)),
                              writes=[sck + (0,), sck + (1,), sck + (2,), sck + (3,)])
                        for jd in range(4):
                            pl = tt * 4 + jd
                            dst = wdec[:, pl * 2048:(pl + 1) * 2048]
                            alias = ([("w", 0, 4 * pl + q) for q in range(4)] if pl < 4 else
                                     [("w", 1, 4 * (pl - 4) + q) for q in range(4)])
                            mk.emit_decode(pkb[:], scb[:], dst, jd, tmpa, tmpb,
                                           [pkk], [sck + (q,) for q in range(4)], [("dw", pl)] + alias, ("t",))
                    for hc in range(NT):
                        hoff = 256 * (hc % 8)
                        hpar = 1 if hc >= 8 else 0
                        np_ = nt * 4
                        pd_, pdn_ = (ps2, "ps2") if hc % 2 == 0 else (ps3, "ps3")
                        for pl in range(np_):
                            lhs = wdec[:, pl * 2048 + hoff:pl * 2048 + hoff + 256].rearrange(
                                "p (a two) -> p a two", two=2)[:, :, hpar]
                            alias_r = ([("w", 0, 4 * pl + q) for q in range(4)] if pl < 4 else
                                       [("w", 1, 4 * (pl - 4) + q) for q in range(4)])
                            for nh in range(2):
                                P.emit("tensor", lambda t, _l=lhs, _pl=pl, _nh=nh, _ps=pd_, _f=pl == 0, _la=pl == np_ - 1:
                                       t.matmul(_ps[:, _nh * 512:_nh * 512 + 512], _l,
                                                inter[:, _pl * T + _nh * 512:_pl * T + _nh * 512 + 512],
                                                start=_f, stop=_la),
                                       inc=sem.t if (pl == np_ - 1 and nh == 1) else None,
                                       reads=[("dw", pl), ("intD", pl)] + alias_r, writes=[("P", pdn_, nh)])
                        if ci == 0:
                            P.emit("scalar", lambda s, _hc=hc, _ps=pd_: s.copy(y[:, _hc * T:(_hc + 1) * T], _ps[:]),
                                   inc=sem.s, reads=[("P", pdn_, 0), ("P", pdn_, 1)], writes=[("yD", hc)])
                        else:
                            P.emit("vector", lambda v, _hc=hc, _ps=pd_: v.tensor_tensor(y[:, _hc * T:(_hc + 1) * T],
                                   y[:, _hc * T:(_hc + 1) * T], _ps[:], ALU.add),
                                   inc=sem.v, reads=[("yD", hc), ("P", pdn_, 0), ("P", pdn_, 1)], writes=[("yD", hc)])
            P.join([("w", 0, k) for k in range(16)] + [("w", 1, k) for k in range(16)] +
                   [("dw", pl) for pl in range(8)] + [("intD", q) for q in range(8)] +
                   [("pkD", 0), ("pkD", 1)] +
                   [("scD", hq, q) for hq in range(2) for q in range(4)] +
                   [("gsb",), ("t", "a"), ("t", "b")] +
                   [("xj", jp, tt) for jp in range(4) for tt in range(4)])
            with ExitStack() as ph2:
                Rr = lambda *ar: ph2.enter_context(nc.sbuf_tensor(*ar, side="right"))
                sq = Rr("sqD", [128, 2 * T], F32)
                rows = Rr("rowsD", [1, 5 * T], F32)
                RW = [("rwD",)]
                for hc in range(NT):
                    buf = sq[:, (hc % 2) * T:(hc % 2) * T + T]
                    P.emit("scalar", lambda s, _b=buf, _hc=hc: s.activation(_b, y[:, _hc * T:(_hc + 1) * T], AF.Square),
                           inc=sem.s, reads=[("yD", hc)], writes=[("sqD", hc % 2)])
                    for nh in range(2):
                        P.emit("tensor", lambda t, _b=buf, _nh=nh, _f=hc == 0, _la=hc == NT - 1:
                               t.matmul(ps0[0:1, _nh * 512:_nh * 512 + 512], mk.ones_col,
                                        _b[:, _nh * 512:_nh * 512 + 512], start=_f, stop=_la),
                               inc=sem.t if (hc == NT - 1 and nh == 1) else None,
                               reads=[("sqD", hc % 2), ("cst",)], writes=[("P", "ps0", nh)])
                msq = rows[0:1, 0:T]
                P.emit("vector", lambda v: v.tensor_scalar(msq, ps0[0:1, :], 1.0 / D, None, ALU.mult),
                       inc=sem.v, reads=[("P", "ps0", 0), ("P", "ps0", 1)], writes=RW)
                P.emit("scalar", lambda s: s.activation(msq, msq, AF.Ln, bias=mk.eps_rms),
                       reads=RW + [("cst",)], writes=RW)
                P.emit("scalar", lambda s: s.activation(msq, msq, AF.Exp, scale=-0.5), inc=sem.s,
                       reads=RW, writes=RW)
                for nh in range(2):
                    P.emit("tensor", lambda t, _nh=nh: t.matmul(ps1[:, _nh * 512:_nh * 512 + 512],
                           mk.orow[0:1, 0:128], msq[0:1, _nh * 512:_nh * 512 + 512], start=True, stop=True),
                           inc=sem.t if nh == 1 else None,
                           reads=RW + [("orow",)], writes=[("P", "ps1", nh)])
                for kc in range(NT):
                    P.emit("vector", lambda v, _kc=kc: v.tensor_tensor(y[:, _kc * T:(_kc + 1) * T],
                           y[:, _kc * T:(_kc + 1) * T], ps1[:], ALU.mult),
                           reads=[("yD", kc), ("P", "ps1", 0), ("P", "ps1", 1)], writes=[("yD", kc)])
                    P.emit("vector", lambda v, _kc=kc: v.tensor_tensor(y[:, _kc * T:(_kc + 1) * T],
                           y[:, _kc * T:(_kc + 1) * T], ST(_kc), ALU.add),
                           inc=sem.v, reads=[("yD", kc), ("x3", kc)], writes=[("yD", kc)])
                r_aps = [y[:, kc * T:(kc + 1) * T] for kc in range(NT)]
                kn = dict(sq="sqD2", rows="rwD2", s1=("P", "ps0"), s2=("P", "ps2"),
                          b1=("P", "ps1"), b2=("P", "ps3"))
                mk.emit_ln(r_aps, [("yD", kc) for kc in range(NT)],
                           [ST(kc) for kc in range(NT)], [("x4", kc) for kc in range(NT)],
                           sq, rows, ps0[0:1, :], ps2[0:1, :], ps1, ps3, kn)
                tok_out = None
                for kc in range(NT):
                    tok_out = mk.P.dma("gpsimd", mk.S("outF"),
                                       out_ap.rearrange("(kc p) t -> kc p t", p=128)[kc],
                                       ST(kc), reads=[("x4", kc)])
                mk.P.raw("gpsimd", lambda e, _t=tok_out: e.wait_ge(_t.sem.h, _t.tgt))
        P.finalize(es.enter_context(nc.Block()), es)
    return nc




from concourse.bass_utils import run_bass_kernel_spmd


def kernel(**inputs):
    g, percore = host_prep(inputs)
    nc = bass.Bass("TRN2", target_bir_lowering=False)
    build_fused(nc)
    in_maps = []
    for c in range(NCORES):
        m = dict(percore[c])
        m.update(consts=g["consts"], onesrow=g["onesrow"],
                 l0_wq=g["l0_wq"], l0_wk_nat=g["l0_wk_nat"], l0_wv_nat=g["l0_wv_nat"],
                 l0_mem=g["l0_mem"], l0_memn=g["l0_memn"], l0_memw=g["l0_memw"],
                 l0_gpk=g["l0_gpk"], l0_upk=g["l0_upk"], l0_dpk=g["l0_dpk"],
                 l0_gsc=g["l0_gsc"], l0_usc=g["l0_usc"], l0_dsc=g["l0_dsc"],
                 l1_wq=g["l1_wq"], l1_wk=g["l1_wk"], l1_wv=g["l1_wv"], l1_wo=g["l1_wo"],
                 l1_gpk=g["l1_gpk"], l1_dpk=g["l1_dpk"], l1_gsc=g["l1_gsc"], l1_dsc=g["l1_dsc"])
        in_maps.append(m)
    res = run_bass_kernel_spmd(nc, in_maps, core_ids=list(range(NCORES)))
    return host_finish([np.asarray(res.results[c]["x4_out"], np.float32) for c in range(NCORES)])
